# version 14
# baseline (speedup 1.0000x reference)
import contextlib

import numpy as np

import concourse.bass as bass
import concourse.mybir as mybir
from concourse.bass import IndirectOffsetOnAxis
from concourse.bass_utils import run_bass_kernel_spmd

F32 = mybir.dt.float32
BF16 = mybir.dt.bfloat16
I32 = mybir.dt.int32
AF = mybir.ActivationFunctionType
ALU = mybir.AluOpType

LN_TP = True
L0_DEBUG = False
B_LOC = 2
L = 2048
T = B_LOC * L
D0 = 1024
DI = 2048
NST = 16
RNK = 64
HID = 3072
NEXP = 8
MTOP = 256
PC0 = 256
PCF = 512
PC1 = 256
NDC0 = D0 // 128
NDC1 = DI // 128
INV127 = float(np.float32(1.0) / np.float32(127.0))

ENGS = ("sp", "pe", "act", "dve", "pool")


class Prog:

    DMA_POOL = {"sp": 14, "act": 8, "pool": 8}

    def __init__(self, nc):
        self.nc = nc
        self.q = {e: [] for e in ENGS}
        self.cnt = dict.fromkeys(ENGS, 0)
        self.dcnt = {}
        self.rr = dict.fromkeys(ENGS, 0)
        self.wm = {}
        self.sems = {}
        self.wr = {}
        self.rd = {}
        self.trace_log = []

    def make_sems(self, stack):
        for e in ENGS:
            self.sems["q" + e] = stack.enter_context(self.nc.semaphore("q" + e))
            for i in range(self.DMA_POOL.get(e, 0)):
                self.sems[f"d{e}{i}"] = stack.enter_context(
                    self.nc.semaphore(f"d{e}{i}"))
                self.dcnt[(e, i)] = 0

    def _wait(self, eng, sem_name, val):
        if val <= 0:
            return
        key = (eng, sem_name)
        if self.wm.get(key, 0) >= val:
            return
        self.wm[key] = val
        sem = self.sems[sem_name]
        self.q[eng].append(lambda E, s=sem, v=val: E.wait_ge(s, v))

    def _deps(self, eng, toks):
        best = {}
        for t in toks:
            if t is None:
                continue
            s, v = t
            if best.get(s, 0) >= v:
                continue
            best[s] = v
        for s, v in best.items():
            self._wait(eng, s, v)

    def _collect(self, reads, writes):
        toks = []
        for k in reads:
            t = self.wr.get(k)
            if t is not None:
                toks.append(t)
        for k in writes:
            t = self.wr.get(k)
            if t is not None:
                toks.append(t)
            toks.extend(self.rd.get(k, ()))
        return toks

    def _record(self, tok, reads, writes):
        for k in reads:
            self.rd.setdefault(k, []).append(tok)
        for k in writes:
            self.wr[k] = tok
            self.rd[k] = []

    def predeps(self, key):
        t = [self.wr[key]] if key in self.wr else []
        return t + list(self.rd.get(key, ()))

    def op(self, eng, fn, reads=(), writes=(), extra=(), tag=None):
        if tag is None:
            tag = getattr(fn, "__qualname__", "?")
        self.trace_log.append((eng, tag))
        self._deps(eng, self._collect(reads, writes) + list(extra))
        self.cnt[eng] += 1
        sem = self.sems["q" + eng]

        def run(E, fn=fn, sem=sem):
            fn(E).then_inc(sem, 1)
        self.q[eng].append(run)
        tok = ("q" + eng, self.cnt[eng])
        self._record(tok, reads, writes)
        return tok

    def _dma_slot(self, eng):
        i = self.rr[eng]
        self.rr[eng] = (i + 1) % self.DMA_POOL[eng]
        sem_name = f"d{eng}{i}"
        prev = self.dcnt[(eng, i)]
        self._wait(eng, sem_name, prev)
        self.dcnt[(eng, i)] = prev + 16
        return sem_name, prev + 16

    def dma(self, eng, out, in_, reads=(), writes=(), extra=(), **kw):
        self._deps(eng, self._collect(reads, writes) + list(extra))
        sem_name, val = self._dma_slot(eng)
        sem = self.sems[sem_name]

        def run(E, out=out, in_=in_, sem=sem, kw=kw):
            E.dma_start(out=out, in_=in_, **kw).then_inc(sem, 16)
        self.q[eng].append(run)
        tok = (sem_name, val)
        self._record(tok, reads, writes)
        return tok

    def idma(self, out, out_off, in_, in_off, reads=(), writes=(), extra=(),
             bounds_check=None):
        eng = "pool"
        self._deps(eng, self._collect(reads, writes) + list(extra))
        sem_name, val = self._dma_slot(eng)
        sem = self.sems[sem_name]

        def run(E, sem=sem):
            E.indirect_dma_start(out=out, out_offset=out_off, in_=in_,
                                 in_offset=in_off,
                                 bounds_check=bounds_check).then_inc(sem, 16)
        self.q[eng].append(run)
        tok = (sem_name, val)
        self._record(tok, reads, writes)
        return tok

    def barrier(self):
        toks = []
        for e in ENGS:
            toks.append(("q" + e, self.cnt[e]))
            for i in range(self.DMA_POOL.get(e, 0)):
                toks.append((f"d{e}{i}", self.dcnt[(e, i)]))
        for e in ENGS:
            self._deps(e, toks)

    def emit(self, block):
        secs = {"sp": block.sync, "pe": block.tensor, "act": block.scalar,
                "dve": block.vector, "pool": block.gpsimd}
        for e in ENGS:
            ops = self.q[e]

            def body(E, ops=ops):
                for f in ops:
                    f(E)
            secs[e](body)


def build_program(stage="full", debug_out=None):
    nc = bass.Bass("TRN2", target_bir_lowering=False, debug=False,
                   detect_race_conditions=False)
    top = contextlib.ExitStack()

    def din(name, shape, dtype=F32):
        return nc.dram_tensor(name, shape, dtype, kind="ExternalInput").ap()

    x_in = din("x_in", [B_LOC, L, D0])
    cw0 = din("cw0", [128, 4 * NDC0]); cb0 = din("cb0", [128, NDC0])
    xp0T = din("xp0T", [D0, 96])
    dt0T = din("dt0T", [RNK, D0]); db0 = din("db0", [128, NDC0])
    al0 = din("al0", [D0, NST]); dsk0 = din("dsk0", [128, NDC0])
    wgT = din("wgT", [D0, HID], I32)
    wuT = din("wuT", [D0, HID], I32)
    wdT = din("wdT", [HID, D0], I32)
    ipT = din("ipT", [D0, 2 * DI])
    cw1 = din("cw1", [128, 4 * NDC1]); cb1 = din("cb1", [128, NDC1])
    xp1T = din("xp1T", [DI, 96])
    dt1T = din("dt1T", [RNK, DI]); db1 = din("db1", [128, NDC1])
    al1 = din("al1", [DI, NST]); dd1 = din("dd1", [128, NDC1])
    opT = din("opT", [DI, D0])
    gw = din("gw", [D0, NEXP])
    exps = din("exps", [NEXP, D0, D0])
    id_bf = din("id_bf", [128, 128], BF16)
    id_f32 = din("id_f32", [128, 128])
    lt8 = din("lt8", [NEXP, NEXP], BF16)
    miota = din("miota", [1, MTOP])
    miocol = din("miocol", [128, 2])

    out = nc.dram_tensor("out", [B_LOC, L, D0], F32, kind="ExternalOutput").ap()

    def scr(name, shape, dtype=F32):
        return nc.dram_tensor(name, shape, dtype, kind="Internal").ap()

    XN32 = scr("XN32", [D0, T])
    XNBF = scr("XNBF", [D0, T], BF16)
    XNTM = scr("XNTM", [B_LOC, L, D0], BF16)
    X1TM = scr("X1TM", [B_LOC, L, D0])
    FFN0 = scr("FFN0", [D0, T])
    BCR = scr("BCR", [2, B_LOC, 32, L])
    WGBF = scr("WGBF", [D0, HID], BF16)
    WUBF = scr("WUBF", [D0, HID], BF16)
    U1 = scr("U1", [DI, T])
    D1 = scr("D1", [DI, T])
    ZS = scr("ZS", [DI, T])
    MOTM = scr("MOTM", [B_LOC, L, D0])
    GVD = scr("GVD", [1, L])
    OPBF = scr("OPBF", [8 * 128, D0], BF16)
    PWD = scr("PWD", [B_LOC, NEXP, L])

    dbg_src = {"XN32": XN32, "XNBF": XNBF, "X1TM": X1TM, "FFN0": FFN0,
               "U1": U1, "D1": D1, "ZS": ZS, "MOTM": MOTM, "XNTM": XNTM,
               "BCR": BCR, "PWD": PWD}
    dbg = None
    if debug_out is not None:
        src = dbg_src[debug_out]
        dbg = nc.dram_tensor("dbg", list(src.shape), src.dtype,
                             kind="ExternalOutput").ap()

    p = Prog(nc)
    globals()["LAST_PROG"] = p
    p.make_sems(top)

    def sb(name, shape, dtype=F32, stack=top):
        return stack.enter_context(nc.sbuf_tensor(name, shape, dtype))

    def ps(name, shape, dtype=F32, stack=top):
        return stack.enter_context(nc.psum_tensor(name, shape, dtype))

    t_id_bf = sb("t_id_bf", [128, 128], BF16)
    t_id_f = sb("t_id_f", [128, 128])
    t_A0 = sb("t_A0", [128, NST * NDC0])
    t_iA0 = sb("t_iA0", [128, NST * NDC0])
    t_niA0 = sb("t_niA0", [128, NST * NDC0])
    t_A1 = sb("t_A1", [128, NST * NDC1])
    t_cw0 = sb("t_cw0", [128, 4 * NDC0])
    t_cw1 = sb("t_cw1", [128, 4 * NDC1])
    t_cb0 = sb("t_cb0", [128, NDC0])
    t_cb1 = sb("t_cb1", [128, NDC1])
    t_db0 = sb("t_db0", [128, NDC0])
    t_db1 = sb("t_db1", [128, NDC1])
    t_dsk = sb("t_dsk", [128, NDC0])
    t_dd1 = sb("t_dd1", [128, NDC1])
    t_st0 = sb("t_st0", [128, NST * NDC0])
    t_st1 = sb("t_st1", [128, NST * NDC1])

    for i, v in enumerate([1e-5, -0.5, -1.0, 1.0, INV127]):
        ct = sb(f"cst{i}", [128, 1])
        p.op("pool", lambda E, ct=ct, v=v: E.memset(ct[:], v),
             writes=[("cst", v)])
        nc.const_aps.register(F32, v, ct[:, 0:1])

    p.dma("sp", t_id_bf[:], id_bf, writes=["idbf"])
    p.dma("sp", t_id_f[:], id_f32, writes=["idf"])
    for tile, src, key in ((t_cw0, cw0, "cw0"), (t_cw1, cw1, "cw1"),
                           (t_cb0, cb0, "cb0"), (t_cb1, cb1, "cb1"),
                           (t_db0, db0, "db0"), (t_db1, db1, "db1"),
                           (t_dsk, dsk0, "dsk"), (t_dd1, dd1, "dd1")):
        p.dma("sp", tile[:, :], src, writes=[key])

    def prep_A(tA, alog, ndc, key, neg_exp_of_neg=False):
        for dc in range(ndc):
            sl = tA[:, dc * NST:(dc + 1) * NST]
            p.dma("sp", sl, alog[dc * 128:(dc + 1) * 128, :],
                  writes=[(key, dc)])
            p.op("act", lambda E, sl=sl: E.activation(
                out=sl, in_=sl, func=AF.Exp,
                scale=(-1.0 if neg_exp_of_neg else 1.0)),
                reads=[(key, dc)], writes=[(key, dc)])
            p.op("dve", lambda E, sl=sl: E.tensor_scalar_mul(
                out=sl, in0=sl, scalar1=-1.0),
                reads=[(key, dc)], writes=[(key, dc)])
    prep_A(t_A0, al0, NDC0, "A0")
    prep_A(t_iA0, al0, NDC0, "iA0", neg_exp_of_neg=True)
    for dc in range(NDC0):
        sl = t_niA0[:, dc * NST:(dc + 1) * NST]
        p.dma("sp", sl, al0[dc * 128:(dc + 1) * 128, :],
              writes=[("niA0", dc)])
        p.op("act", lambda E, sl=sl: E.activation(
            out=sl, in_=sl, func=AF.Exp, scale=-1.0),
            reads=[("niA0", dc)], writes=[("niA0", dc), "niA0"])
    prep_A(t_A1, al1, NDC1, "A1")

    with contextlib.ExitStack() as st:
        c_i = sb("cvt_i", [128, HID], I32, st)
        c_o = sb("cvt_o", [128, HID], BF16, st)
        for srcT, dstT, key in ((wgT, WGBF, "wg"), (wuT, WUBF, "wu")):
            for dc in range(NDC0):
                p.dma("sp", c_i[:], srcT[dc * 128:(dc + 1) * 128, :],
                      writes=["cvt_i"])
                p.op("dve", lambda E: E.tensor_copy(out=c_o[:], in_=c_i[:]),
                     reads=["cvt_i"], writes=["cvt_o"])
                p.dma("sp", dstT[dc * 128:(dc + 1) * 128, :], c_o[:],
                      reads=["cvt_o"], writes=[(key, dc)])
        p.barrier()

    def load_bf(dst_sl, src_ap, key, st, stage_tile, rows=128):
        skey = ("stg", id(stage_tile))
        p.dma("sp", stage_tile[0:rows, :src_ap.shape[-1]], src_ap,
              writes=[skey])
        p.op("dve", lambda E: E.tensor_copy(
            out=dst_sl, in_=stage_tile[0:rows, :src_ap.shape[-1]]),
            reads=[skey], writes=[key])

    def emit_ln(layer):
        src_tm = x_in if layer == 0 else X1TM
        with contextlib.ExitStack() as st:
            t_x = [sb(f"ln{layer}x{i}", [128, D0], F32, st) for i in range(3)]
            t_stt = [sb(f"ln{layer}st{i}", [128, 12], F32, st) for i in range(3)]
            t_mv = [sb(f"ln{layer}mv{i}", [128, 2], F32, st) for i in range(3)]
            t_rs = [sb(f"ln{layer}rs{i}", [128, 1], F32, st) for i in range(3)]
            t_nm = [sb(f"ln{layer}nm{i}", [128, 1], F32, st) for i in range(3)]
            t_xn = [sb(f"ln{layer}xn{i}", [128, D0], F32, st) for i in range(3)]
            t_xnb = [sb(f"ln{layer}xb{i}", [128, D0], BF16, st)
                     for i in range(3)]
            pp = [ps(f"ln{layer}ps{q}", [128, 128], F32, st)
                  for q in range(3)] if layer == 0 else []
            ppb = [ps(f"ln{layer}pb{q}", [128, 128], BF16, st)
                   for q in range(3)]
            t_evb = [sb(f"ln{layer}eb{q}", [128, 128], BF16, st)
                     for q in range(3)]
            t_evf = [sb(f"ln{layer}ef{q}", [128, 128], F32, st)
                     for q in range(3)]
            for tt in range(T // 128):
                k = tt % 3
                b, l0 = divmod(tt * 128, L)
                p.dma("sp", t_x[k][:], src_tm[b, l0:l0 + 128, :],
                      writes=[("lx", k)])
                p.op("dve", lambda E, k=k: E.bn_stats(
                    t_stt[k][:, 0:6], t_x[k][:, 0:512]),
                    reads=[("lx", k)], writes=[("lst", k)])
                p.op("dve", lambda E, k=k: E.bn_stats(
                    t_stt[k][:, 6:12], t_x[k][:, 512:1024]),
                    reads=[("lx", k)], writes=[("lst2", k)])
                p.op("dve", lambda E, k=k: E.bn_aggr(
                    t_mv[k][:], t_stt[k][:]),
                    reads=[("lst", k), ("lst2", k)], writes=[("lmv", k)])
                p.op("act", lambda E, k=k: E.activation(
                    out=t_rs[k][:], in_=t_mv[k][:, 1:2], func=AF.Ln,
                    bias=1e-5), reads=[("lmv", k)], writes=[("lrs", k)])
                p.op("act", lambda E, k=k: E.activation(
                    out=t_rs[k][:], in_=t_rs[k][:], func=AF.Exp, scale=-0.5),
                    reads=[("lrs", k)], writes=[("lrs", k)])
                p.op("dve", lambda E, k=k: E.tensor_scalar_mul(
                    out=t_nm[k][:], in0=t_mv[k][:, 0:1], scalar1=-1.0),
                    reads=[("lmv", k)], writes=[("lnm", k)])
                p.op("dve", lambda E, k=k: E.tensor_scalar(
                    out=t_xn[k][:], in0=t_x[k][:], scalar1=t_nm[k][:, 0:1],
                    scalar2=t_rs[k][:, 0:1], op0=ALU.add, op1=ALU.mult),
                    reads=[("lx", k), ("lnm", k), ("lrs", k)],
                    writes=[("lxn", k)])
                p.op("act", lambda E, k=k: E.copy(
                    out=t_xnb[k][:], in_=t_xn[k][:]),
                    reads=[("lxn", k)], writes=[("lxb", k)])
                if layer == 1:
                    p.dma("sp", XNTM[b, l0:l0 + 128, :], t_xnb[k][:],
                          reads=[("lxb", k)], writes=[("XNTM", tt)])
                for dc in range(NDC0 if LN_TP else 0):
                    q = dc % 3
                    p.op("pe", lambda E, k=k, dc=dc, q=q: E.transpose(
                        out=ppb[q][:],
                        in_=t_xnb[k][:, dc * 128:(dc + 1) * 128],
                        identity=t_id_bf[:]),
                        reads=[("lxb", k), "idbf"], writes=[("tpb", q)])
                    p.op("act", lambda E, q=q: E.copy(
                        out=t_evb[q][:], in_=ppb[q][:]),
                        reads=[("tpb", q)], writes=[("evb", q)])
                    p.dma("act", XNBF[dc * 128:(dc + 1) * 128,
                                      tt * 128:(tt + 1) * 128],
                          t_evb[q][:],
                          reads=[("evb", q)], writes=[("XNBF", tt, dc)])
                    if layer == 0:
                        p.op("pe", lambda E, k=k, dc=dc, q=q: E.transpose(
                            out=pp[q][:],
                            in_=t_xn[k][:, dc * 128:(dc + 1) * 128],
                            identity=t_id_f[:]),
                            reads=[("lxn", k), "idf"], writes=[("tpf", q)])
                        p.op("dve", lambda E, q=q: E.tensor_copy(
                            out=t_evf[q][:],
                            in_=pp[q][:]),
                            reads=[("tpf", q)], writes=[("evf", q)])
                        p.dma("pool", XN32[dc * 128:(dc + 1) * 128,
                                           tt * 128:(tt + 1) * 128],
                              t_evf[q][:],
                              reads=[("evf", q)], writes=[("XN32", tt, dc)])
            p.barrier()

    def emit_ffn0():
        with contextlib.ExitStack() as st:
            t_wd = sb("f0wd", [128, 24 * 1024], BF16, st)
            stg_i = sb("f0stgi", [128, 1024], I32, st)
            t_xnb = [sb(f"f0xn{d}", [128, PCF], BF16, st) for d in range(NDC0)]
            t_wg = [sb(f"f0wg{i}", [128, 1024], BF16, st) for i in range(2)]
            t_wu = [sb(f"f0wu{i}", [128, 1024], BF16, st) for i in range(2)]
            t_sg = [sb(f"f0sg{i}", [128, PCF], F32, st) for i in range(2)]
            t_h = sb("f0h", [128, 24 * PCF], BF16, st)
            t_fo = [sb(f"f0o{i}", [128, PCF], F32, st) for i in range(2)]
            pg = [ps(f"f0pg{i}", [128, PCF], F32, st) for i in range(2)]
            pu = [ps(f"f0pu{i}", [128, PCF], F32, st) for i in range(2)]
            pd = [ps(f"f0pd{i}", [128, PCF], F32, st) for i in range(4)]
            for hc in range(24):
                p.dma("sp", stg_i[:], wdT[hc * 128:(hc + 1) * 128, :],
                      writes=["f0stg"])
                p.op("dve", lambda E, hc=hc: E.tensor_copy(
                    out=t_wd[:, hc * 1024:(hc + 1) * 1024], in_=stg_i[:]),
                    reads=["f0stg"], writes=[("wd", hc)])
            for pc in range(T // PCF):
                c0 = pc * PCF
                for dc in range(NDC0):
                    p.dma("sp", t_xnb[dc][:],
                          XNBF[dc * 128:(dc + 1) * 128, c0:c0 + PCF],
                          writes=[("f0x", dc)])
                for hc in range(24):
                    kb = hc % 2
                    p.dma("sp", t_wg[kb][:],
                          WGBF[:, hc * 128:(hc + 1) * 128].rearrange(
                              "(c p) h -> p c h", p=128),
                          writes=[("f0wg", kb)])
                    p.dma("sp", t_wu[kb][:],
                          WUBF[:, hc * 128:(hc + 1) * 128].rearrange(
                              "(c p) h -> p c h", p=128),
                          writes=[("f0wu", kb)])
                    for dc in range(NDC0):
                        p.op("pe", lambda E, kb=kb, dc=dc: E.matmul(
                            out=pg[kb][:], lhsT=t_wg[kb][:, dc * 128:(dc + 1) * 128],
                            rhs=t_xnb[dc][:], start=(dc == 0), stop=(dc == 7)),
                            reads=[("f0wg", kb), ("f0x", dc)],
                            writes=([("psg", kb)] if dc == 7 else []),
                            extra=p.predeps(("psg", kb)) if dc == 0 else [])
                        p.op("pe", lambda E, kb=kb, dc=dc: E.matmul(
                            out=pu[kb][:], lhsT=t_wu[kb][:, dc * 128:(dc + 1) * 128],
                            rhs=t_xnb[dc][:], start=(dc == 0), stop=(dc == 7)),
                            reads=[("f0wu", kb), ("f0x", dc)],
                            writes=([("psu", kb)] if dc == 7 else []),
                            extra=p.predeps(("psu", kb)) if dc == 0 else [])
                    p.op("act", lambda E, kb=kb: E.activation(
                        out=t_sg[kb][:], in_=pg[kb][:], func=AF.Silu,
                        scale=INV127),
                        reads=[("psg", kb)], writes=[("fsg", kb)])
                    p.op("dve", lambda E, kb=kb, hc=hc: E.scalar_tensor_tensor(
                        out=t_h[:, hc * PCF:(hc + 1) * PCF], in0=pu[kb][:],
                        scalar=INV127, in1=t_sg[kb][:], op0=ALU.mult,
                        op1=ALU.mult),
                        reads=[("psu", kb), ("fsg", kb)], writes=[("fh", hc)])
                for half in range(2):
                    for hc in range(24):
                        for do in range(4):
                            dd = half * 4 + do
                            p.op("pe", lambda E, hc=hc, do=do, dd=dd: E.matmul(
                                out=pd[do][:],
                                lhsT=t_wd[:, hc * 1024 + dd * 128:
                                          hc * 1024 + (dd + 1) * 128],
                                rhs=t_h[:, hc * PCF:(hc + 1) * PCF],
                                start=(hc == 0), stop=(hc == 23)),
                                reads=[("wd", hc), ("fh", hc)],
                                writes=([("psd", do)] if hc == 23 else []),
                                extra=p.predeps(("psd", do)) if hc == 0 else [])
                    for do in range(4):
                        dd = half * 4 + do
                        ko = dd % 2
                        p.op("dve", lambda E, do=do, ko=ko: E.tensor_scalar_mul(
                            out=t_fo[ko][:], in0=pd[do][:], scalar1=INV127),
                            reads=[("psd", do)], writes=[("f0o", ko)])
                        p.dma("sp", FFN0[dd * 128:(dd + 1) * 128, c0:c0 + PCF],
                              t_fo[ko][:], reads=[("f0o", ko)],
                              writes=[("FFN0", pc, dd)])
            p.barrier()

    def emit_l0():
        with contextlib.ExitStack() as st:
            t_xp = sb("l0xp", [128, NDC0 * 96], BF16, st)
            t_dt = sb("l0dt", [RNK, D0], BF16, st)
            stg = sb("l0stg", [128, 1024], F32, st)
            t_xh = [sb(f"l0xh{d}", [128, PC0 + 3], F32, st)
                    for d in range(NDC0)]
            t_co = [sb(f"l0co{d}", [128, PC0], BF16, st) for d in range(NDC0)]
            t_dl = [sb(f"l0dl{d}", [128, PC0], F32, st) for d in range(NDC0)]
            t_y = [sb(f"l0y{d}", [128, PC0], F32, st) for d in range(NDC0)]
            t_bc = sb("l0bc", [32, PC0], F32, st)
            t_dtl = sb("l0dtl", [RNK, PC0], BF16, st)
            t_t = sb("l0t", [128, PC0], F32, st)
            t_a = sb("l0a", [128, PC0], F32, st)
            t_r = sb("l0r", [128, PC0], F32, st)
            t_dAp = [sb(f"l0dAp{i}", [128, NST * PC0], F32, st)
                     for i in range(2)]
            t_T1p = [sb("l0T1p0", [128, NST * PC0], F32, st)]
            t_Bup = [sb("l0Bup0", [128, NST * PC0], F32, st)]
            t_hp = [sb(f"l0hp{i}", [128, NST * PC0], F32, st)
                     for i in range(3)]
            t_Bp = sb("l0Bp", [128, NST * PC0], F32, st)
            t_Cp = sb("l0Cp", [128, NST * PC0], F32, st)
            t_sg = sb("l0sg", [128, NST], F32, st)
            t_f = [sb(f"l0f{i}", [128, PC0], F32, st) for i in range(2)]
            t_xtm = [sb(f"l0xt{i}", [128, D0], F32, st) for i in range(2)]
            t_o = [sb("l0o0", [128, D0], F32, st)]
            p96 = ps("l0p96", [96, PC0], F32, st)
            pdt = [ps(f"l0pdt{i}", [128, PC0], F32, st) for i in range(2)]
            ptm = [ps(f"l0ptm{i}", [128, D0], F32, st) for i in range(2)]

            for dc in range(NDC0):
                load_bf(t_xp[:, dc * 96:(dc + 1) * 96],
                        xp0T[dc * 128:(dc + 1) * 128, :], ("xp0", dc), st, stg)
            load_bf(t_dt[0:RNK, :], dt0T, "dt0", st, stg, rows=RNK)
            p.op("dve", lambda E: E.memset(t_st0[:], 0.0), writes=["st0"])

            for b in range(B_LOC):
                if b > 0:
                    p.op("dve", lambda E: E.memset(t_st0[:], 0.0),
                         writes=["st0"])
                for pc in range(L // PC0):
                    c0 = b * L + pc * PC0
                    for dc in range(NDC0):
                        if pc == 0:
                            p.op("dve", lambda E, dc=dc: E.memset(
                                t_xh[dc][:, 0:3], 0.0), writes=[("xh", dc)])
                            p.dma("sp", t_xh[dc][:, 3:3 + PC0],
                                  XN32[dc * 128:(dc + 1) * 128, c0:c0 + PC0],
                                  writes=[("xh", dc)])
                        else:
                            p.dma("sp", t_xh[dc][:, 0:PC0 + 3],
                                  XN32[dc * 128:(dc + 1) * 128,
                                       c0 - 3:c0 + PC0],
                                  writes=[("xh", dc)])
                    for dc in range(NDC0):
                        p.op("dve", lambda E, dc=dc: E.tensor_scalar(
                            out=t_co[dc][:], in0=t_xh[dc][:, 3:3 + PC0],
                            scalar1=t_cw0[:, dc * 4 + 3:dc * 4 + 4],
                            scalar2=t_cb0[:, dc:dc + 1],
                            op0=ALU.mult, op1=ALU.add),
                            reads=[("xh", dc), "cw0", "cb0"],
                            writes=[("co", dc)])
                        for j in range(3):
                            p.op("dve", lambda E, dc=dc, j=j:
                                 E.scalar_tensor_tensor(
                                     out=t_co[dc][:],
                                     in0=t_xh[dc][:, j:j + PC0],
                                     scalar=t_cw0[:, dc * 4 + j:dc * 4 + j + 1],
                                     in1=t_co[dc][:],
                                     op0=ALU.mult, op1=ALU.add),
                                 reads=[("xh", dc), "cw0", ("co", dc)],
                                 writes=[("co", dc)])
                    if L0_DEBUG:
                        for dc in range(NDC0):
                            p.dma("act", XNBF[dc * 128:(dc + 1) * 128,
                                              c0:c0 + PC0], t_co[dc][:],
                                  reads=[("co", dc)], writes=[("dbgC", dc)])
                    for dc in range(NDC0):
                        p.op("pe", lambda E, dc=dc: E.matmul(
                            out=p96[:], lhsT=t_xp[:, dc * 96:(dc + 1) * 96],
                            rhs=t_co[dc][:], start=(dc == 0),
                            stop=(dc == NDC0 - 1)),
                            reads=[("xp0", dc), ("co", dc)],
                            writes=(["p96"] if dc == NDC0 - 1 else []),
                            extra=p.predeps("p96") if dc == 0 else [])
                    p.op("act", lambda E: E.copy(out=t_bc[:], in_=p96[64:96, :]),
                         reads=["p96"], writes=["bc"])
                    p.op("act", lambda E: E.copy(out=t_dtl[:],
                                                 in_=p96[0:64, :]),
                         reads=["p96"], writes=["dtl"])
                    p.dma("sp", BCR[0, b, :, pc * PC0:pc * PC0 + PC0],
                          t_bc[:], reads=["bc"], writes=[("BCR0", b, pc)])
                    for dc in range(NDC0):
                        kq = dc % 2
                        p.op("pe", lambda E, dc=dc, kq=kq: E.matmul(
                            out=pdt[kq][:], lhsT=t_dt[0:RNK,
                                                      dc * 128:(dc + 1) * 128],
                            rhs=t_dtl[:], start=True, stop=True),
                            reads=["dt0", "dtl"], writes=[("pdt", kq)],
                            extra=p.predeps(("pdt", kq)))
                        p.op("dve", lambda E, dc=dc, kq=kq: E.tensor_scalar(
                            out=t_t[:], in0=pdt[kq][:],
                            scalar1=t_db0[:, dc:dc + 1], scalar2=None,
                            op0=ALU.add),
                            reads=[("pdt", kq), "db0"], writes=["spt"])
                        p._record(p.wr["spt"], [], [("pdtq", kq)])
                        p.op("dve", lambda E: E.scalar_tensor_tensor(
                            out=t_a[:], in0=t_t[:], scalar=-1.0, in1=t_t[:],
                            op0=ALU.mult, op1=ALU.max),
                            reads=["spt"], writes=["spa"])
                        p.op("act", lambda E: E.activation(
                            out=t_a[:], in_=t_a[:], func=AF.Exp, scale=-1.0),
                            reads=["spa"], writes=["spa"])
                        p.op("act", lambda E: E.activation(
                            out=t_a[:], in_=t_a[:], func=AF.Ln, bias=1.0),
                            reads=["spa"], writes=["spa"])
                        p.op("dve", lambda E: E.tensor_scalar_max(
                            out=t_r[:], in0=t_t[:], scalar1=0.0),
                            reads=["spt"], writes=["spr"])
                        p.op("dve", lambda E, dc=dc: E.tensor_add(
                            out=t_dl[dc][:], in0=t_r[:], in1=t_a[:]),
                            reads=["spr", "spa"], writes=[("dl", dc)])
                        if L0_DEBUG:
                            p.dma("act", D1[dc * 128:(dc + 1) * 128,
                                            c0:c0 + PC0], t_dl[dc][:],
                                  reads=[("dl", dc)], writes=[("dbgD", dc)])
                    p.dma("sp", t_Bp[:],
                          BCR[0, b, 0:16,
                              pc * PC0:pc * PC0 + PC0].rearrange(
                              "(one n) l -> one n l", one=1).broadcast_to(
                              [128, NST, PC0]),
                          reads=[("BCR0", b, pc)], writes=["Bp"])
                    p.dma("sp", t_Cp[:],
                          BCR[0, b, 16:32,
                              pc * PC0:pc * PC0 + PC0].rearrange(
                              "(one n) l -> one n l", one=1).broadcast_to(
                              [128, NST, PC0]),
                          reads=[("BCR0", b, pc)], writes=["Cp"])
                    WP = NST * PC0

                    def l0_A(dc):
                        kq = dc % 2
                        kh = dc % 3
                        for n in range(NST):
                            p.op("act", lambda E, dc=dc, n=n, kq=kq:
                                 E.activation(
                                     out=t_dAp[kq][:, n * PC0:(n + 1) * PC0],
                                     in_=t_dl[dc][:], func=AF.Exp,
                                     scale=t_A0[:, dc * NST + n:
                                                dc * NST + n + 1]),
                                 reads=[("dl", dc), "A0"],
                                 writes=[("dAp", kq, n)])
                            p.op("act", lambda E, dc=dc, n=n, kq=kq:
                                 E.activation(
                                     out=t_T1p[0][:, n * PC0:(n + 1) * PC0],
                                     in_=t_dAp[kq][:, n * PC0:(n + 1) * PC0],
                                     func=AF.Abs,
                                     scale=t_iA0[:, dc * NST + n:
                                                 dc * NST + n + 1],
                                     bias=t_niA0[:, dc * NST + n:
                                                 dc * NST + n + 1]),
                                 reads=[("dAp", kq, n), "iA0", "niA0"],
                                 writes=[("T1p", 0, n)])
                        dA_all = [("dAp", kq, n) for n in range(NST)]
                        T1_all = [("T1p", 0, n) for n in range(NST)]
                        p.op("dve", lambda E, dc=dc: E.tensor_tensor(
                            out=t_Bup[0][:].rearrange(
                                "p (n l) -> p n l", n=NST),
                            in0=t_xh[dc][:, 3:3 + PC0].rearrange(
                                "p (one l) -> p one l", one=1).broadcast_to(
                                [128, NST, PC0]),
                            in1=t_Bp[:].rearrange("p (n l) -> p n l", n=NST),
                            op=ALU.mult),
                            reads=[("xh", dc), "Bp"], writes=[("Bup", 0)])
                        p.op("dve", lambda E: E.tensor_mul(
                            out=t_Bup[0][:], in0=t_T1p[0][:],
                            in1=t_Bup[0][:]),
                            reads=T1_all + [("Bup", 0)], writes=[("Bup", 0)])
                        p.op("dve", lambda E, dc=dc, kq=kq: E.tensor_tensor(
                            out=t_sg[:], in0=t_dAp[kq][:, 0:WP:PC0],
                            in1=t_st0[:, dc * NST:(dc + 1) * NST],
                            op=ALU.mult),
                            reads=dA_all + ["st0"], writes=["sg0"])
                        p.op("dve", lambda E: E.tensor_tensor(
                            out=t_Bup[0][:, 0:WP:PC0],
                            in0=t_Bup[0][:, 0:WP:PC0], in1=t_sg[:],
                            op=ALU.add),
                            reads=["sg0", ("Bup", 0)], writes=[("Bup", 0)])
                        p.op("dve", lambda E, kq=kq: E.memset(
                            t_dAp[kq][:, 0:WP:PC0], 0.0),
                            reads=[], writes=dA_all)
                        p.op("dve", lambda E, kq=kq, kh=kh:
                             E.tensor_tensor_scan(
                                 out=t_hp[kh][:], data0=t_dAp[kq][:],
                                 data1=t_Bup[0][:], initial=0.0,
                                 op0=ALU.mult, op1=ALU.add),
                             reads=dA_all + [("Bup", 0)],
                             writes=[("hp", kh)])
                        p.op("dve", lambda E, dc=dc, kh=kh: E.tensor_copy(
                            out=t_st0[:, dc * NST:(dc + 1) * NST],
                            in_=t_hp[kh][:, PC0 - 1:WP:PC0]),
                            reads=[("hp", kh)], writes=["st0"])

                    def l0_B(dc):
                        kh = dc % 3
                        p.op("pool", lambda E, kh=kh: E.tensor_mul(
                            out=t_hp[kh][:], in0=t_hp[kh][:], in1=t_Cp[:]),
                            reads=[("hp", kh), "Cp"],
                            writes=[("hp", kh)])
                        p.op("pool", lambda E, kh=kh: E.tensor_add(
                            out=t_hp[kh][:, 0:8 * PC0],
                            in0=t_hp[kh][:, 0:8 * PC0],
                            in1=t_hp[kh][:, 8 * PC0:16 * PC0]),
                            reads=[("hp", kh)], writes=[("hp", kh)])
                        for w in (4, 2):
                            p.op("dve", lambda E, kh=kh, w=w: E.tensor_add(
                                out=t_hp[kh][:, 0:w * PC0],
                                in0=t_hp[kh][:, 0:w * PC0],
                                in1=t_hp[kh][:, w * PC0:2 * w * PC0]),
                                reads=[("hp", kh)], writes=[("hp", kh)])
                        p.op("dve", lambda E, dc=dc, kh=kh: E.tensor_add(
                            out=t_y[dc][:], in0=t_hp[kh][:, 0:PC0],
                            in1=t_hp[kh][:, PC0:2 * PC0]),
                            reads=[("hp", kh)],
                            writes=[("y", dc), ("hp", kh)])

                    for dc in range(NDC0):
                        l0_A(dc)
                        if dc >= 2:
                            l0_B(dc - 2)
                    l0_B(NDC0 - 2)
                    l0_B(NDC0 - 1)
                    for dc in range(NDC0):
                        kq = dc % 2
                        if L0_DEBUG:
                            p.dma("act", U1[dc * 128:(dc + 1) * 128,
                                            c0:c0 + PC0], t_y[dc][:],
                                  reads=[("y", dc)], writes=[("dbgY", dc)])
                        p.op("dve", lambda E, dc=dc: E.scalar_tensor_tensor(
                            out=t_y[dc][:], in0=t_xh[dc][:, 3:3 + PC0],
                            scalar=t_dsk[:, dc:dc + 1], in1=t_y[dc][:],
                            op0=ALU.mult, op1=ALU.add),
                            reads=[("xh", dc), "dsk", ("y", dc)],
                            writes=[("y", dc)])
                        p.dma("sp", t_f[kq][:],
                              FFN0[dc * 128:(dc + 1) * 128, c0:c0 + PC0],
                              writes=[("ff", kq)])
                        p.op("dve", lambda E, dc=dc, kq=kq: E.tensor_add(
                            out=t_y[dc][:], in0=t_y[dc][:], in1=t_f[kq][:]),
                            reads=[("ff", kq), ("y", dc)], writes=[("y", dc)])
                    for tb in range(PC0 // 128):
                        kq = tb % 2
                        for dc in range(NDC0):
                            p.op("pe", lambda E, dc=dc, tb=tb, kq=kq:
                                 E.transpose(
                                     out=ptm[kq][:, dc * 128:(dc + 1) * 128],
                                     in_=t_y[dc][:, tb * 128:(tb + 1) * 128],
                                     identity=t_id_f[:]),
                                 reads=[("y", dc), "idf"],
                                 writes=[("ptm", kq, dc)])
                        r0 = pc * PC0 + tb * 128
                        p.dma("sp", t_xtm[kq][:], x_in[b, r0:r0 + 128, :],
                              writes=[("xtm", kq)])
                        p.op("dve", lambda E, kq=kq: E.tensor_add(
                            out=t_o[0][:], in0=ptm[kq][:], in1=t_xtm[kq][:]),
                            reads=[("xtm", kq)] + [("ptm", kq, d)
                                                   for d in range(NDC0)],
                            writes=[("otm", 0)])
                        p.dma("sp", X1TM[b, r0:r0 + 128, :], t_o[0][:],
                              reads=[("otm", 0)], writes=[("X1TM", b, r0)])
            p.barrier()

    def emit_i1():
        with contextlib.ExitStack() as st:
            t_ip = sb("i1ip", [128, NDC0 * 4096], BF16, st)
            stg = sb("i1stg", [128, 4096], F32, st)
            t_xnb = [sb(f"i1x{d}", [128, PCF], BF16, st) for d in range(NDC0)]
            t_xh = [sb(f"i1xh{d}", [128, PCF + 3], F32, st)
                    for d in range(NDC1)]
            t_tail = sb("i1tail", [128, 3 * NDC1], F32, st)
            t_zs = [sb(f"i1z{i}", [128, PCF], F32, st) for i in range(2)]
            t_u = [sb(f"i1u{d}", [128, PCF], F32, st) for d in range(2)]
            t_ub = [sb(f"i1ub{d}", [128, PCF], BF16, st) for d in range(2)]
            t_xp = sb("i1xp", [128, NDC1 * 96], BF16, st)
            t_dt = sb("i1dt", [RNK, DI], BF16, st)
            t_bc = sb("i1bc", [32, PCF], F32, st)
            t_dtl = sb("i1dtl", [RNK, PCF], BF16, st)
            t_t = sb("i1t", [128, PCF], F32, st)
            t_a = sb("i1a", [128, PCF], F32, st)
            t_r = sb("i1r", [128, PCF], F32, st)
            t_dl = [sb(f"i1dl{i}", [128, PCF], F32, st) for i in range(2)]
            pxz = [ps(f"i1px{i}", [128, PCF], F32, st) for i in range(4)]
            p96 = ps("i1p96", [96, PCF], F32, st)
            pdt = [ps(f"i1pdt{i}", [128, PCF], F32, st) for i in range(2)]

            for kc in range(NDC0):
                load_bf(t_ip[:, kc * 4096:(kc + 1) * 4096],
                        ipT[kc * 128:(kc + 1) * 128, :], ("ip", kc), st, stg)
            for dc in range(NDC1):
                load_bf(t_xp[:, dc * 96:(dc + 1) * 96],
                        xp1T[dc * 128:(dc + 1) * 128, :], ("xp1", dc), st, stg)
            load_bf(t_dt[0:RNK, :], dt1T, "dt1", st, stg, rows=RNK)

            for pc in range(T // PCF):
                c0 = pc * PCF
                new_seq = (c0 % L) == 0
                for dc in range(NDC0):
                    p.dma("sp", t_xnb[dc][:],
                          XNBF[dc * 128:(dc + 1) * 128, c0:c0 + PCF],
                          writes=[("i1x", dc)])
                for do in range(NDC1):
                    kq = do % 4
                    for kc in range(NDC0):
                        p.op("pe", lambda E, do=do, kc=kc, kq=kq: E.matmul(
                            out=pxz[kq][:],
                            lhsT=t_ip[:, kc * 4096 + do * 128:
                                      kc * 4096 + (do + 1) * 128],
                            rhs=t_xnb[kc][:], start=(kc == 0),
                            stop=(kc == NDC0 - 1)),
                            reads=[("ip", kc), ("i1x", kc)],
                            writes=([("pxz", kq)] if kc == NDC0 - 1 else []),
                            extra=p.predeps(("pxz", kq)) if kc == 0 else [])
                    if new_seq:
                        p.op("dve", lambda E, do=do: E.memset(
                            t_xh[do][:, 0:3], 0.0), writes=[("i1xh", do)])
                    else:
                        p.op("dve", lambda E, do=do: E.tensor_copy(
                            out=t_xh[do][:, 0:3],
                            in_=t_tail[:, do * 3:(do + 1) * 3]),
                            reads=[("tail", do)], writes=[("i1xh", do)])
                    p.op("act", lambda E, do=do, kq=kq: E.copy(
                        out=t_xh[do][:, 3:3 + PCF], in_=pxz[kq][:]),
                        reads=[("pxz", kq)], writes=[("i1xh", do)])
                    p.op("dve", lambda E, do=do: E.tensor_copy(
                        out=t_tail[:, do * 3:(do + 1) * 3],
                        in_=t_xh[do][:, PCF:PCF + 3]),
                        reads=[("i1xh", do)], writes=[("tail", do)])
                for dz in range(NDC1):
                    do = NDC1 + dz
                    kq = do % 4
                    kz = dz % 2
                    for kc in range(NDC0):
                        p.op("pe", lambda E, do=do, kc=kc, kq=kq: E.matmul(
                            out=pxz[kq][:],
                            lhsT=t_ip[:, kc * 4096 + do * 128:
                                      kc * 4096 + (do + 1) * 128],
                            rhs=t_xnb[kc][:], start=(kc == 0),
                            stop=(kc == NDC0 - 1)),
                            reads=[("ip", kc), ("i1x", kc)],
                            writes=([("pxz", kq)] if kc == NDC0 - 1 else []),
                            extra=p.predeps(("pxz", kq)) if kc == 0 else [])
                    p.op("act", lambda E, dz=dz, kq=kq, kz=kz: E.activation(
                        out=t_zs[kz][:], in_=pxz[kq][:], func=AF.Silu),
                        reads=[("pxz", kq)], writes=[("zs", kz)])
                    p.dma("sp", ZS[dz * 128:(dz + 1) * 128, c0:c0 + PCF],
                          t_zs[kz][:], reads=[("zs", kz)],
                          writes=[("ZS", pc, dz)])
                for dc in range(NDC1):
                    ku = dc % 2
                    p.op("dve", lambda E, dc=dc, ku=ku: E.tensor_scalar(
                        out=t_u[ku][:], in0=t_xh[dc][:, 3:3 + PCF],
                        scalar1=t_cw1[:, dc * 4 + 3:dc * 4 + 4],
                        scalar2=t_cb1[:, dc:dc + 1],
                        op0=ALU.mult, op1=ALU.add),
                        reads=[("i1xh", dc), "cw1", "cb1"],
                        writes=[("u", ku)])
                    for j in range(3):
                        p.op("dve", lambda E, dc=dc, j=j, ku=ku:
                             E.scalar_tensor_tensor(
                                 out=t_u[ku][:], in0=t_xh[dc][:, j:j + PCF],
                                 scalar=t_cw1[:, dc * 4 + j:dc * 4 + j + 1],
                                 in1=t_u[ku][:], op0=ALU.mult, op1=ALU.add),
                             reads=[("i1xh", dc), "cw1", ("u", ku)],
                             writes=[("u", ku)])
                    p.op("act", lambda E, ku=ku: E.activation(
                        out=t_u[ku][:], in_=t_u[ku][:], func=AF.Silu),
                        reads=[("u", ku)], writes=[("u", ku)])
                    p.dma("sp", U1[dc * 128:(dc + 1) * 128, c0:c0 + PCF],
                          t_u[ku][:], reads=[("u", ku)],
                          writes=[("U1", pc, dc)])
                    p.op("dve", lambda E, ku=ku: E.tensor_copy(
                        out=t_ub[ku][:], in_=t_u[ku][:]),
                        reads=[("u", ku)], writes=[("ub", ku)])
                    p.op("pe", lambda E, dc=dc, ku=ku: E.matmul(
                        out=p96[:], lhsT=t_xp[:, dc * 96:(dc + 1) * 96],
                        rhs=t_ub[ku][:], start=(dc == 0),
                        stop=(dc == NDC1 - 1)),
                        reads=[("xp1", dc), ("ub", ku)],
                        writes=(["i1p96"] if dc == NDC1 - 1 else []),
                        extra=p.predeps("i1p96") if dc == 0 else [])
                p.op("act", lambda E: E.copy(out=t_bc[:], in_=p96[64:96, :]),
                     reads=["i1p96"], writes=["i1bc"])
                p.op("act", lambda E: E.copy(out=t_dtl[:], in_=p96[0:64, :]),
                     reads=["i1p96"], writes=["i1dtl"])
                b, lc = divmod(c0, L)
                p.dma("sp", BCR[1, b, :, lc:lc + PCF], t_bc[:],
                      reads=["i1bc"], writes=[("BCR1", pc)])
                for dc in range(NDC1):
                    kq = dc % 2
                    p.op("pe", lambda E, dc=dc, kq=kq: E.matmul(
                        out=pdt[kq][:],
                        lhsT=t_dt[0:RNK, dc * 128:(dc + 1) * 128],
                        rhs=t_dtl[:], start=True, stop=True),
                        reads=["dt1", "i1dtl"], writes=[("i1pdt", kq)],
                        extra=p.predeps(("i1pdt", kq)))
                    p.op("dve", lambda E, dc=dc, kq=kq: E.tensor_scalar(
                        out=t_t[:], in0=pdt[kq][:],
                        scalar1=t_db1[:, dc:dc + 1], scalar2=None,
                        op0=ALU.add),
                        reads=[("i1pdt", kq), "db1"], writes=["i1spt"])
                    p._record(p.wr["i1spt"], [], [("i1pdtq", kq)])
                    p.op("dve", lambda E: E.scalar_tensor_tensor(
                        out=t_a[:], in0=t_t[:], scalar=-1.0, in1=t_t[:],
                        op0=ALU.mult, op1=ALU.max),
                        reads=["i1spt"], writes=["i1spa"])
                    p.op("act", lambda E: E.activation(
                        out=t_a[:], in_=t_a[:], func=AF.Exp, scale=-1.0),
                        reads=["i1spa"], writes=["i1spa"])
                    p.op("act", lambda E: E.activation(
                        out=t_a[:], in_=t_a[:], func=AF.Ln, bias=1.0),
                        reads=["i1spa"], writes=["i1spa"])
                    p.op("dve", lambda E: E.tensor_scalar_max(
                        out=t_r[:], in0=t_t[:], scalar1=0.0),
                        reads=["i1spt"], writes=["i1spr"])
                    kd = dc % 2
                    p.op("dve", lambda E, kd=kd: E.tensor_add(
                        out=t_dl[kd][:], in0=t_r[:], in1=t_a[:]),
                        reads=["i1spr", "i1spa"], writes=[("i1dl", kd)])
                    p.dma("sp", D1[dc * 128:(dc + 1) * 128, c0:c0 + PCF],
                          t_dl[kd][:], reads=[("i1dl", kd)],
                          writes=[("D1", pc, dc)])
            p.barrier()

    def emit_moe():
        with contextlib.ExitStack() as st:
            stg = sb("mstg", [128, 1024], F32, st)
            t_gvt = [sb(f"mgvt{b}", [128, 16], F32, st) for b in range(B_LOC)]
            t_pfT = [sb(f"mpfT{b}", [128, 128], F32, st)
                     for b in range(B_LOC)]
            t_mio = sb("mmio", [128, MTOP], F32, st)
            t_mioc = sb("mmioc", [128, 2], F32, st)
            stA = contextlib.ExitStack()
            t_gwb = sb("mgw", [128, NDC0 * 8], BF16, stA)
            t_xnb = [sb(f"mx{i}", [128, 512], BF16, stA) for i in range(2)]
            t_gl = [sb(f"mgl{b}", [8, L], F32, stA) for b in range(B_LOC)]
            t_ex = [sb(f"mex{b}", [8, L], F32, stA) for b in range(B_LOC)]
            t_tt = [sb(f"mtt{b}", [128, 128], F32, stA) for b in range(B_LOC)]
            t_sc1 = [sb(f"msc1{b}", [8, 1], F32, stA) for b in range(B_LOC)]
            t_sc2 = [sb(f"msc2{b}", [8, 1], F32, stA) for b in range(B_LOC)]
            t_sc3 = [sb(f"msc3{b}", [8, 1], F32, stA) for b in range(B_LOC)]
            t_nmx = [sb(f"mnm{b}", [128, 16], F32, stA) for b in range(B_LOC)]
            t_exs = sb("mexs", [128, 8], F32, stA)
            t_sm = [sb(f"msm{b}", [128, 16], F32, stA) for b in range(B_LOC)]
            t_cg = [sb(f"mcg{b}", [8, L], F32, stA) for b in range(B_LOC)]
            t_lo = [sb(f"mlo{b}", [8, 1], F32, stA) for b in range(B_LOC)]
            t_hi = [sb(f"mhi{b}", [8, 1], F32, stA) for b in range(B_LOC)]
            t_ta = [sb(f"mta{b}", [8, 1], F32, stA) for b in range(B_LOC)]
            t_g = [sb(f"mg{b}", [8, 1], F32, stA) for b in range(B_LOC)]
            t_m = [sb(f"mm{b}", [8, L], F32, stA) for b in range(B_LOC)]
            t_c = [sb(f"mc{b}", [8, 1], F32, stA) for b in range(B_LOC)]
            t_ones = sb("mones", [8, L], F32, stA)
            t_fl = [sb(f"mfl{b}", [8, L], F32, stA) for b in range(B_LOC)]
            t_pf = [sb(f"mpf{b}", [8, L], F32, stA) for b in range(B_LOC)]
            t_W = [sb(f"mW{b}", [8, L], F32, stA) for b in range(B_LOC)]
            t_pF = [sb(f"mpF{b}", [8, L], F32, stA) for b in range(B_LOC)]
            t_pW = sb("mpW", [8, L], F32, stA)
            t_wcg = sb("mwcg", [8, L], F32, stA)
            t_gvr = sb("mgvr", [1, L], F32, stA)
            t_lt8 = sb("mlt8", [8, 8], BF16, stA)
            t_on8 = sb("mon8", [8, 1], F32, stA)

            for kc in range(NDC0):
                load_bf(t_gwb[:, kc * 8:(kc + 1) * 8],
                        gw[kc * 128:(kc + 1) * 128, :], ("gw", kc), st, stg)
            p.op("dve", lambda E: E.memset(t_ones[:], 1.0), writes=["mones"])
            p.op("dve", lambda E: E.memset(t_on8[:], 1.0), writes=["mon8"])
            p.dma("sp", t_lt8[:], lt8, writes=["mlt8"])
            p.dma("sp", t_mio[:], miota.broadcast_to([128, MTOP]),
                  writes=["mmio"])
            p.dma("sp", t_mioc[:], miocol, writes=["mmioc"])

            pg8 = ps("mpg", [8, 512], F32, stA)
            ptr = ps("mptr", [128, 8], F32, stA)
            ptr2 = ps("mptr2", [8, 128], F32, stA)
            pgv = ps("mpgv", [8, 512], F32, stA)

            for b in range(B_LOC):
                for half in range(L // 512):
                    c0 = b * L + half * 512
                    for kc in range(NDC0):
                        kx = kc % 2
                        p.dma("sp", t_xnb[kx][:],
                              XNBF[kc * 128:(kc + 1) * 128, c0:c0 + 512],
                              writes=[("mxn", kx)])
                        p.op("pe", lambda E, kc=kc, kx=kx: E.matmul(
                            out=pg8[:], lhsT=t_gwb[:, kc * 8:(kc + 1) * 8],
                            rhs=t_xnb[kx][:], start=(kc == 0),
                            stop=(kc == NDC0 - 1)),
                            reads=[("gw", kc), ("mxn", kx)],
                            writes=(["mpg"] if kc == NDC0 - 1 else []),
                            extra=p.predeps("mpg") if kc == 0 else [])
                    p.op("act", lambda E, b=b, half=half: E.copy(
                        out=t_gl[b][:, half * 512:(half + 1) * 512],
                        in_=pg8[:]), reads=["mpg"], writes=[("gl", b)])
            for b in range(B_LOC):
                p.op("dve", lambda E, b=b: E.tensor_scalar_max(
                    out=t_gl[b][:], in0=t_gl[b][:], scalar1=1e-6),
                    reads=[("gl", b)], writes=[("gl", b)])
                p.op("act", lambda E, b=b: E.activation(
                    out=t_gl[b][:], in_=t_gl[b][:], func=AF.Ln),
                    reads=[("gl", b)], writes=[("gl", b)])
            for it in range(8):
                for b in range(B_LOC):
                    p.op("dve", lambda E, b=b: E.reduce_max(
                        out=t_sc1[b][:], in_=t_gl[b][:],
                        axis=mybir.AxisListType.X, negate=True),
                        reads=[("gl", b)], writes=[("sc1", b)])
                    p.op("act", lambda E, b=b: E.activation(
                        out=t_ex[b][:], in_=t_gl[b][:], func=AF.Exp,
                        bias=t_sc1[b][:, 0:1]),
                        reads=[("gl", b), ("sc1", b)], writes=[("ex", b)])
                    p.op("dve", lambda E, b=b: E.reduce_sum(
                        out=t_sc2[b][:], in_=t_ex[b][:],
                        axis=mybir.AxisListType.X),
                        reads=[("ex", b)], writes=[("sc2", b)])
                    p.op("act", lambda E, b=b: E.activation(
                        out=t_sc3[b][:], in_=t_sc2[b][:], func=AF.Ln),
                        reads=[("sc2", b)], writes=[("sc3", b)])
                    p.op("dve", lambda E, b=b: E.tensor_sub(
                        out=t_sc3[b][:], in0=t_sc3[b][:], in1=t_sc1[b][:]),
                        reads=[("sc3", b), ("sc1", b)], writes=[("sc3", b)])
                    p.op("dve", lambda E, b=b: E.tensor_scalar(
                        out=t_gl[b][:], in0=t_gl[b][:],
                        scalar1=t_sc3[b][:, 0:1], scalar2=None,
                        op0=ALU.subtract),
                        reads=[("gl", b), ("sc3", b)], writes=[("gl", b)])
                    for lc in range(L // 128):
                        p.op("pe", lambda E, b=b, lc=lc: E.transpose(
                            out=ptr[:], in_=t_gl[b][0:8,
                                                    lc * 128:(lc + 1) * 128],
                            identity=t_id_f[0:8, 0:8]),
                            reads=[("gl", b), "idf"], writes=["mptr"])
                        p.op("act", lambda E, b=b, lc=lc: E.copy(
                            out=t_tt[b][:, lc * 8:(lc + 1) * 8], in_=ptr[:]),
                            reads=["mptr"], writes=[("mtt", b, lc)])
                    p.op("dve", lambda E, b=b: E.tensor_reduce(
                        out=t_nmx[b].rearrange("p (a one) -> p a one", one=1),
                        in_=t_tt[b][:].rearrange("p (a x) -> p a x", x=8),
                        axis=mybir.AxisListType.X, op=ALU.max, negate=True),
                        reads=[("mtt", b, lc) for lc in range(16)],
                        writes=[("nmx", b)])
                    for lc in range(16):
                        p.op("act", lambda E, b=b, lc=lc: E.activation(
                            out=t_exs[:],
                            in_=t_tt[b][:, lc * 8:(lc + 1) * 8],
                            func=AF.Exp, bias=t_nmx[b][:, lc:lc + 1],
                            accum_out=t_sm[b][:, lc:lc + 1]),
                            reads=[("mtt", b, lc), ("nmx", b)],
                            writes=[("msm", b, lc), "mexs"])
                    p.op("act", lambda E, b=b: E.activation(
                        out=t_sm[b][:], in_=t_sm[b][:], func=AF.Ln),
                        reads=[("msm", b, lc) for lc in range(16)],
                        writes=[("msml", b)])
                    p.op("dve", lambda E, b=b: E.tensor_sub(
                        out=t_sm[b][:], in0=t_sm[b][:], in1=t_nmx[b][:]),
                        reads=[("msml", b), ("nmx", b)], writes=[("msml", b)])
                    for lc in range(16):
                        p.op("dve", lambda E, b=b, lc=lc: E.tensor_scalar(
                            out=t_tt[b][:, lc * 8:(lc + 1) * 8],
                            in0=t_tt[b][:, lc * 8:(lc + 1) * 8],
                            scalar1=t_sm[b][:, lc:lc + 1], scalar2=None,
                            op0=ALU.subtract),
                            reads=[("msml", b), ("mtt", b, lc)],
                            writes=[("mtt", b, lc)])
                        p.op("pe", lambda E, b=b, lc=lc: E.transpose(
                            out=ptr2[:], in_=t_tt[b][:, lc * 8:(lc + 1) * 8],
                            identity=t_id_f[:]),
                            reads=[("mtt", b, lc), "idf"], writes=["mptr2"])
                        p.op("act", lambda E, b=b, lc=lc: E.copy(
                            out=t_gl[b][0:8, lc * 128:(lc + 1) * 128],
                            in_=ptr2[0:8, :]),
                            reads=["mptr2"], writes=[("gl", b)])
            for b in range(B_LOC):
                p.op("act", lambda E, b=b: E.activation(
                    out=t_cg[b][:], in_=t_gl[b][:], func=AF.Exp),
                    reads=[("gl", b)], writes=[("cg", b)])
                p.op("dve", lambda E, b=b: E.memset(t_lo[b][:], 0.0),
                     writes=[("lo", b)])
                p.op("dve", lambda E, b=b: E.memset(t_hi[b][:], 1.01),
                     writes=[("hi", b)])
            for it in range(30):
                for b in range(B_LOC):
                    p.op("dve", lambda E, b=b: E.tensor_add(
                        out=t_ta[b][:], in0=t_lo[b][:], in1=t_hi[b][:]),
                        reads=[("lo", b), ("hi", b)], writes=[("ta", b)])
                    p.op("dve", lambda E, b=b: E.tensor_scalar_mul(
                        out=t_ta[b][:], in0=t_ta[b][:], scalar1=0.5),
                        reads=[("ta", b)], writes=[("ta", b)])
                    p.op("dve", lambda E, b=b: E.tensor_scalar(
                        out=t_m[b][:], in0=t_cg[b][:],
                        scalar1=t_ta[b][:, 0:1], scalar2=None, op0=ALU.is_gt),
                        reads=[("cg", b), ("ta", b)], writes=[("msk", b)])
                    p.op("dve", lambda E, b=b: E.reduce_sum(
                        out=t_c[b][:], in_=t_m[b][:],
                        axis=mybir.AxisListType.X),
                        reads=[("msk", b)], writes=[("cnt", b)])
                    p.op("dve", lambda E, b=b: E.tensor_scalar(
                        out=t_g[b][:], in0=t_c[b][:], scalar1=float(MTOP),
                        scalar2=None, op0=ALU.is_gt),
                        reads=[("cnt", b)], writes=[("g", b)])
                    p.op("dve", lambda E, b=b: E.tensor_sub(
                        out=t_c[b][:], in0=t_ta[b][:], in1=t_lo[b][:]),
                        reads=[("ta", b), ("lo", b)], writes=[("cnt", b)])
                    p.op("dve", lambda E, b=b: E.tensor_mul(
                        out=t_c[b][:], in0=t_c[b][:], in1=t_g[b][:]),
                        reads=[("cnt", b), ("g", b)], writes=[("cnt", b)])
                    p.op("dve", lambda E, b=b: E.tensor_add(
                        out=t_lo[b][:], in0=t_lo[b][:], in1=t_c[b][:]),
                        reads=[("cnt", b), ("lo", b)], writes=[("lo", b)])
                    p.op("dve", lambda E, b=b: E.tensor_sub(
                        out=t_c[b][:], in0=t_hi[b][:], in1=t_ta[b][:]),
                        reads=[("hi", b), ("ta", b)], writes=[("cnt", b)])
                    p.op("dve", lambda E, b=b: E.tensor_mul(
                        out=t_c[b][:], in0=t_c[b][:], in1=t_g[b][:]),
                        reads=[("cnt", b), ("g", b)], writes=[("cnt", b)])
                    p.op("dve", lambda E, b=b: E.tensor_add(
                        out=t_hi[b][:], in0=t_ta[b][:], in1=t_c[b][:]),
                        reads=[("cnt", b), ("ta", b)], writes=[("hi", b)])
            for b in range(B_LOC):
                p.op("dve", lambda E, b=b: E.tensor_scalar(
                    out=t_fl[b][:], in0=t_cg[b][:], scalar1=t_hi[b][:, 0:1],
                    scalar2=None, op0=ALU.is_gt),
                    reads=[("cg", b), ("hi", b)], writes=[("str", b)])
                p.op("dve", lambda E, b=b: E.reduce_sum(
                    out=t_c[b][:], in_=t_fl[b][:], axis=mybir.AxisListType.X),
                    reads=[("str", b)], writes=[("c1", b)])
                p.op("dve", lambda E, b=b: E.tensor_scalar(
                    out=t_c[b][:], in0=t_c[b][:], scalar1=-1.0,
                    scalar2=float(MTOP), op0=ALU.mult, op1=ALU.add),
                    reads=[("c1", b)], writes=[("c1", b)])
                p.op("dve", lambda E, b=b: E.tensor_scalar(
                    out=t_m[b][:], in0=t_cg[b][:], scalar1=t_lo[b][:, 0:1],
                    scalar2=None, op0=ALU.is_gt),
                    reads=[("cg", b), ("lo", b)], writes=[("msk", b)])
                p.op("dve", lambda E, b=b: E.tensor_sub(
                    out=t_m[b][:], in0=t_m[b][:], in1=t_fl[b][:]),
                    reads=[("msk", b), ("str", b)], writes=[("msk", b)])
                p.op("dve", lambda E, b=b: E.tensor_tensor_scan(
                    out=t_pf[b][:], data0=t_ones[0:8, :], data1=t_m[b][:],
                    initial=0.0, op0=ALU.mult, op1=ALU.add),
                    reads=[("msk", b), "mones"], writes=[("pf", b)])
                p.op("dve", lambda E, b=b: E.tensor_scalar(
                    out=t_pf[b][:], in0=t_pf[b][:], scalar1=t_c[b][:, 0:1],
                    scalar2=None, op0=ALU.is_le),
                    reads=[("pf", b), ("c1", b)], writes=[("pf", b)])
                p.op("dve", lambda E, b=b: E.tensor_mul(
                    out=t_m[b][:], in0=t_m[b][:], in1=t_pf[b][:]),
                    reads=[("msk", b), ("pf", b)], writes=[("msk", b)])
                p.op("dve", lambda E, b=b: E.tensor_add(
                    out=t_fl[b][:], in0=t_fl[b][:], in1=t_m[b][:]),
                    reads=[("str", b), ("msk", b)], writes=[("fl", b)])
                p.op("dve", lambda E, b=b: E.tensor_tensor_scan(
                    out=t_pf[b][:], data0=t_ones[0:8, :], data1=t_fl[b][:],
                    initial=0.0, op0=ALU.mult, op1=ALU.add),
                    reads=[("fl", b), "mones"], writes=[("pf", b)])
                p.op("dve", lambda E, b=b: E.tensor_sub(
                    out=t_pf[b][:], in0=t_pf[b][:], in1=t_fl[b][:]),
                    reads=[("pf", b), ("fl", b)], writes=[("pf", b)])
                t_flb = sb(f"mflb{b}", [8, L], BF16, stA)
                p.op("dve", lambda E, b=b, t_flb=t_flb: E.tensor_copy(
                    out=t_flb[:], in_=t_fl[b][:]),
                    reads=[("fl", b)], writes=[("flb", b)])
                for sl in range(4):
                    p.op("pe", lambda E, b=b, sl=sl, t_flb=t_flb: E.matmul(
                        out=pg8[:], lhsT=t_lt8[:],
                        rhs=t_flb[:, sl * 512:(sl + 1) * 512],
                        start=True, stop=True),
                        reads=["mlt8", ("flb", b)], writes=["mpg"],
                        extra=p.predeps("mpg"))
                    p.op("dve", lambda E, b=b, sl=sl: E.tensor_scalar(
                        out=t_W[b][:, sl * 512:(sl + 1) * 512], in0=pg8[:],
                        scalar1=0.5, scalar2=None, op0=ALU.is_lt),
                        reads=["mpg"], writes=[("W", b)])
                    p._record(p.wr[("W", b)], [], ["mpg"])
                p.op("dve", lambda E, b=b: E.tensor_mul(
                    out=t_W[b][:], in0=t_W[b][:], in1=t_fl[b][:]),
                    reads=[("W", b), ("fl", b)], writes=[("W", b)])
                p.op("dve", lambda E, b=b: E.tensor_mul(
                    out=t_wcg[:], in0=t_W[b][:], in1=t_cg[b][:]),
                    reads=[("W", b), ("cg", b)], writes=["wcg"])
                for sl in range(4):
                    p.op("pe", lambda E, sl=sl: E.matmul(
                        out=pgv[0:1, :], lhsT=t_on8[:],
                        rhs=t_wcg[:, sl * 512:(sl + 1) * 512],
                        start=True, stop=True),
                        reads=["mon8", "wcg"], writes=["mpgv"],
                        extra=p.predeps("mpgv"))
                    p.op("act", lambda E, sl=sl: E.copy(
                        out=t_gvr[:, sl * 512:(sl + 1) * 512],
                        in_=pgv[0:1, :]),
                        reads=["mpgv"], writes=["gvr"])
                    p._record(p.wr["gvr"], [], ["mpgv"])
                p.dma("act", GVD, t_gvr[:], reads=["gvr"], writes=["GVD"])
                p.dma("sp", t_gvt[b][:],
                      GVD.rearrange("one (c pp) -> (one pp) c", pp=128),
                      reads=["GVD"], writes=[("gvt", b)],
                      allow_slow_non_contiguous=True)
                p.op("dve", lambda E, b=b: E.tensor_mul(
                    out=t_pF[b][:], in0=t_pf[b][:], in1=t_fl[b][:]),
                    reads=[("pf", b), ("fl", b)], writes=[("pF", b)])
                p.op("dve", lambda E, b=b: E.tensor_scalar(
                    out=t_m[b][:], in0=t_fl[b][:], scalar1=-1.0,
                    scalar2=None, op0=ALU.add),
                    reads=[("fl", b)], writes=[("msk", b)])
                p.op("dve", lambda E, b=b: E.tensor_add(
                    out=t_pF[b][:], in0=t_pF[b][:], in1=t_m[b][:]),
                    reads=[("pF", b), ("msk", b)], writes=[("pF", b)])
                p.op("dve", lambda E, b=b: E.tensor_mul(
                    out=t_pW[:], in0=t_pf[b][:], in1=t_W[b][:]),
                    reads=[("pf", b), ("W", b)], writes=["pW"])
                p.op("dve", lambda E, b=b: E.tensor_scalar(
                    out=t_m[b][:], in0=t_W[b][:], scalar1=-1.0,
                    scalar2=None, op0=ALU.add),
                    reads=[("W", b)], writes=[("msk", b)])
                p.op("dve", lambda E, b=b: E.tensor_add(
                    out=t_pW[:], in0=t_pW[:], in1=t_m[b][:]),
                    reads=["pW", ("msk", b)], writes=["pW"])
                p.dma("act", PWD[b], t_pW[:], reads=["pW"],
                      writes=[("PWD", b)])
                for lc in range(16):
                    p.op("pe", lambda E, b=b, lc=lc: E.transpose(
                        out=ptr[:], in_=t_pF[b][0:8,
                                               lc * 128:(lc + 1) * 128],
                        identity=t_id_f[0:8, 0:8]),
                        reads=[("pF", b), "idf"], writes=["mptr"])
                    p.op("act", lambda E, b=b, lc=lc: E.copy(
                        out=t_pfT[b][:, lc * 8:(lc + 1) * 8], in_=ptr[:]),
                        reads=["mptr"], writes=[("pfT", b)])
            p.barrier()
            stA.close()

            t_xtm = [sb(f"mxt{i}", [128, D0], BF16, st) for i in range(16)]
            t_ct = [sb(f"mct{i}", [128, MTOP], BF16, st) for i in range(2)]
            t_xs = [sb(f"mxs{q}", [128, D0], BF16, st) for q in range(2)]
            t_rx = [sb(f"mrx{d}", [128, 2 * 128], BF16, st)
                    for d in range(NDC0)]
            t_we = sb("mwe", [128, NDC0 * D0], BF16, st)
            t_eo = [sb(f"meo{d}", [128, MTOP], F32, st) for d in range(NDC0)]
            t_vals = [sb(f"mvl{e}", [128, 2 * D0], BF16, st)
                      for e in range(NEXP)]
            t_D = [sb(f"mD{e}", [128, 2 * L], BF16, st) for e in range(NEXP)]
            t_pwr = sb("mpwr", [128, L], F32, st)
            t_mo = [sb(f"mmo{i}", [128, 512], F32, st) for i in range(2)]
            pgx = [ps(f"mpgx{i}", [128, 512], F32, st) for i in range(4)]
            pxT = ps("mpxT", [128, 128], BF16, st)
            pex = [ps(f"mpex{i}", [128, MTOP], F32, st) for i in range(2)]
            pvT = ps("mpvT", [128, 128], F32, st)

            for b in range(B_LOC):
                for lc in range(16):
                    p.dma("sp", t_xtm[lc][:],
                          XNTM[b, lc * 128:(lc + 1) * 128, :],
                          writes=[("xtm", lc)])
                for e in range(NEXP):
                    for kc in range(NDC0):
                        load_bf(t_we[:, kc * D0:(kc + 1) * D0],
                                exps[e, kc * 128:(kc + 1) * 128, :],
                                ("we", kc), st, stg)
                    for lc in range(16):
                        kc2 = lc % 2
                        p.op("dve", lambda E, b=b, e=e, lc=lc, kc2=kc2:
                             E.tensor_scalar(
                                 out=t_ct[kc2][:], in0=t_mio[:],
                                 scalar1=t_pfT[b][:, lc * 8 + e:
                                                  lc * 8 + e + 1],
                                 scalar2=None, op0=ALU.is_equal),
                             reads=[("pfT", b), "mmio"],
                             writes=[("ct", kc2)])
                        for mh in range(2):
                            for dh in range(2):
                                pq = mh * 2 + dh
                                p.op("pe", lambda E, lc=lc, mh=mh, dh=dh,
                                     pq=pq, kc2=kc2: E.matmul(
                                         out=pgx[pq][:],
                                         lhsT=t_ct[kc2][:, mh * 128:
                                                        (mh + 1) * 128],
                                         rhs=t_xtm[lc][:, dh * 512:
                                                       (dh + 1) * 512],
                                         start=(lc == 0), stop=(lc == 15)),
                                     reads=[("ct", kc2), ("xtm", lc)],
                                     writes=([("pgx", pq)]
                                             if lc == 15 else []),
                                     extra=(p.predeps(("pgx", pq))
                                            if lc == 0 else []))
                    for mh in range(2):
                        for dh in range(2):
                            pq = mh * 2 + dh
                            p.op("act", lambda E, mh=mh, dh=dh, pq=pq: E.copy(
                                out=t_xs[mh][:, dh * 512:(dh + 1) * 512],
                                in_=pgx[pq][:]),
                                reads=[("pgx", pq)], writes=[("xs", mh)])
                            p._record(p.wr[("xs", mh)], [], [("pgx", pq)])
                    for dc in range(NDC0):
                        for mh in range(2):
                            p.op("pe", lambda E, dc=dc, mh=mh: E.transpose(
                                out=pxT[:],
                                in_=t_xs[mh][:, dc * 128:(dc + 1) * 128],
                                identity=t_id_bf[:]),
                                reads=[("xs", mh), "idbf"], writes=["pxT"])
                            p.op("act", lambda E, dc=dc, mh=mh: E.copy(
                                out=t_rx[dc][:, mh * 128:(mh + 1) * 128],
                                in_=pxT[:]),
                                reads=["pxT"], writes=[("mrx", dc)])
                            p._record(p.wr[("mrx", dc)], [], ["pxT"])
                    for do in range(NDC0):
                        kq = do % 2
                        for kc in range(NDC0):
                            p.op("pe", lambda E, do=do, kc=kc, kq=kq:
                                 E.matmul(
                                     out=pex[kq][:],
                                     lhsT=t_we[:, kc * D0 + do * 128:
                                               kc * D0 + (do + 1) * 128],
                                     rhs=t_rx[kc][:], start=(kc == 0),
                                     stop=(kc == NDC0 - 1)),
                                 reads=[("we", kc), ("mrx", kc)],
                                 writes=([("pex", kq)]
                                         if kc == NDC0 - 1 else []),
                                 extra=(p.predeps(("pex", kq))
                                        if kc == 0 else []))
                        p.op("act", lambda E, do=do, kq=kq: E.copy(
                            out=t_eo[do][:], in_=pex[kq][:]),
                            reads=[("pex", kq)], writes=[("meo", do)])
                        p._record(p.wr[("meo", do)], [], [("pex", kq)])
                    for do in range(NDC0):
                        for mh in range(2):
                            p.op("pe", lambda E, do=do, mh=mh: E.transpose(
                                out=pvT[:],
                                in_=t_eo[do][:, mh * 128:(mh + 1) * 128],
                                identity=t_id_f[:]),
                                reads=[("meo", do), "idf"], writes=["pvT"])
                            p.op("act", lambda E, do=do, mh=mh, e=e: E.copy(
                                out=t_vals[e][:, mh * D0 + do * 128:
                                              mh * D0 + (do + 1) * 128],
                                in_=pvT[:]),
                                reads=["pvT"], writes=[("vals", e)])
                            p._record(p.wr[("vals", e)], [], ["pvT"])
                    p.dma("sp", t_pwr[:],
                          PWD[b, e:e + 1, :].broadcast_to([128, L]),
                          reads=[("PWD", b)], writes=["pwr"])
                    for mc in range(2):
                        p.op("dve", lambda E, e=e, mc=mc: E.tensor_scalar(
                            out=t_D[e][:, mc * L:(mc + 1) * L],
                            in0=t_pwr[:], scalar1=t_mioc[:, mc:mc + 1],
                            scalar2=None, op0=ALU.is_equal),
                            reads=["pwr", "mmioc"], writes=[("D", e)])
                for lc in range(16):
                    for dh in range(2):
                        pq = dh
                        for e in range(NEXP):
                            for mc in range(2):
                                first = (e == 0 and mc == 0)
                                last = (e == NEXP - 1 and mc == 1)
                                p.op("pe", lambda E, lc=lc, dh=dh, e=e,
                                     mc=mc, pq=pq, first=first, last=last:
                                     E.matmul(
                                         out=pgx[pq][:],
                                         lhsT=t_D[e][:, mc * L + lc * 128:
                                                     mc * L + (lc + 1) * 128],
                                         rhs=t_vals[e][:, mc * D0 + dh * 512:
                                                       mc * D0 + (dh + 1) * 512],
                                         start=first, stop=last),
                                     reads=[("D", e), ("vals", e)],
                                     writes=([("pgx", pq)] if last else []),
                                     extra=(p.predeps(("pgx", pq))
                                            if first else []))
                        p.op("dve", lambda E, b=b, lc=lc, dh=dh, pq=pq:
                             E.tensor_scalar(
                                 out=t_mo[dh][:], in0=pgx[pq][:],
                                 scalar1=t_gvt[b][:, lc:lc + 1],
                                 scalar2=None, op0=ALU.mult),
                             reads=[("pgx", pq), ("gvt", b)],
                             writes=[("mo", dh)])
                        p._record(p.wr[("mo", dh)], [], [("pgx", pq)])
                        p.dma("act", MOTM[b, lc * 128:(lc + 1) * 128,
                                          dh * 512:(dh + 1) * 512],
                              t_mo[dh][:], reads=[("mo", dh)],
                              writes=[("MOTM", b, lc, dh)])
            p.barrier()
    def emit_s1():
        with contextlib.ExitStack() as st:
            t_op = sb("s1op", [128, NDC1 * D0], BF16, st)
            stg = sb("s1stg", [128, 1024], F32, st)
            t_dl = [sb(f"s1dl{d}", [128, PC1], F32, st) for d in range(2)]
            t_u = [sb(f"s1u{d}", [128, PC1], F32, st) for d in range(2)]
            t_uk = [sb(f"s1uk{d}", [128, PC1], F32, st) for d in range(2)]
            t_du = [sb(f"s1du{d}", [128, PC1], F32, st) for d in range(2)]
            t_y2 = [sb(f"s1y2{d}", [128, PC1], F32, st) for d in range(2)]
            t_yb = [sb(f"s1yb{d}", [128, PC1], BF16, st)
                    for d in range(NDC1)]
            t_zs = [sb(f"s1z{i}", [128, PC1], F32, st) for i in range(2)]
            t_dAp1 = [sb(f"s1dAp{i}", [128, NST * PC1], F32, st)
                      for i in range(2)]
            t_Bup1 = [sb("s1Bup0", [128, NST * PC1], F32, st)]
            t_hp1 = [sb(f"s1hp{i}", [128, NST * PC1], F32, st)
                     for i in range(2)]
            t_hbf = sb("s1hbf", [128, NST * PC1], BF16, st)
            t_hT = sb("s1hT", [128, NST * PC1], BF16, st)
            t_dg = sb("s1dg", [128, 2 * NST * 128], BF16, st)
            t_cb = [sb(f"s1cb{i}", [128, 128], F32, st) for i in range(2)]
            t_Bp1 = sb("s1Bp", [128, NST * PC1], F32, st)
            t_sg1 = sb("s1sg", [128, NST], F32, st)
            t_at = [sb(f"s1at{i}", [128, PC1], F32, st)
                    for i in range(NDC0)]
            t_x1 = [sb("s1x10", [128, D0], F32, st)]
            t_o = [sb("s1o0", [128, D0], F32, st)]
            pop = [ps(f"s1pop{i}", [128, PC1], F32, st) for i in range(2)]
            ptm = [ps("s1ptm0", [128, D0], F32, st)]
            phT = [ps(f"s1phT{i}", [128, 1024], BF16, st) for i in range(2)]
            pyf = [ps(f"s1pyf{i}", [128, PC1], F32, st) for i in range(2)]

            for kc in range(NDC1):
                load_bf(t_op[:, kc * D0:(kc + 1) * D0],
                        opT[kc * 128:(kc + 1) * 128, :], ("op", kc), st, stg)
            p.op("dve", lambda E: E.memset(t_st1[:], 0.0), writes=["st1"])
            for b in range(B_LOC):
                if b > 0:
                    p.op("dve", lambda E: E.memset(t_st1[:], 0.0),
                         writes=["st1"])
                for pc in range(L // PC1):
                    c0 = b * L + pc * PC1
                    lc = pc * PC1
                    p.dma("sp", t_Bp1[:],
                          BCR[1, b, 0:16, lc:lc + PC1].rearrange(
                              "(one n) l -> one n l", one=1).broadcast_to(
                              [128, NST, PC1]),
                          writes=["Bp1"])
                    for n in range(NST):
                        for lb in range(PC1 // 128):
                            j = n * 2 + lb
                            kc2 = j % 2
                            p.dma("sp", t_cb[kc2][:],
                                  BCR[1, b, 16 + n:17 + n,
                                      lc + lb * 128:lc + (lb + 1) * 128]
                                  .broadcast_to([128, 128]),
                                  writes=[("cb", kc2)])
                            p.op("dve", lambda E, j=j, kc2=kc2:
                                 E.tensor_mul(
                                     out=t_dg[:, j * 128:(j + 1) * 128],
                                     in0=t_id_f[:], in1=t_cb[kc2][:]),
                                 reads=[("cb", kc2), "idf"],
                                 writes=[("dg", j)])
                    WP1 = NST * PC1

                    def s1_A(dc):
                        kq = dc % 2
                        kh = dc % 2
                        p.dma("sp", t_dl[kq][:],
                              D1[dc * 128:(dc + 1) * 128, c0:c0 + PC1],
                              writes=[("dl", kq)])
                        p.dma("sp", t_u[kq][:],
                              U1[dc * 128:(dc + 1) * 128, c0:c0 + PC1],
                              writes=[("u", kq)])
                        p.op("dve", lambda E, kq=kq: E.tensor_mul(
                            out=t_du[kq][:], in0=t_dl[kq][:],
                            in1=t_u[kq][:]),
                            reads=[("dl", kq), ("u", kq)],
                            writes=[("du", kq)])
                        for n in range(NST):
                            p.op("act", lambda E, dc=dc, n=n, kq=kq:
                                 E.activation(
                                     out=t_dAp1[kq][:, n * PC1:(n + 1) * PC1],
                                     in_=t_dl[kq][:], func=AF.Exp,
                                     scale=t_A1[:, dc * NST + n:
                                                dc * NST + n + 1]),
                                 reads=[("dl", kq), "A1"],
                                 writes=[("dAp1", kq, n)])
                        dA_all = [("dAp1", kq, n) for n in range(NST)]
                        p.op("dve", lambda E, kq=kq: E.tensor_tensor(
                            out=t_Bup1[0][:].rearrange(
                                "p (n l) -> p n l", n=NST),
                            in0=t_du[kq][:].rearrange(
                                "p (one l) -> p one l", one=1).broadcast_to(
                                [128, NST, PC1]),
                            in1=t_Bp1[:].rearrange("p (n l) -> p n l", n=NST),
                            op=ALU.mult),
                            reads=[("du", kq), "Bp1"], writes=[("Bup1", 0)])
                        p.op("dve", lambda E, dc=dc, kq=kq: E.tensor_tensor(
                            out=t_sg1[:], in0=t_dAp1[kq][:, 0:WP1:PC1],
                            in1=t_st1[:, dc * NST:(dc + 1) * NST],
                            op=ALU.mult),
                            reads=dA_all + ["st1"], writes=["sg1"])
                        p.op("dve", lambda E: E.tensor_tensor(
                            out=t_Bup1[0][:, 0:WP1:PC1],
                            in0=t_Bup1[0][:, 0:WP1:PC1], in1=t_sg1[:],
                            op=ALU.add),
                            reads=["sg1", ("Bup1", 0)],
                            writes=[("Bup1", 0)])
                        p.op("dve", lambda E, kq=kq: E.memset(
                            t_dAp1[kq][:, 0:WP1:PC1], 0.0),
                            reads=[], writes=dA_all)
                        p.op("dve", lambda E, kq=kq, kh=kh:
                             E.tensor_tensor_scan(
                                 out=t_hp1[kh][:], data0=t_dAp1[kq][:],
                                 data1=t_Bup1[0][:], initial=0.0,
                                 op0=ALU.mult, op1=ALU.add),
                             reads=dA_all + [("Bup1", 0)],
                             writes=[("hp1", kh)])
                        p.op("dve", lambda E, dc=dc, kh=kh: E.tensor_copy(
                            out=t_st1[:, dc * NST:(dc + 1) * NST],
                            in_=t_hp1[kh][:, PC1 - 1:WP1:PC1]),
                            reads=[("hp1", kh)], writes=["st1"])

                    def s1_B(dc):
                        kh = dc % 2
                        kz = dc % 2
                        kp = dc % 2
                        p.op("pool", lambda E, kh=kh: E.tensor_copy(
                            out=t_hbf[:], in_=t_hp1[kh][:]),
                            reads=[("hp1", kh)], writes=["hbf"])
                        for g in range(4):
                            kt = g % 2
                            for jj in range(8):
                                j = g * 8 + jj
                                n, lb = j // 2, j % 2
                                p.op("pe", lambda E, n=n, lb=lb, jj=jj,
                                     kt=kt: E.transpose(
                                         out=phT[kt][:, jj * 128:
                                                     (jj + 1) * 128],
                                         in_=t_hbf[:, n * PC1 + lb * 128:
                                                   n * PC1 + (lb + 1) * 128],
                                         identity=t_id_bf[:]),
                                     reads=["hbf", "idbf"],
                                     writes=[("phT", kt, jj)])
                            p.op("act", lambda E, g=g, kt=kt: E.copy(
                                out=t_hT[:, g * 1024:(g + 1) * 1024],
                                in_=phT[kt][:]),
                                reads=[("phT", kt, jj) for jj in range(8)],
                                writes=[("hT", g)])
                            p._record(p.wr[("hT", g)], [],
                                      [("phT", kt, jj) for jj in range(8)])
                        for lb in range(PC1 // 128):
                            for n in range(NST):
                                j = n * 2 + lb
                                p.op("pe", lambda E, j=j, lb=lb, kp=kp,
                                     n=n: E.matmul(
                                         out=pyf[kp][:, lb * 128:
                                                     (lb + 1) * 128],
                                         lhsT=t_hT[:, j * 128:(j + 1) * 128],
                                         rhs=t_dg[:, j * 128:(j + 1) * 128],
                                         start=(n == 0), stop=(n == NST - 1)),
                                     reads=[("hT", j // 8), ("dg", j)],
                                     writes=([("pyf", kp, lb)]
                                             if n == NST - 1 else []),
                                     extra=(p.predeps(("pyf", kp, lb))
                                            if n == 0 else []))
                        p.dma("sp", t_uk[kz][:],
                              U1[dc * 128:(dc + 1) * 128, c0:c0 + PC1],
                              writes=[("uk", kz)])
                        p.op("dve", lambda E, dc=dc, kz=kz, kp=kp:
                             E.scalar_tensor_tensor(
                                 out=t_y2[kz][:], in0=t_uk[kz][:],
                                 scalar=t_dd1[:, dc:dc + 1],
                                 in1=pyf[kp][:],
                                 op0=ALU.mult, op1=ALU.add),
                             reads=[("uk", kz), "dd1",
                                    ("pyf", kp, 0), ("pyf", kp, 1)],
                             writes=[("y2", kz)])
                        p._record(p.wr[("y2", kz)], [],
                                  [("pyf", kp, 0), ("pyf", kp, 1)])
                        p.dma("sp", t_zs[kz][:],
                              ZS[dc * 128:(dc + 1) * 128, c0:c0 + PC1],
                              writes=[("zs", kz)])
                        p.op("dve", lambda E, dc=dc, kz=kz: E.tensor_mul(
                            out=t_yb[dc][:], in0=t_y2[kz][:],
                            in1=t_zs[kz][:]),
                            reads=[("y2", kz), ("zs", kz)],
                            writes=[("yb", dc)])

                    for dc in range(NDC1):
                        s1_A(dc)
                        if dc >= 3:
                            s1_B(dc - 3)
                    for dc in range(NDC1 - 3, NDC1):
                        s1_B(dc)

                    for do in range(NDC0):
                        kq = do % 2
                        for kc in range(NDC1):
                            p.op("pe", lambda E, do=do, kc=kc, kq=kq:
                                 E.matmul(
                                     out=pop[kq][:],
                                     lhsT=t_op[:, kc * D0 + do * 128:
                                               kc * D0 + (do + 1) * 128],
                                     rhs=t_yb[kc][:], start=(kc == 0),
                                     stop=(kc == NDC1 - 1)),
                                 reads=[("op", kc), ("yb", kc)],
                                 writes=([("pop", kq)]
                                         if kc == NDC1 - 1 else []),
                                 extra=(p.predeps(("pop", kq))
                                        if kc == 0 else []))
                        p.op("act", lambda E, do=do, kq=kq: E.copy(
                            out=t_at[do][:], in_=pop[kq][:]),
                            reads=[("pop", kq)], writes=[("at", do)])
                        p._record(p.wr[("at", do)], [], [("pop", kq)])
                    for tb in range(PC1 // 128):
                        kq = tb % 2
                        for do in range(NDC0):
                            p.op("pe", lambda E, do=do, tb=tb:
                                 E.transpose(
                                     out=ptm[0][:, do * 128:(do + 1) * 128],
                                     in_=t_at[do][:, tb * 128:(tb + 1) * 128],
                                     identity=t_id_f[:]),
                                 reads=[("at", do), "idf"],
                                 writes=[("ptm", do)])
                        r0 = pc * PC1 + tb * 128
                        p.dma("sp", t_x1[0][:], X1TM[b, r0:r0 + 128, :],
                              writes=[("x1", 0)])
                        p.dma("sp", t_o[0][:], MOTM[b, r0:r0 + 128, :],
                              writes=[("otm", 0)])
                        p.op("dve", lambda E: E.tensor_add(
                            out=t_o[0][:], in0=ptm[0][:], in1=t_o[0][:]),
                            reads=[("otm", 0)] + [("ptm", d)
                                                  for d in range(NDC0)],
                            writes=[("otm", 0)])
                        p._record(p.wr[("otm", 0)], [],
                                  [("ptm", d) for d in range(NDC0)])
                        p.op("dve", lambda E: E.tensor_add(
                            out=t_o[0][:], in0=t_o[0][:], in1=t_x1[0][:]),
                            reads=[("x1", 0), ("otm", 0)],
                            writes=[("otm", 0)])
                        p.dma("sp", out[b, r0:r0 + 128, :], t_o[0][:],
                              reads=[("otm", 0)], writes=[("out", b, r0)])
            p.barrier()

    if stage != "prep":
        emit_ln(0)
    if stage in ("ffn0", "l0", "x1", "i1", "moe", "s1", "full"):
        emit_ffn0()
    if stage in ("l0", "x1", "i1", "moe", "s1", "full"):
        emit_l0()
    if stage in ("i1", "moe", "s1", "full"):
        emit_ln(1)
        emit_i1()
    if stage in ("moe", "s1", "full"):
        emit_moe()
    if stage in ("s1", "full"):
        emit_s1()

    if stage not in ("s1", "full"):
        zo = sb("zout", [128, D0])
        p.op("dve", lambda E: E.memset(zo[:], 0.0), writes=["zout"])
        for b in range(B_LOC):
            for i in range(L // 128):
                p.dma("sp", out[b, i * 128:(i + 1) * 128, :], zo[:],
                      reads=["zout"], writes=[("outz", b, i)])

    if dbg is not None:
        src = dbg_src[debug_out]
        p.barrier()
        flat = int(np.prod(src.shape))
        sf = src.rearrange(
            " ".join(f"a{i}" for i in range(len(src.shape))) + " -> ("
            + " ".join(f"a{i}" for i in range(len(src.shape))) + ")")
        df = dbg.rearrange(
            " ".join(f"a{i}" for i in range(len(src.shape))) + " -> ("
            + " ".join(f"a{i}" for i in range(len(src.shape))) + ")")
        tcp = sb("dbgcp", [128, 4096], src.dtype)
        CH = 128 * 4096
        off = 0
        i = 0
        while off < flat:
            n = min(CH, flat - off)
            rows = n // 4096
            assert rows * 4096 == n, f"debug copy misalign {flat}"
            p.dma("sp", tcp[0:rows, :],
                  sf[off:off + n].rearrange("(p f) -> p f", f=4096),
                  writes=["dbgcp"])
            p.dma("act", df[off:off + n].rearrange("(p f) -> p f", f=4096),
                  tcp[0:rows, :], reads=["dbgcp"], writes=[("dbg", i)])
            off += n
            i += 1

    p.barrier()
    with nc.Block() as block:
        p.emit(block)
    top.close()
    return nc


def _host_inputs(w):
    tr = lambda a: np.ascontiguousarray(np.transpose(np.asarray(a)))
    cols = lambda v, ndc: np.ascontiguousarray(
        np.asarray(v).reshape(ndc, 128).T)
    d = {}
    d["cw0"] = np.ascontiguousarray(
        np.asarray(w["L0_attn_conv_w"])[:, 0, :].reshape(NDC0, 128, 4)
        .transpose(1, 0, 2).reshape(128, NDC0 * 4))
    d["cb0"] = cols(w["L0_attn_conv_b"], NDC0)
    d["xp0T"] = np.ascontiguousarray(tr(w["L0_attn_x_proj_w"]))
    d["dt0T"] = tr(w["L0_attn_dt_proj_w"])
    d["db0"] = cols(w["L0_attn_dt_proj_b"], NDC0)
    d["al0"] = np.asarray(w["L0_attn_A_log"])
    d["dsk0"] = cols(w["L0_attn_D_skip"], NDC0)
    d["wgT"] = tr(w["L0_ffn_w_gate"])
    d["wuT"] = tr(w["L0_ffn_w_up"])
    d["wdT"] = tr(w["L0_ffn_w_down"])
    d["ipT"] = tr(w["L1_attn_in_proj_w"])
    d["cw1"] = np.ascontiguousarray(
        np.asarray(w["L1_attn_conv_w"])[:, 0, :].reshape(NDC1, 128, 4)
        .transpose(1, 0, 2).reshape(128, NDC1 * 4))
    d["cb1"] = cols(w["L1_attn_conv_b"], NDC1)
    d["xp1T"] = np.ascontiguousarray(tr(w["L1_attn_x_proj_w"]))
    d["dt1T"] = tr(w["L1_attn_dt_proj_w"])
    d["db1"] = cols(w["L1_attn_dt_proj_b"], NDC1)
    d["al1"] = np.asarray(w["L1_attn_A_log"])
    d["dd1"] = cols(w["L1_attn_D"], NDC1)
    d["opT"] = tr(w["L1_attn_out_proj_w"])
    d["gw"] = np.asarray(w["L1_ffn_gate_w"])
    d["exps"] = np.asarray(w["L1_ffn_experts"])
    import ml_dtypes
    d["id_bf"] = np.eye(128, dtype=ml_dtypes.bfloat16)
    d["id_f32"] = np.eye(128, dtype=np.float32)
    d["lt8"] = np.tril(np.ones((NEXP, NEXP), np.float32), -1).astype(
        ml_dtypes.bfloat16)
    d["miota"] = np.arange(MTOP, dtype=np.float32).reshape(1, MTOP)
    d["miocol"] = np.arange(MTOP, dtype=np.float32).reshape(
        2, 128).T.copy()
    return d


def kernel(**inputs):
    x = np.asarray(inputs["L0_attn_u"])
    base = _host_inputs(inputs)
    nc = build_program("full")
    in_maps = []
    for c in range(8):
        m = dict(base)
        m["x_in"] = np.ascontiguousarray(x[2 * c:2 * c + 2])
        in_maps.append(m)
    res = run_bass_kernel_spmd(nc, in_maps, core_ids=list(range(8)))
    return np.concatenate([r["out"] for r in res.results], axis=0)


# version 15 (final)
# speedup vs baseline: 1.0450x; 1.0450x over previous
import contextlib

import numpy as np

import concourse.bass as bass
import concourse.mybir as mybir
from concourse.bass import IndirectOffsetOnAxis
from concourse.bass_utils import run_bass_kernel_spmd

F32 = mybir.dt.float32
BF16 = mybir.dt.bfloat16
I32 = mybir.dt.int32
AF = mybir.ActivationFunctionType
ALU = mybir.AluOpType

LN_TP = True
L0_DEBUG = False
B_LOC = 2
L = 2048
T = B_LOC * L
D0 = 1024
DI = 2048
NST = 16
RNK = 64
HID = 3072
NEXP = 8
MTOP = 256
PC0 = 256
PCF = 512
PC1 = 256
NDC0 = D0 // 128
NDC1 = DI // 128
INV127 = float(np.float32(1.0) / np.float32(127.0))

ENGS = ("sp", "pe", "act", "dve", "pool")


class Prog:

    DMA_POOL = {"sp": 14, "act": 8, "pool": 8}

    def __init__(self, nc):
        self.nc = nc
        self.q = {e: [] for e in ENGS}
        self.cnt = dict.fromkeys(ENGS, 0)
        self.dcnt = {}
        self.rr = dict.fromkeys(ENGS, 0)
        self.wm = {}
        self.sems = {}
        self.wr = {}
        self.rd = {}
        self.trace_log = []

    def make_sems(self, stack):
        for e in ENGS:
            self.sems["q" + e] = stack.enter_context(self.nc.semaphore("q" + e))
            for i in range(self.DMA_POOL.get(e, 0)):
                self.sems[f"d{e}{i}"] = stack.enter_context(
                    self.nc.semaphore(f"d{e}{i}"))
                self.dcnt[(e, i)] = 0

    def _wait(self, eng, sem_name, val):
        if val <= 0:
            return
        key = (eng, sem_name)
        if self.wm.get(key, 0) >= val:
            return
        self.wm[key] = val
        sem = self.sems[sem_name]
        self.q[eng].append(lambda E, s=sem, v=val: E.wait_ge(s, v))

    def _deps(self, eng, toks):
        best = {}
        for t in toks:
            if t is None:
                continue
            s, v = t
            if best.get(s, 0) >= v:
                continue
            best[s] = v
        for s, v in best.items():
            self._wait(eng, s, v)

    def _collect(self, reads, writes):
        toks = []
        for k in reads:
            t = self.wr.get(k)
            if t is not None:
                toks.append(t)
        for k in writes:
            t = self.wr.get(k)
            if t is not None:
                toks.append(t)
            toks.extend(self.rd.get(k, ()))
        return toks

    def _record(self, tok, reads, writes):
        for k in reads:
            self.rd.setdefault(k, []).append(tok)
        for k in writes:
            self.wr[k] = tok
            self.rd[k] = []

    def predeps(self, key):
        t = [self.wr[key]] if key in self.wr else []
        return t + list(self.rd.get(key, ()))

    def op(self, eng, fn, reads=(), writes=(), extra=(), tag=None):
        if tag is None:
            tag = getattr(fn, "__qualname__", "?")
        self.trace_log.append((eng, tag))
        self._deps(eng, self._collect(reads, writes) + list(extra))
        self.cnt[eng] += 1
        sem = self.sems["q" + eng]

        def run(E, fn=fn, sem=sem):
            fn(E).then_inc(sem, 1)
        self.q[eng].append(run)
        tok = ("q" + eng, self.cnt[eng])
        self._record(tok, reads, writes)
        return tok

    def _dma_slot(self, eng):
        i = self.rr[eng]
        self.rr[eng] = (i + 1) % self.DMA_POOL[eng]
        sem_name = f"d{eng}{i}"
        prev = self.dcnt[(eng, i)]
        self._wait(eng, sem_name, prev)
        self.dcnt[(eng, i)] = prev + 16
        return sem_name, prev + 16

    def dma(self, eng, out, in_, reads=(), writes=(), extra=(), **kw):
        self._deps(eng, self._collect(reads, writes) + list(extra))
        sem_name, val = self._dma_slot(eng)
        sem = self.sems[sem_name]

        def run(E, out=out, in_=in_, sem=sem, kw=kw):
            E.dma_start(out=out, in_=in_, **kw).then_inc(sem, 16)
        self.q[eng].append(run)
        tok = (sem_name, val)
        self._record(tok, reads, writes)
        return tok

    def idma(self, out, out_off, in_, in_off, reads=(), writes=(), extra=(),
             bounds_check=None):
        eng = "pool"
        self._deps(eng, self._collect(reads, writes) + list(extra))
        sem_name, val = self._dma_slot(eng)
        sem = self.sems[sem_name]

        def run(E, sem=sem):
            E.indirect_dma_start(out=out, out_offset=out_off, in_=in_,
                                 in_offset=in_off,
                                 bounds_check=bounds_check).then_inc(sem, 16)
        self.q[eng].append(run)
        tok = (sem_name, val)
        self._record(tok, reads, writes)
        return tok

    def barrier(self):
        toks = []
        for e in ENGS:
            toks.append(("q" + e, self.cnt[e]))
            for i in range(self.DMA_POOL.get(e, 0)):
                toks.append((f"d{e}{i}", self.dcnt[(e, i)]))
        for e in ENGS:
            self._deps(e, toks)

    def emit(self, block):
        secs = {"sp": block.sync, "pe": block.tensor, "act": block.scalar,
                "dve": block.vector, "pool": block.gpsimd}
        for e in ENGS:
            ops = self.q[e]

            def body(E, ops=ops):
                for f in ops:
                    f(E)
            secs[e](body)


def build_program(stage="full", debug_out=None):
    nc = bass.Bass("TRN2", target_bir_lowering=False, debug=False,
                   detect_race_conditions=False)
    top = contextlib.ExitStack()

    def din(name, shape, dtype=F32):
        return nc.dram_tensor(name, shape, dtype, kind="ExternalInput").ap()

    x_in = din("x_in", [B_LOC, L, D0])
    cw0 = din("cw0", [128, 4 * NDC0]); cb0 = din("cb0", [128, NDC0])
    xp0T = din("xp0T", [D0, 96])
    dt0T = din("dt0T", [RNK, D0]); db0 = din("db0", [128, NDC0])
    al0 = din("al0", [D0, NST]); dsk0 = din("dsk0", [128, NDC0])
    wgT = din("wgT", [D0, HID], I32)
    wuT = din("wuT", [D0, HID], I32)
    wdT = din("wdT", [HID, D0], I32)
    ipT = din("ipT", [D0, 2 * DI])
    cw1 = din("cw1", [128, 4 * NDC1]); cb1 = din("cb1", [128, NDC1])
    xp1T = din("xp1T", [DI, 96])
    dt1T = din("dt1T", [RNK, DI]); db1 = din("db1", [128, NDC1])
    al1 = din("al1", [DI, NST]); dd1 = din("dd1", [128, NDC1])
    opT = din("opT", [DI, D0])
    gw = din("gw", [D0, NEXP])
    exps = din("exps", [NEXP, D0, D0])
    id_bf = din("id_bf", [128, 128], BF16)
    id_f32 = din("id_f32", [128, 128])
    lt8 = din("lt8", [NEXP, NEXP], BF16)
    miota = din("miota", [1, MTOP])
    miocol = din("miocol", [128, 2])

    out = nc.dram_tensor("out", [B_LOC, L, D0], F32, kind="ExternalOutput").ap()

    def scr(name, shape, dtype=F32):
        return nc.dram_tensor(name, shape, dtype, kind="Internal").ap()

    XN32 = scr("XN32", [D0, T])
    XNBF = scr("XNBF", [D0, T], BF16)
    XNTM = scr("XNTM", [B_LOC, L, D0], BF16)
    X1TM = scr("X1TM", [B_LOC, L, D0])
    FFN0 = scr("FFN0", [D0, T])
    BCR = scr("BCR", [2, B_LOC, 32, L])
    WGBF = scr("WGBF", [D0, HID], BF16)
    WUBF = scr("WUBF", [D0, HID], BF16)
    U1 = scr("U1", [DI, T])
    D1 = scr("D1", [DI, T])
    ZS = scr("ZS", [DI, T])
    MOTM = scr("MOTM", [B_LOC, L, D0])
    GVD = scr("GVD", [1, L])
    OPBF = scr("OPBF", [8 * 128, D0], BF16)
    PWD = scr("PWD", [B_LOC, NEXP, L])

    dbg_src = {"XN32": XN32, "XNBF": XNBF, "X1TM": X1TM, "FFN0": FFN0,
               "U1": U1, "D1": D1, "ZS": ZS, "MOTM": MOTM, "XNTM": XNTM,
               "BCR": BCR, "PWD": PWD}
    dbg = None
    if debug_out is not None:
        src = dbg_src[debug_out]
        dbg = nc.dram_tensor("dbg", list(src.shape), src.dtype,
                             kind="ExternalOutput").ap()

    p = Prog(nc)
    globals()["LAST_PROG"] = p
    p.make_sems(top)

    def sb(name, shape, dtype=F32, stack=top):
        return stack.enter_context(nc.sbuf_tensor(name, shape, dtype))

    def ps(name, shape, dtype=F32, stack=top):
        return stack.enter_context(nc.psum_tensor(name, shape, dtype))

    t_id_bf = sb("t_id_bf", [128, 128], BF16)
    t_id_f = sb("t_id_f", [128, 128])
    t_A0 = sb("t_A0", [128, NST * NDC0])
    t_iA0 = sb("t_iA0", [128, NST * NDC0])
    t_niA0 = sb("t_niA0", [128, NST * NDC0])
    t_A1 = sb("t_A1", [128, NST * NDC1])
    t_cw0 = sb("t_cw0", [128, 4 * NDC0])
    t_cw1 = sb("t_cw1", [128, 4 * NDC1])
    t_cb0 = sb("t_cb0", [128, NDC0])
    t_cb1 = sb("t_cb1", [128, NDC1])
    t_db0 = sb("t_db0", [128, NDC0])
    t_db1 = sb("t_db1", [128, NDC1])
    t_dsk = sb("t_dsk", [128, NDC0])
    t_dd1 = sb("t_dd1", [128, NDC1])
    t_st0 = sb("t_st0", [128, NST * NDC0])
    t_st1 = sb("t_st1", [128, NST * NDC1])

    for i, v in enumerate([1e-5, -0.5, -1.0, 1.0, INV127]):
        ct = sb(f"cst{i}", [128, 1])
        p.op("pool", lambda E, ct=ct, v=v: E.memset(ct[:], v),
             writes=[("cst", v)])
        nc.const_aps.register(F32, v, ct[:, 0:1])

    p.dma("sp", t_id_bf[:], id_bf, writes=["idbf"])
    p.dma("sp", t_id_f[:], id_f32, writes=["idf"])
    for tile, src, key in ((t_cw0, cw0, "cw0"), (t_cw1, cw1, "cw1"),
                           (t_cb0, cb0, "cb0"), (t_cb1, cb1, "cb1"),
                           (t_db0, db0, "db0"), (t_db1, db1, "db1"),
                           (t_dsk, dsk0, "dsk"), (t_dd1, dd1, "dd1")):
        p.dma("sp", tile[:, :], src, writes=[key])

    def prep_A(tA, alog, ndc, key, neg_exp_of_neg=False):
        for dc in range(ndc):
            sl = tA[:, dc * NST:(dc + 1) * NST]
            p.dma("sp", sl, alog[dc * 128:(dc + 1) * 128, :],
                  writes=[(key, dc)])
            p.op("act", lambda E, sl=sl: E.activation(
                out=sl, in_=sl, func=AF.Exp,
                scale=(-1.0 if neg_exp_of_neg else 1.0)),
                reads=[(key, dc)], writes=[(key, dc)])
            p.op("dve", lambda E, sl=sl: E.tensor_scalar_mul(
                out=sl, in0=sl, scalar1=-1.0),
                reads=[(key, dc)], writes=[(key, dc)])
    prep_A(t_A0, al0, NDC0, "A0")
    prep_A(t_iA0, al0, NDC0, "iA0", neg_exp_of_neg=True)
    for dc in range(NDC0):
        sl = t_niA0[:, dc * NST:(dc + 1) * NST]
        p.dma("sp", sl, al0[dc * 128:(dc + 1) * 128, :],
              writes=[("niA0", dc)])
        p.op("act", lambda E, sl=sl: E.activation(
            out=sl, in_=sl, func=AF.Exp, scale=-1.0),
            reads=[("niA0", dc)], writes=[("niA0", dc), "niA0"])
    prep_A(t_A1, al1, NDC1, "A1")

    with contextlib.ExitStack() as st:
        c_i = sb("cvt_i", [128, HID], I32, st)
        c_o = sb("cvt_o", [128, HID], BF16, st)
        for srcT, dstT, key in ((wgT, WGBF, "wg"), (wuT, WUBF, "wu")):
            for dc in range(NDC0):
                p.dma("sp", c_i[:], srcT[dc * 128:(dc + 1) * 128, :],
                      writes=["cvt_i"])
                p.op("dve", lambda E: E.tensor_copy(out=c_o[:], in_=c_i[:]),
                     reads=["cvt_i"], writes=["cvt_o"])
                p.dma("sp", dstT[dc * 128:(dc + 1) * 128, :], c_o[:],
                      reads=["cvt_o"], writes=[(key, dc)])
        p.barrier()

    def load_bf(dst_sl, src_ap, key, st, stage_tile, rows=128):
        skey = ("stg", id(stage_tile))
        p.dma("sp", stage_tile[0:rows, :src_ap.shape[-1]], src_ap,
              writes=[skey])
        p.op("dve", lambda E: E.tensor_copy(
            out=dst_sl, in_=stage_tile[0:rows, :src_ap.shape[-1]]),
            reads=[skey], writes=[key])

    def emit_ln(layer):
        src_tm = x_in if layer == 0 else X1TM
        with contextlib.ExitStack() as st:
            t_x = [sb(f"ln{layer}x{i}", [128, D0], F32, st) for i in range(3)]
            t_stt = [sb(f"ln{layer}st{i}", [128, 12], F32, st) for i in range(3)]
            t_mv = [sb(f"ln{layer}mv{i}", [128, 2], F32, st) for i in range(3)]
            t_rs = [sb(f"ln{layer}rs{i}", [128, 1], F32, st) for i in range(3)]
            t_nm = [sb(f"ln{layer}nm{i}", [128, 1], F32, st) for i in range(3)]
            t_xn = [sb(f"ln{layer}xn{i}", [128, D0], F32, st) for i in range(3)]
            t_xnb = [sb(f"ln{layer}xb{i}", [128, D0], BF16, st)
                     for i in range(3)]
            pp = [ps(f"ln{layer}ps{q}", [128, 128], F32, st)
                  for q in range(3)] if layer == 0 else []
            ppb = [ps(f"ln{layer}pb{q}", [128, 128], BF16, st)
                   for q in range(3)]
            t_evb = [sb(f"ln{layer}eb{q}", [128, 128], BF16, st)
                     for q in range(3)]
            t_evf = [sb(f"ln{layer}ef{q}", [128, 128], F32, st)
                     for q in range(3)]
            for tt in range(T // 128):
                k = tt % 3
                b, l0 = divmod(tt * 128, L)
                p.dma("sp", t_x[k][:], src_tm[b, l0:l0 + 128, :],
                      writes=[("lx", k)])
                p.op("dve", lambda E, k=k: E.bn_stats(
                    t_stt[k][:, 0:6], t_x[k][:, 0:512]),
                    reads=[("lx", k)], writes=[("lst", k)])
                p.op("dve", lambda E, k=k: E.bn_stats(
                    t_stt[k][:, 6:12], t_x[k][:, 512:1024]),
                    reads=[("lx", k)], writes=[("lst2", k)])
                p.op("dve", lambda E, k=k: E.bn_aggr(
                    t_mv[k][:], t_stt[k][:]),
                    reads=[("lst", k), ("lst2", k)], writes=[("lmv", k)])
                p.op("act", lambda E, k=k: E.activation(
                    out=t_rs[k][:], in_=t_mv[k][:, 1:2], func=AF.Ln,
                    bias=1e-5), reads=[("lmv", k)], writes=[("lrs", k)])
                p.op("act", lambda E, k=k: E.activation(
                    out=t_rs[k][:], in_=t_rs[k][:], func=AF.Exp, scale=-0.5),
                    reads=[("lrs", k)], writes=[("lrs", k)])
                p.op("dve", lambda E, k=k: E.tensor_scalar_mul(
                    out=t_nm[k][:], in0=t_mv[k][:, 0:1], scalar1=-1.0),
                    reads=[("lmv", k)], writes=[("lnm", k)])
                p.op("dve", lambda E, k=k: E.tensor_scalar(
                    out=t_xn[k][:], in0=t_x[k][:], scalar1=t_nm[k][:, 0:1],
                    scalar2=t_rs[k][:, 0:1], op0=ALU.add, op1=ALU.mult),
                    reads=[("lx", k), ("lnm", k), ("lrs", k)],
                    writes=[("lxn", k)])
                p.op("act", lambda E, k=k: E.copy(
                    out=t_xnb[k][:], in_=t_xn[k][:]),
                    reads=[("lxn", k)], writes=[("lxb", k)])
                if layer == 1:
                    p.dma("sp", XNTM[b, l0:l0 + 128, :], t_xnb[k][:],
                          reads=[("lxb", k)], writes=[("XNTM", tt)])
                for dc in range(NDC0 if LN_TP else 0):
                    q = dc % 3
                    p.op("pe", lambda E, k=k, dc=dc, q=q: E.transpose(
                        out=ppb[q][:],
                        in_=t_xnb[k][:, dc * 128:(dc + 1) * 128],
                        identity=t_id_bf[:]),
                        reads=[("lxb", k), "idbf"], writes=[("tpb", q)])
                    p.op("act", lambda E, q=q: E.copy(
                        out=t_evb[q][:], in_=ppb[q][:]),
                        reads=[("tpb", q)], writes=[("evb", q)])
                    p.dma("act", XNBF[dc * 128:(dc + 1) * 128,
                                      tt * 128:(tt + 1) * 128],
                          t_evb[q][:],
                          reads=[("evb", q)], writes=[("XNBF", tt, dc)])
                    if layer == 0:
                        p.op("pe", lambda E, k=k, dc=dc, q=q: E.transpose(
                            out=pp[q][:],
                            in_=t_xn[k][:, dc * 128:(dc + 1) * 128],
                            identity=t_id_f[:]),
                            reads=[("lxn", k), "idf"], writes=[("tpf", q)])
                        p.op("dve", lambda E, q=q: E.tensor_copy(
                            out=t_evf[q][:],
                            in_=pp[q][:]),
                            reads=[("tpf", q)], writes=[("evf", q)])
                        p.dma("pool", XN32[dc * 128:(dc + 1) * 128,
                                           tt * 128:(tt + 1) * 128],
                              t_evf[q][:],
                              reads=[("evf", q)], writes=[("XN32", tt, dc)])
            p.barrier()

    def emit_ffn0():
        with contextlib.ExitStack() as st:
            t_wd = sb("f0wd", [128, 24 * 1024], BF16, st)
            stg_i = sb("f0stgi", [128, 1024], I32, st)
            t_xnb = [sb(f"f0xn{d}", [128, PCF], BF16, st) for d in range(NDC0)]
            t_wg = [sb(f"f0wg{i}", [128, 1024], BF16, st) for i in range(2)]
            t_wu = [sb(f"f0wu{i}", [128, 1024], BF16, st) for i in range(2)]
            t_sg = [sb(f"f0sg{i}", [128, PCF], F32, st) for i in range(2)]
            t_h = sb("f0h", [128, 24 * PCF], BF16, st)
            t_fo = [sb(f"f0o{i}", [128, PCF], F32, st) for i in range(2)]
            pg = [ps(f"f0pg{i}", [128, PCF], F32, st) for i in range(2)]
            pu = [ps(f"f0pu{i}", [128, PCF], F32, st) for i in range(2)]
            pd = [ps(f"f0pd{i}", [128, PCF], F32, st) for i in range(4)]
            for hc in range(24):
                p.dma("sp", stg_i[:], wdT[hc * 128:(hc + 1) * 128, :],
                      writes=["f0stg"])
                p.op("dve", lambda E, hc=hc: E.tensor_copy(
                    out=t_wd[:, hc * 1024:(hc + 1) * 1024], in_=stg_i[:]),
                    reads=["f0stg"], writes=[("wd", hc)])
            for pc in range(T // PCF):
                c0 = pc * PCF
                for dc in range(NDC0):
                    p.dma("sp", t_xnb[dc][:],
                          XNBF[dc * 128:(dc + 1) * 128, c0:c0 + PCF],
                          writes=[("f0x", dc)])
                for hc in range(24):
                    kb = hc % 2
                    p.dma("sp", t_wg[kb][:],
                          WGBF[:, hc * 128:(hc + 1) * 128].rearrange(
                              "(c p) h -> p c h", p=128),
                          writes=[("f0wg", kb)])
                    p.dma("sp", t_wu[kb][:],
                          WUBF[:, hc * 128:(hc + 1) * 128].rearrange(
                              "(c p) h -> p c h", p=128),
                          writes=[("f0wu", kb)])
                    for dc in range(NDC0):
                        p.op("pe", lambda E, kb=kb, dc=dc: E.matmul(
                            out=pg[kb][:], lhsT=t_wg[kb][:, dc * 128:(dc + 1) * 128],
                            rhs=t_xnb[dc][:], start=(dc == 0), stop=(dc == 7)),
                            reads=[("f0wg", kb), ("f0x", dc)],
                            writes=([("psg", kb)] if dc == 7 else []),
                            extra=p.predeps(("psg", kb)) if dc == 0 else [])
                        p.op("pe", lambda E, kb=kb, dc=dc: E.matmul(
                            out=pu[kb][:], lhsT=t_wu[kb][:, dc * 128:(dc + 1) * 128],
                            rhs=t_xnb[dc][:], start=(dc == 0), stop=(dc == 7)),
                            reads=[("f0wu", kb), ("f0x", dc)],
                            writes=([("psu", kb)] if dc == 7 else []),
                            extra=p.predeps(("psu", kb)) if dc == 0 else [])
                    p.op("act", lambda E, kb=kb: E.activation(
                        out=t_sg[kb][:], in_=pg[kb][:], func=AF.Silu,
                        scale=INV127),
                        reads=[("psg", kb)], writes=[("fsg", kb)])
                    p.op("dve", lambda E, kb=kb, hc=hc: E.scalar_tensor_tensor(
                        out=t_h[:, hc * PCF:(hc + 1) * PCF], in0=pu[kb][:],
                        scalar=INV127, in1=t_sg[kb][:], op0=ALU.mult,
                        op1=ALU.mult),
                        reads=[("psu", kb), ("fsg", kb)], writes=[("fh", hc)])
                for half in range(2):
                    for hc in range(24):
                        for do in range(4):
                            dd = half * 4 + do
                            p.op("pe", lambda E, hc=hc, do=do, dd=dd: E.matmul(
                                out=pd[do][:],
                                lhsT=t_wd[:, hc * 1024 + dd * 128:
                                          hc * 1024 + (dd + 1) * 128],
                                rhs=t_h[:, hc * PCF:(hc + 1) * PCF],
                                start=(hc == 0), stop=(hc == 23)),
                                reads=[("wd", hc), ("fh", hc)],
                                writes=([("psd", do)] if hc == 23 else []),
                                extra=p.predeps(("psd", do)) if hc == 0 else [])
                    for do in range(4):
                        dd = half * 4 + do
                        ko = dd % 2
                        p.op("dve", lambda E, do=do, ko=ko: E.tensor_scalar_mul(
                            out=t_fo[ko][:], in0=pd[do][:], scalar1=INV127),
                            reads=[("psd", do)], writes=[("f0o", ko)])
                        p.dma("sp", FFN0[dd * 128:(dd + 1) * 128, c0:c0 + PCF],
                              t_fo[ko][:], reads=[("f0o", ko)],
                              writes=[("FFN0", pc, dd)])
            p.barrier()

    def emit_l0():
        with contextlib.ExitStack() as st:
            t_xp = sb("l0xp", [128, NDC0 * 96], BF16, st)
            t_dt = sb("l0dt", [RNK, D0], BF16, st)
            stg = sb("l0stg", [128, 1024], F32, st)
            t_xh = [sb(f"l0xh{d}", [128, PC0 + 3], F32, st)
                    for d in range(NDC0)]
            t_co = [sb(f"l0co{d}", [128, PC0], BF16, st) for d in range(NDC0)]
            t_dl = [sb(f"l0dl{d}", [128, PC0], F32, st) for d in range(NDC0)]
            t_y = [sb(f"l0y{d}", [128, PC0], F32, st) for d in range(NDC0)]
            t_bc = sb("l0bc", [32, PC0], F32, st)
            t_dtl = sb("l0dtl", [RNK, PC0], BF16, st)
            t_t = sb("l0t", [128, PC0], F32, st)
            t_a = sb("l0a", [128, PC0], F32, st)
            t_r = sb("l0r", [128, PC0], F32, st)
            t_dAp = [sb(f"l0dAp{i}", [128, NST * PC0], F32, st)
                     for i in range(2)]
            t_T1p = [sb("l0T1p0", [128, NST * PC0], F32, st)]
            t_Bup = [sb("l0Bup0", [128, NST * PC0], F32, st)]
            t_hp = [sb(f"l0hp{i}", [128, NST * PC0], F32, st)
                     for i in range(3)]
            t_Bp = sb("l0Bp", [128, NST * PC0], F32, st)
            t_Cp = sb("l0Cp", [128, NST * PC0], F32, st)
            t_sg = sb("l0sg", [128, NST], F32, st)
            t_f = [sb(f"l0f{i}", [128, PC0], F32, st) for i in range(2)]
            t_xtm = [sb(f"l0xt{i}", [128, D0], F32, st) for i in range(2)]
            t_o = [sb("l0o0", [128, D0], F32, st)]
            p96 = ps("l0p96", [96, PC0], F32, st)
            pdt = [ps(f"l0pdt{i}", [128, PC0], F32, st) for i in range(2)]
            ptm = [ps(f"l0ptm{i}", [128, D0], F32, st) for i in range(2)]

            for dc in range(NDC0):
                load_bf(t_xp[:, dc * 96:(dc + 1) * 96],
                        xp0T[dc * 128:(dc + 1) * 128, :], ("xp0", dc), st, stg)
            load_bf(t_dt[0:RNK, :], dt0T, "dt0", st, stg, rows=RNK)
            p.op("dve", lambda E: E.memset(t_st0[:], 0.0), writes=["st0"])

            for b in range(B_LOC):
                if b > 0:
                    p.op("dve", lambda E: E.memset(t_st0[:], 0.0),
                         writes=["st0"])
                for pc in range(L // PC0):
                    c0 = b * L + pc * PC0
                    for dc in range(NDC0):
                        if pc == 0:
                            p.op("dve", lambda E, dc=dc: E.memset(
                                t_xh[dc][:, 0:3], 0.0), writes=[("xh", dc)])
                            p.dma("sp", t_xh[dc][:, 3:3 + PC0],
                                  XN32[dc * 128:(dc + 1) * 128, c0:c0 + PC0],
                                  writes=[("xh", dc)])
                        else:
                            p.dma("sp", t_xh[dc][:, 0:PC0 + 3],
                                  XN32[dc * 128:(dc + 1) * 128,
                                       c0 - 3:c0 + PC0],
                                  writes=[("xh", dc)])
                    for dc in range(NDC0):
                        p.op("dve", lambda E, dc=dc: E.tensor_scalar(
                            out=t_co[dc][:], in0=t_xh[dc][:, 3:3 + PC0],
                            scalar1=t_cw0[:, dc * 4 + 3:dc * 4 + 4],
                            scalar2=t_cb0[:, dc:dc + 1],
                            op0=ALU.mult, op1=ALU.add),
                            reads=[("xh", dc), "cw0", "cb0"],
                            writes=[("co", dc)])
                        for j in range(3):
                            p.op("dve", lambda E, dc=dc, j=j:
                                 E.scalar_tensor_tensor(
                                     out=t_co[dc][:],
                                     in0=t_xh[dc][:, j:j + PC0],
                                     scalar=t_cw0[:, dc * 4 + j:dc * 4 + j + 1],
                                     in1=t_co[dc][:],
                                     op0=ALU.mult, op1=ALU.add),
                                 reads=[("xh", dc), "cw0", ("co", dc)],
                                 writes=[("co", dc)])
                    if L0_DEBUG:
                        for dc in range(NDC0):
                            p.dma("act", XNBF[dc * 128:(dc + 1) * 128,
                                              c0:c0 + PC0], t_co[dc][:],
                                  reads=[("co", dc)], writes=[("dbgC", dc)])
                    for dc in range(NDC0):
                        p.op("pe", lambda E, dc=dc: E.matmul(
                            out=p96[:], lhsT=t_xp[:, dc * 96:(dc + 1) * 96],
                            rhs=t_co[dc][:], start=(dc == 0),
                            stop=(dc == NDC0 - 1)),
                            reads=[("xp0", dc), ("co", dc)],
                            writes=(["p96"] if dc == NDC0 - 1 else []),
                            extra=p.predeps("p96") if dc == 0 else [])
                    p.op("act", lambda E: E.copy(out=t_bc[:], in_=p96[64:96, :]),
                         reads=["p96"], writes=["bc"])
                    p.op("act", lambda E: E.copy(out=t_dtl[:],
                                                 in_=p96[0:64, :]),
                         reads=["p96"], writes=["dtl"])
                    p.dma("sp", BCR[0, b, :, pc * PC0:pc * PC0 + PC0],
                          t_bc[:], reads=["bc"], writes=[("BCR0", b, pc)])
                    for dc in range(NDC0):
                        kq = dc % 2
                        p.op("pe", lambda E, dc=dc, kq=kq: E.matmul(
                            out=pdt[kq][:], lhsT=t_dt[0:RNK,
                                                      dc * 128:(dc + 1) * 128],
                            rhs=t_dtl[:], start=True, stop=True),
                            reads=["dt0", "dtl"], writes=[("pdt", kq)],
                            extra=p.predeps(("pdt", kq)))
                        p.op("dve", lambda E, dc=dc, kq=kq: E.tensor_scalar(
                            out=t_t[:], in0=pdt[kq][:],
                            scalar1=t_db0[:, dc:dc + 1], scalar2=None,
                            op0=ALU.add),
                            reads=[("pdt", kq), "db0"], writes=["spt"])
                        p._record(p.wr["spt"], [], [("pdtq", kq)])
                        p.op("dve", lambda E: E.scalar_tensor_tensor(
                            out=t_a[:], in0=t_t[:], scalar=-1.0, in1=t_t[:],
                            op0=ALU.mult, op1=ALU.max),
                            reads=["spt"], writes=["spa"])
                        p.op("act", lambda E: E.activation(
                            out=t_a[:], in_=t_a[:], func=AF.Exp, scale=-1.0),
                            reads=["spa"], writes=["spa"])
                        p.op("act", lambda E: E.activation(
                            out=t_a[:], in_=t_a[:], func=AF.Ln, bias=1.0),
                            reads=["spa"], writes=["spa"])
                        p.op("dve", lambda E: E.tensor_scalar_max(
                            out=t_r[:], in0=t_t[:], scalar1=0.0),
                            reads=["spt"], writes=["spr"])
                        p.op("dve", lambda E, dc=dc: E.tensor_add(
                            out=t_dl[dc][:], in0=t_r[:], in1=t_a[:]),
                            reads=["spr", "spa"], writes=[("dl", dc)])
                        if L0_DEBUG:
                            p.dma("act", D1[dc * 128:(dc + 1) * 128,
                                            c0:c0 + PC0], t_dl[dc][:],
                                  reads=[("dl", dc)], writes=[("dbgD", dc)])
                    p.dma("sp", t_Bp[:],
                          BCR[0, b, 0:16,
                              pc * PC0:pc * PC0 + PC0].rearrange(
                              "(one n) l -> one n l", one=1).broadcast_to(
                              [128, NST, PC0]),
                          reads=[("BCR0", b, pc)], writes=["Bp"])
                    p.dma("sp", t_Cp[:],
                          BCR[0, b, 16:32,
                              pc * PC0:pc * PC0 + PC0].rearrange(
                              "(one n) l -> one n l", one=1).broadcast_to(
                              [128, NST, PC0]),
                          reads=[("BCR0", b, pc)], writes=["Cp"])
                    WP = NST * PC0

                    def l0_A(dc):
                        kq = dc % 2
                        kh = dc % 3
                        for n in range(NST):
                            p.op("act", lambda E, dc=dc, n=n, kq=kq:
                                 E.activation(
                                     out=t_dAp[kq][:, n * PC0:(n + 1) * PC0],
                                     in_=t_dl[dc][:], func=AF.Exp,
                                     scale=t_A0[:, dc * NST + n:
                                                dc * NST + n + 1]),
                                 reads=[("dl", dc), "A0"],
                                 writes=[("dAp", kq, n)])
                            p.op("act", lambda E, dc=dc, n=n, kq=kq:
                                 E.activation(
                                     out=t_T1p[0][:, n * PC0:(n + 1) * PC0],
                                     in_=t_dAp[kq][:, n * PC0:(n + 1) * PC0],
                                     func=AF.Abs,
                                     scale=t_iA0[:, dc * NST + n:
                                                 dc * NST + n + 1],
                                     bias=t_niA0[:, dc * NST + n:
                                                 dc * NST + n + 1]),
                                 reads=[("dAp", kq, n), "iA0", "niA0"],
                                 writes=[("T1p", 0, n)])
                        dA_all = [("dAp", kq, n) for n in range(NST)]
                        T1_all = [("T1p", 0, n) for n in range(NST)]
                        p.op("dve", lambda E, dc=dc: E.tensor_tensor(
                            out=t_Bup[0][:].rearrange(
                                "p (n l) -> p n l", n=NST),
                            in0=t_xh[dc][:, 3:3 + PC0].rearrange(
                                "p (one l) -> p one l", one=1).broadcast_to(
                                [128, NST, PC0]),
                            in1=t_Bp[:].rearrange("p (n l) -> p n l", n=NST),
                            op=ALU.mult),
                            reads=[("xh", dc), "Bp"], writes=[("Bup", 0)])
                        p.op("dve", lambda E: E.tensor_mul(
                            out=t_Bup[0][:], in0=t_T1p[0][:],
                            in1=t_Bup[0][:]),
                            reads=T1_all + [("Bup", 0)], writes=[("Bup", 0)])
                        p.op("dve", lambda E, dc=dc, kq=kq: E.tensor_tensor(
                            out=t_sg[:], in0=t_dAp[kq][:, 0:WP:PC0],
                            in1=t_st0[:, dc * NST:(dc + 1) * NST],
                            op=ALU.mult),
                            reads=dA_all + ["st0"], writes=["sg0"])
                        p.op("dve", lambda E: E.tensor_tensor(
                            out=t_Bup[0][:, 0:WP:PC0],
                            in0=t_Bup[0][:, 0:WP:PC0], in1=t_sg[:],
                            op=ALU.add),
                            reads=["sg0", ("Bup", 0)], writes=[("Bup", 0)])
                        p.op("dve", lambda E, kq=kq: E.memset(
                            t_dAp[kq][:, 0:WP:PC0], 0.0),
                            reads=[], writes=dA_all)
                        p.op("dve", lambda E, kq=kq, kh=kh:
                             E.tensor_tensor_scan(
                                 out=t_hp[kh][:], data0=t_dAp[kq][:],
                                 data1=t_Bup[0][:], initial=0.0,
                                 op0=ALU.mult, op1=ALU.add),
                             reads=dA_all + [("Bup", 0)],
                             writes=[("hp", kh)])
                        p.op("dve", lambda E, dc=dc, kh=kh: E.tensor_copy(
                            out=t_st0[:, dc * NST:(dc + 1) * NST],
                            in_=t_hp[kh][:, PC0 - 1:WP:PC0]),
                            reads=[("hp", kh)], writes=["st0"])

                    def l0_B(dc):
                        kh = dc % 3
                        p.op("pool", lambda E, kh=kh: E.tensor_mul(
                            out=t_hp[kh][:], in0=t_hp[kh][:], in1=t_Cp[:]),
                            reads=[("hp", kh), "Cp"],
                            writes=[("hp", kh)])
                        p.op("pool", lambda E, kh=kh: E.tensor_add(
                            out=t_hp[kh][:, 0:8 * PC0],
                            in0=t_hp[kh][:, 0:8 * PC0],
                            in1=t_hp[kh][:, 8 * PC0:16 * PC0]),
                            reads=[("hp", kh)], writes=[("hp", kh)])
                        for w in (4, 2):
                            p.op("dve", lambda E, kh=kh, w=w: E.tensor_add(
                                out=t_hp[kh][:, 0:w * PC0],
                                in0=t_hp[kh][:, 0:w * PC0],
                                in1=t_hp[kh][:, w * PC0:2 * w * PC0]),
                                reads=[("hp", kh)], writes=[("hp", kh)])
                        p.op("dve", lambda E, dc=dc, kh=kh: E.tensor_add(
                            out=t_y[dc][:], in0=t_hp[kh][:, 0:PC0],
                            in1=t_hp[kh][:, PC0:2 * PC0]),
                            reads=[("hp", kh)],
                            writes=[("y", dc), ("hp", kh)])

                    for dc in range(NDC0):
                        l0_A(dc)
                        if dc >= 2:
                            l0_B(dc - 2)
                    l0_B(NDC0 - 2)
                    l0_B(NDC0 - 1)
                    for dc in range(NDC0):
                        kq = dc % 2
                        if L0_DEBUG:
                            p.dma("act", U1[dc * 128:(dc + 1) * 128,
                                            c0:c0 + PC0], t_y[dc][:],
                                  reads=[("y", dc)], writes=[("dbgY", dc)])
                        p.op("dve", lambda E, dc=dc: E.scalar_tensor_tensor(
                            out=t_y[dc][:], in0=t_xh[dc][:, 3:3 + PC0],
                            scalar=t_dsk[:, dc:dc + 1], in1=t_y[dc][:],
                            op0=ALU.mult, op1=ALU.add),
                            reads=[("xh", dc), "dsk", ("y", dc)],
                            writes=[("y", dc)])
                        p.dma("sp", t_f[kq][:],
                              FFN0[dc * 128:(dc + 1) * 128, c0:c0 + PC0],
                              writes=[("ff", kq)])
                        p.op("dve", lambda E, dc=dc, kq=kq: E.tensor_add(
                            out=t_y[dc][:], in0=t_y[dc][:], in1=t_f[kq][:]),
                            reads=[("ff", kq), ("y", dc)], writes=[("y", dc)])
                    for tb in range(PC0 // 128):
                        kq = tb % 2
                        for dc in range(NDC0):
                            p.op("pe", lambda E, dc=dc, tb=tb, kq=kq:
                                 E.transpose(
                                     out=ptm[kq][:, dc * 128:(dc + 1) * 128],
                                     in_=t_y[dc][:, tb * 128:(tb + 1) * 128],
                                     identity=t_id_f[:]),
                                 reads=[("y", dc), "idf"],
                                 writes=[("ptm", kq, dc)])
                        r0 = pc * PC0 + tb * 128
                        p.dma("sp", t_xtm[kq][:], x_in[b, r0:r0 + 128, :],
                              writes=[("xtm", kq)])
                        p.op("dve", lambda E, kq=kq: E.tensor_add(
                            out=t_o[0][:], in0=ptm[kq][:], in1=t_xtm[kq][:]),
                            reads=[("xtm", kq)] + [("ptm", kq, d)
                                                   for d in range(NDC0)],
                            writes=[("otm", 0)])
                        p.dma("sp", X1TM[b, r0:r0 + 128, :], t_o[0][:],
                              reads=[("otm", 0)], writes=[("X1TM", b, r0)])
            p.barrier()

    def emit_i1():
        with contextlib.ExitStack() as st:
            t_ip = sb("i1ip", [128, NDC0 * 4096], BF16, st)
            stg = sb("i1stg", [128, 4096], F32, st)
            t_xnb = [sb(f"i1x{d}", [128, PCF], BF16, st) for d in range(NDC0)]
            t_xh = [sb(f"i1xh{d}", [128, PCF + 3], F32, st)
                    for d in range(NDC1)]
            t_tail = sb("i1tail", [128, 3 * NDC1], F32, st)
            t_zs = [sb(f"i1z{i}", [128, PCF], F32, st) for i in range(2)]
            t_u = [sb(f"i1u{d}", [128, PCF], F32, st) for d in range(2)]
            t_ub = [sb(f"i1ub{d}", [128, PCF], BF16, st) for d in range(2)]
            t_xp = sb("i1xp", [128, NDC1 * 96], BF16, st)
            t_dt = sb("i1dt", [RNK, DI], BF16, st)
            t_bc = sb("i1bc", [32, PCF], F32, st)
            t_dtl = sb("i1dtl", [RNK, PCF], BF16, st)
            t_t = sb("i1t", [128, PCF], F32, st)
            t_a = sb("i1a", [128, PCF], F32, st)
            t_r = sb("i1r", [128, PCF], F32, st)
            t_dl = [sb(f"i1dl{i}", [128, PCF], F32, st) for i in range(2)]
            pxz = [ps(f"i1px{i}", [128, PCF], F32, st) for i in range(4)]
            p96 = ps("i1p96", [96, PCF], F32, st)
            pdt = [ps(f"i1pdt{i}", [128, PCF], F32, st) for i in range(2)]

            for kc in range(NDC0):
                load_bf(t_ip[:, kc * 4096:(kc + 1) * 4096],
                        ipT[kc * 128:(kc + 1) * 128, :], ("ip", kc), st, stg)
            for dc in range(NDC1):
                load_bf(t_xp[:, dc * 96:(dc + 1) * 96],
                        xp1T[dc * 128:(dc + 1) * 128, :], ("xp1", dc), st, stg)
            load_bf(t_dt[0:RNK, :], dt1T, "dt1", st, stg, rows=RNK)

            for pc in range(T // PCF):
                c0 = pc * PCF
                new_seq = (c0 % L) == 0
                for dc in range(NDC0):
                    p.dma("sp", t_xnb[dc][:],
                          XNBF[dc * 128:(dc + 1) * 128, c0:c0 + PCF],
                          writes=[("i1x", dc)])
                for do in range(NDC1):
                    kq = do % 4
                    for kc in range(NDC0):
                        p.op("pe", lambda E, do=do, kc=kc, kq=kq: E.matmul(
                            out=pxz[kq][:],
                            lhsT=t_ip[:, kc * 4096 + do * 128:
                                      kc * 4096 + (do + 1) * 128],
                            rhs=t_xnb[kc][:], start=(kc == 0),
                            stop=(kc == NDC0 - 1)),
                            reads=[("ip", kc), ("i1x", kc)],
                            writes=([("pxz", kq)] if kc == NDC0 - 1 else []),
                            extra=p.predeps(("pxz", kq)) if kc == 0 else [])
                    if new_seq:
                        p.op("dve", lambda E, do=do: E.memset(
                            t_xh[do][:, 0:3], 0.0), writes=[("i1xh", do)])
                    else:
                        p.op("dve", lambda E, do=do: E.tensor_copy(
                            out=t_xh[do][:, 0:3],
                            in_=t_tail[:, do * 3:(do + 1) * 3]),
                            reads=[("tail", do)], writes=[("i1xh", do)])
                    p.op("act", lambda E, do=do, kq=kq: E.copy(
                        out=t_xh[do][:, 3:3 + PCF], in_=pxz[kq][:]),
                        reads=[("pxz", kq)], writes=[("i1xh", do)])
                    p.op("dve", lambda E, do=do: E.tensor_copy(
                        out=t_tail[:, do * 3:(do + 1) * 3],
                        in_=t_xh[do][:, PCF:PCF + 3]),
                        reads=[("i1xh", do)], writes=[("tail", do)])
                for dz in range(NDC1):
                    do = NDC1 + dz
                    kq = do % 4
                    kz = dz % 2
                    for kc in range(NDC0):
                        p.op("pe", lambda E, do=do, kc=kc, kq=kq: E.matmul(
                            out=pxz[kq][:],
                            lhsT=t_ip[:, kc * 4096 + do * 128:
                                      kc * 4096 + (do + 1) * 128],
                            rhs=t_xnb[kc][:], start=(kc == 0),
                            stop=(kc == NDC0 - 1)),
                            reads=[("ip", kc), ("i1x", kc)],
                            writes=([("pxz", kq)] if kc == NDC0 - 1 else []),
                            extra=p.predeps(("pxz", kq)) if kc == 0 else [])
                    p.op("act", lambda E, dz=dz, kq=kq, kz=kz: E.activation(
                        out=t_zs[kz][:], in_=pxz[kq][:], func=AF.Silu),
                        reads=[("pxz", kq)], writes=[("zs", kz)])
                    p.dma("sp", ZS[dz * 128:(dz + 1) * 128, c0:c0 + PCF],
                          t_zs[kz][:], reads=[("zs", kz)],
                          writes=[("ZS", pc, dz)])
                for dc in range(NDC1):
                    ku = dc % 2
                    p.op("dve", lambda E, dc=dc, ku=ku: E.tensor_scalar(
                        out=t_u[ku][:], in0=t_xh[dc][:, 3:3 + PCF],
                        scalar1=t_cw1[:, dc * 4 + 3:dc * 4 + 4],
                        scalar2=t_cb1[:, dc:dc + 1],
                        op0=ALU.mult, op1=ALU.add),
                        reads=[("i1xh", dc), "cw1", "cb1"],
                        writes=[("u", ku)])
                    for j in range(3):
                        p.op("dve", lambda E, dc=dc, j=j, ku=ku:
                             E.scalar_tensor_tensor(
                                 out=t_u[ku][:], in0=t_xh[dc][:, j:j + PCF],
                                 scalar=t_cw1[:, dc * 4 + j:dc * 4 + j + 1],
                                 in1=t_u[ku][:], op0=ALU.mult, op1=ALU.add),
                             reads=[("i1xh", dc), "cw1", ("u", ku)],
                             writes=[("u", ku)])
                    p.op("act", lambda E, ku=ku: E.activation(
                        out=t_u[ku][:], in_=t_u[ku][:], func=AF.Silu),
                        reads=[("u", ku)], writes=[("u", ku)])
                    p.dma("sp", U1[dc * 128:(dc + 1) * 128, c0:c0 + PCF],
                          t_u[ku][:], reads=[("u", ku)],
                          writes=[("U1", pc, dc)])
                    p.op("dve", lambda E, ku=ku: E.tensor_copy(
                        out=t_ub[ku][:], in_=t_u[ku][:]),
                        reads=[("u", ku)], writes=[("ub", ku)])
                    p.op("pe", lambda E, dc=dc, ku=ku: E.matmul(
                        out=p96[:], lhsT=t_xp[:, dc * 96:(dc + 1) * 96],
                        rhs=t_ub[ku][:], start=(dc == 0),
                        stop=(dc == NDC1 - 1)),
                        reads=[("xp1", dc), ("ub", ku)],
                        writes=(["i1p96"] if dc == NDC1 - 1 else []),
                        extra=p.predeps("i1p96") if dc == 0 else [])
                p.op("act", lambda E: E.copy(out=t_bc[:], in_=p96[64:96, :]),
                     reads=["i1p96"], writes=["i1bc"])
                p.op("act", lambda E: E.copy(out=t_dtl[:], in_=p96[0:64, :]),
                     reads=["i1p96"], writes=["i1dtl"])
                b, lc = divmod(c0, L)
                p.dma("sp", BCR[1, b, :, lc:lc + PCF], t_bc[:],
                      reads=["i1bc"], writes=[("BCR1", pc)])
                for dc in range(NDC1):
                    kq = dc % 2
                    p.op("pe", lambda E, dc=dc, kq=kq: E.matmul(
                        out=pdt[kq][:],
                        lhsT=t_dt[0:RNK, dc * 128:(dc + 1) * 128],
                        rhs=t_dtl[:], start=True, stop=True),
                        reads=["dt1", "i1dtl"], writes=[("i1pdt", kq)],
                        extra=p.predeps(("i1pdt", kq)))
                    p.op("dve", lambda E, dc=dc, kq=kq: E.tensor_scalar(
                        out=t_t[:], in0=pdt[kq][:],
                        scalar1=t_db1[:, dc:dc + 1], scalar2=None,
                        op0=ALU.add),
                        reads=[("i1pdt", kq), "db1"], writes=["i1spt"])
                    p._record(p.wr["i1spt"], [], [("i1pdtq", kq)])
                    p.op("dve", lambda E: E.scalar_tensor_tensor(
                        out=t_a[:], in0=t_t[:], scalar=-1.0, in1=t_t[:],
                        op0=ALU.mult, op1=ALU.max),
                        reads=["i1spt"], writes=["i1spa"])
                    p.op("act", lambda E: E.activation(
                        out=t_a[:], in_=t_a[:], func=AF.Exp, scale=-1.0),
                        reads=["i1spa"], writes=["i1spa"])
                    p.op("act", lambda E: E.activation(
                        out=t_a[:], in_=t_a[:], func=AF.Ln, bias=1.0),
                        reads=["i1spa"], writes=["i1spa"])
                    p.op("dve", lambda E: E.tensor_scalar_max(
                        out=t_r[:], in0=t_t[:], scalar1=0.0),
                        reads=["i1spt"], writes=["i1spr"])
                    kd = dc % 2
                    p.op("dve", lambda E, kd=kd: E.tensor_add(
                        out=t_dl[kd][:], in0=t_r[:], in1=t_a[:]),
                        reads=["i1spr", "i1spa"], writes=[("i1dl", kd)])
                    p.dma("sp", D1[dc * 128:(dc + 1) * 128, c0:c0 + PCF],
                          t_dl[kd][:], reads=[("i1dl", kd)],
                          writes=[("D1", pc, dc)])
            p.barrier()

    def emit_moe():
        with contextlib.ExitStack() as st:
            stg = sb("mstg", [128, 1024], F32, st)
            t_gvt = [sb(f"mgvt{b}", [128, 16], F32, st) for b in range(B_LOC)]
            t_pfT = [sb(f"mpfT{b}", [128, 128], F32, st)
                     for b in range(B_LOC)]
            t_mio = sb("mmio", [128, MTOP], F32, st)
            t_mioc = sb("mmioc", [128, 2], F32, st)
            stA = contextlib.ExitStack()
            t_gwb = sb("mgw", [128, NDC0 * 8], BF16, stA)
            t_xnb = [sb(f"mx{i}", [128, 512], BF16, stA) for i in range(2)]
            t_gl = [sb(f"mgl{b}", [8, L], F32, stA) for b in range(B_LOC)]
            t_ex = [sb(f"mex{b}", [8, L], F32, stA) for b in range(B_LOC)]
            t_tt = [sb(f"mtt{b}", [128, 128], F32, stA) for b in range(B_LOC)]
            t_sc1 = [sb(f"msc1{b}", [8, 1], F32, stA) for b in range(B_LOC)]
            t_sc2 = [sb(f"msc2{b}", [8, 1], F32, stA) for b in range(B_LOC)]
            t_sc3 = [sb(f"msc3{b}", [8, 1], F32, stA) for b in range(B_LOC)]
            t_nmx = [sb(f"mnm{b}", [128, 16], F32, stA) for b in range(B_LOC)]
            t_exs = sb("mexs", [128, 8], F32, stA)
            t_sm = [sb(f"msm{b}", [128, 16], F32, stA) for b in range(B_LOC)]
            t_cg = [sb(f"mcg{b}", [8, L], F32, stA) for b in range(B_LOC)]
            t_lo = [sb(f"mlo{b}", [8, 1], F32, stA) for b in range(B_LOC)]
            t_hi = [sb(f"mhi{b}", [8, 1], F32, stA) for b in range(B_LOC)]
            t_ta = [sb(f"mta{b}", [8, 1], F32, stA) for b in range(B_LOC)]
            t_g = [sb(f"mg{b}", [8, 1], F32, stA) for b in range(B_LOC)]
            t_m = [sb(f"mm{b}", [8, L], F32, stA) for b in range(B_LOC)]
            t_c = [sb(f"mc{b}", [8, 1], F32, stA) for b in range(B_LOC)]
            t_ones = sb("mones", [8, L], F32, stA)
            t_fl = [sb(f"mfl{b}", [8, L], F32, stA) for b in range(B_LOC)]
            t_pf = [sb(f"mpf{b}", [8, L], F32, stA) for b in range(B_LOC)]
            t_W = [sb(f"mW{b}", [8, L], F32, stA) for b in range(B_LOC)]
            t_pF = [sb(f"mpF{b}", [8, L], F32, stA) for b in range(B_LOC)]
            t_pW = sb("mpW", [8, L], F32, stA)
            t_wcg = sb("mwcg", [8, L], F32, stA)
            t_gvr = sb("mgvr", [1, L], F32, stA)
            t_lt8 = sb("mlt8", [8, 8], BF16, stA)
            t_on8 = sb("mon8", [8, 1], F32, stA)

            for kc in range(NDC0):
                load_bf(t_gwb[:, kc * 8:(kc + 1) * 8],
                        gw[kc * 128:(kc + 1) * 128, :], ("gw", kc), st, stg)
            p.op("dve", lambda E: E.memset(t_ones[:], 1.0), writes=["mones"])
            p.op("dve", lambda E: E.memset(t_on8[:], 1.0), writes=["mon8"])
            p.dma("sp", t_lt8[:], lt8, writes=["mlt8"])
            p.dma("sp", t_mio[:], miota.broadcast_to([128, MTOP]),
                  writes=["mmio"])
            p.dma("sp", t_mioc[:], miocol, writes=["mmioc"])

            pg8 = ps("mpg", [8, 512], F32, stA)
            ptr = ps("mptr", [128, 8], F32, stA)
            ptr2 = ps("mptr2", [8, 128], F32, stA)
            pgv = ps("mpgv", [8, 512], F32, stA)

            for b in range(B_LOC):
                for half in range(L // 512):
                    c0 = b * L + half * 512
                    for kc in range(NDC0):
                        kx = kc % 2
                        p.dma("sp", t_xnb[kx][:],
                              XNBF[kc * 128:(kc + 1) * 128, c0:c0 + 512],
                              writes=[("mxn", kx)])
                        p.op("pe", lambda E, kc=kc, kx=kx: E.matmul(
                            out=pg8[:], lhsT=t_gwb[:, kc * 8:(kc + 1) * 8],
                            rhs=t_xnb[kx][:], start=(kc == 0),
                            stop=(kc == NDC0 - 1)),
                            reads=[("gw", kc), ("mxn", kx)],
                            writes=(["mpg"] if kc == NDC0 - 1 else []),
                            extra=p.predeps("mpg") if kc == 0 else [])
                    p.op("act", lambda E, b=b, half=half: E.copy(
                        out=t_gl[b][:, half * 512:(half + 1) * 512],
                        in_=pg8[:]), reads=["mpg"], writes=[("gl", b)])
            for b in range(B_LOC):
                p.op("dve", lambda E, b=b: E.tensor_scalar_max(
                    out=t_gl[b][:], in0=t_gl[b][:], scalar1=1e-6),
                    reads=[("gl", b)], writes=[("gl", b)])
                p.op("act", lambda E, b=b: E.activation(
                    out=t_gl[b][:], in_=t_gl[b][:], func=AF.Ln),
                    reads=[("gl", b)], writes=[("gl", b)])
            for it in range(8):
                for b in range(B_LOC):
                    p.op("dve", lambda E, b=b: E.reduce_max(
                        out=t_sc1[b][:], in_=t_gl[b][:],
                        axis=mybir.AxisListType.X, negate=True),
                        reads=[("gl", b)], writes=[("sc1", b)])
                    p.op("act", lambda E, b=b: E.activation(
                        out=t_ex[b][:], in_=t_gl[b][:], func=AF.Exp,
                        bias=t_sc1[b][:, 0:1]),
                        reads=[("gl", b), ("sc1", b)], writes=[("ex", b)])
                    p.op("dve", lambda E, b=b: E.reduce_sum(
                        out=t_sc2[b][:], in_=t_ex[b][:],
                        axis=mybir.AxisListType.X),
                        reads=[("ex", b)], writes=[("sc2", b)])
                    p.op("act", lambda E, b=b: E.activation(
                        out=t_sc3[b][:], in_=t_sc2[b][:], func=AF.Ln),
                        reads=[("sc2", b)], writes=[("sc3", b)])
                    p.op("dve", lambda E, b=b: E.tensor_sub(
                        out=t_sc3[b][:], in0=t_sc3[b][:], in1=t_sc1[b][:]),
                        reads=[("sc3", b), ("sc1", b)], writes=[("sc3", b)])
                    p.op("dve", lambda E, b=b: E.tensor_scalar(
                        out=t_gl[b][:], in0=t_gl[b][:],
                        scalar1=t_sc3[b][:, 0:1], scalar2=None,
                        op0=ALU.subtract),
                        reads=[("gl", b), ("sc3", b)], writes=[("gl", b)])
                    for lc in range(L // 128):
                        p.op("pe", lambda E, b=b, lc=lc: E.transpose(
                            out=ptr[:], in_=t_gl[b][0:8,
                                                    lc * 128:(lc + 1) * 128],
                            identity=t_id_f[0:8, 0:8]),
                            reads=[("gl", b), "idf"], writes=["mptr"])
                        p.op("act", lambda E, b=b, lc=lc: E.copy(
                            out=t_tt[b][:, lc * 8:(lc + 1) * 8], in_=ptr[:]),
                            reads=["mptr"], writes=[("mtt", b, lc)])
                    p.op("dve", lambda E, b=b: E.tensor_reduce(
                        out=t_nmx[b].rearrange("p (a one) -> p a one", one=1),
                        in_=t_tt[b][:].rearrange("p (a x) -> p a x", x=8),
                        axis=mybir.AxisListType.X, op=ALU.max, negate=True),
                        reads=[("mtt", b, lc) for lc in range(16)],
                        writes=[("nmx", b)])
                    for lc in range(16):
                        p.op("act", lambda E, b=b, lc=lc: E.activation(
                            out=t_exs[:],
                            in_=t_tt[b][:, lc * 8:(lc + 1) * 8],
                            func=AF.Exp, bias=t_nmx[b][:, lc:lc + 1],
                            accum_out=t_sm[b][:, lc:lc + 1]),
                            reads=[("mtt", b, lc), ("nmx", b)],
                            writes=[("msm", b, lc), "mexs"])
                    p.op("act", lambda E, b=b: E.activation(
                        out=t_sm[b][:], in_=t_sm[b][:], func=AF.Ln),
                        reads=[("msm", b, lc) for lc in range(16)],
                        writes=[("msml", b)])
                    p.op("dve", lambda E, b=b: E.tensor_sub(
                        out=t_sm[b][:], in0=t_sm[b][:], in1=t_nmx[b][:]),
                        reads=[("msml", b), ("nmx", b)], writes=[("msml", b)])
                    for lc in range(16):
                        p.op("dve", lambda E, b=b, lc=lc: E.tensor_scalar(
                            out=t_tt[b][:, lc * 8:(lc + 1) * 8],
                            in0=t_tt[b][:, lc * 8:(lc + 1) * 8],
                            scalar1=t_sm[b][:, lc:lc + 1], scalar2=None,
                            op0=ALU.subtract),
                            reads=[("msml", b), ("mtt", b, lc)],
                            writes=[("mtt", b, lc)])
                        p.op("pe", lambda E, b=b, lc=lc: E.transpose(
                            out=ptr2[:], in_=t_tt[b][:, lc * 8:(lc + 1) * 8],
                            identity=t_id_f[:]),
                            reads=[("mtt", b, lc), "idf"], writes=["mptr2"])
                        p.op("act", lambda E, b=b, lc=lc: E.copy(
                            out=t_gl[b][0:8, lc * 128:(lc + 1) * 128],
                            in_=ptr2[0:8, :]),
                            reads=["mptr2"], writes=[("gl", b)])
            for b in range(B_LOC):
                p.op("act", lambda E, b=b: E.activation(
                    out=t_cg[b][:], in_=t_gl[b][:], func=AF.Exp),
                    reads=[("gl", b)], writes=[("cg", b)])
                p.op("dve", lambda E, b=b: E.memset(t_lo[b][:], 0.0),
                     writes=[("lo", b)])
                p.op("dve", lambda E, b=b: E.memset(t_hi[b][:], 1.01),
                     writes=[("hi", b)])
            for it in range(30):
                for b in range(B_LOC):
                    p.op("dve", lambda E, b=b: E.tensor_add(
                        out=t_ta[b][:], in0=t_lo[b][:], in1=t_hi[b][:]),
                        reads=[("lo", b), ("hi", b)], writes=[("ta", b)])
                    p.op("dve", lambda E, b=b: E.tensor_scalar_mul(
                        out=t_ta[b][:], in0=t_ta[b][:], scalar1=0.5),
                        reads=[("ta", b)], writes=[("ta", b)])
                    p.op("dve", lambda E, b=b: E.tensor_scalar(
                        out=t_m[b][:], in0=t_cg[b][:],
                        scalar1=t_ta[b][:, 0:1], scalar2=None, op0=ALU.is_gt),
                        reads=[("cg", b), ("ta", b)], writes=[("msk", b)])
                    p.op("dve", lambda E, b=b: E.reduce_sum(
                        out=t_c[b][:], in_=t_m[b][:],
                        axis=mybir.AxisListType.X),
                        reads=[("msk", b)], writes=[("cnt", b)])
                    p.op("dve", lambda E, b=b: E.tensor_scalar(
                        out=t_g[b][:], in0=t_c[b][:], scalar1=float(MTOP),
                        scalar2=None, op0=ALU.is_gt),
                        reads=[("cnt", b)], writes=[("g", b)])
                    p.op("dve", lambda E, b=b: E.tensor_sub(
                        out=t_c[b][:], in0=t_ta[b][:], in1=t_lo[b][:]),
                        reads=[("ta", b), ("lo", b)], writes=[("cnt", b)])
                    p.op("dve", lambda E, b=b: E.tensor_mul(
                        out=t_c[b][:], in0=t_c[b][:], in1=t_g[b][:]),
                        reads=[("cnt", b), ("g", b)], writes=[("cnt", b)])
                    p.op("dve", lambda E, b=b: E.tensor_add(
                        out=t_lo[b][:], in0=t_lo[b][:], in1=t_c[b][:]),
                        reads=[("cnt", b), ("lo", b)], writes=[("lo", b)])
                    p.op("dve", lambda E, b=b: E.tensor_sub(
                        out=t_c[b][:], in0=t_hi[b][:], in1=t_ta[b][:]),
                        reads=[("hi", b), ("ta", b)], writes=[("cnt", b)])
                    p.op("dve", lambda E, b=b: E.tensor_mul(
                        out=t_c[b][:], in0=t_c[b][:], in1=t_g[b][:]),
                        reads=[("cnt", b), ("g", b)], writes=[("cnt", b)])
                    p.op("dve", lambda E, b=b: E.tensor_add(
                        out=t_hi[b][:], in0=t_ta[b][:], in1=t_c[b][:]),
                        reads=[("cnt", b), ("ta", b)], writes=[("hi", b)])
            for b in range(B_LOC):
                p.op("dve", lambda E, b=b: E.tensor_scalar(
                    out=t_fl[b][:], in0=t_cg[b][:], scalar1=t_hi[b][:, 0:1],
                    scalar2=None, op0=ALU.is_gt),
                    reads=[("cg", b), ("hi", b)], writes=[("str", b)])
                p.op("dve", lambda E, b=b: E.reduce_sum(
                    out=t_c[b][:], in_=t_fl[b][:], axis=mybir.AxisListType.X),
                    reads=[("str", b)], writes=[("c1", b)])
                p.op("dve", lambda E, b=b: E.tensor_scalar(
                    out=t_c[b][:], in0=t_c[b][:], scalar1=-1.0,
                    scalar2=float(MTOP), op0=ALU.mult, op1=ALU.add),
                    reads=[("c1", b)], writes=[("c1", b)])
                p.op("dve", lambda E, b=b: E.tensor_scalar(
                    out=t_m[b][:], in0=t_cg[b][:], scalar1=t_lo[b][:, 0:1],
                    scalar2=None, op0=ALU.is_gt),
                    reads=[("cg", b), ("lo", b)], writes=[("msk", b)])
                p.op("dve", lambda E, b=b: E.tensor_sub(
                    out=t_m[b][:], in0=t_m[b][:], in1=t_fl[b][:]),
                    reads=[("msk", b), ("str", b)], writes=[("msk", b)])
                p.op("dve", lambda E, b=b: E.tensor_tensor_scan(
                    out=t_pf[b][:], data0=t_ones[0:8, :], data1=t_m[b][:],
                    initial=0.0, op0=ALU.mult, op1=ALU.add),
                    reads=[("msk", b), "mones"], writes=[("pf", b)])
                p.op("dve", lambda E, b=b: E.tensor_scalar(
                    out=t_pf[b][:], in0=t_pf[b][:], scalar1=t_c[b][:, 0:1],
                    scalar2=None, op0=ALU.is_le),
                    reads=[("pf", b), ("c1", b)], writes=[("pf", b)])
                p.op("dve", lambda E, b=b: E.tensor_mul(
                    out=t_m[b][:], in0=t_m[b][:], in1=t_pf[b][:]),
                    reads=[("msk", b), ("pf", b)], writes=[("msk", b)])
                p.op("dve", lambda E, b=b: E.tensor_add(
                    out=t_fl[b][:], in0=t_fl[b][:], in1=t_m[b][:]),
                    reads=[("str", b), ("msk", b)], writes=[("fl", b)])
                p.op("dve", lambda E, b=b: E.tensor_tensor_scan(
                    out=t_pf[b][:], data0=t_ones[0:8, :], data1=t_fl[b][:],
                    initial=0.0, op0=ALU.mult, op1=ALU.add),
                    reads=[("fl", b), "mones"], writes=[("pf", b)])
                p.op("dve", lambda E, b=b: E.tensor_sub(
                    out=t_pf[b][:], in0=t_pf[b][:], in1=t_fl[b][:]),
                    reads=[("pf", b), ("fl", b)], writes=[("pf", b)])
                t_flb = sb(f"mflb{b}", [8, L], BF16, stA)
                p.op("dve", lambda E, b=b, t_flb=t_flb: E.tensor_copy(
                    out=t_flb[:], in_=t_fl[b][:]),
                    reads=[("fl", b)], writes=[("flb", b)])
                for sl in range(4):
                    p.op("pe", lambda E, b=b, sl=sl, t_flb=t_flb: E.matmul(
                        out=pg8[:], lhsT=t_lt8[:],
                        rhs=t_flb[:, sl * 512:(sl + 1) * 512],
                        start=True, stop=True),
                        reads=["mlt8", ("flb", b)], writes=["mpg"],
                        extra=p.predeps("mpg"))
                    p.op("dve", lambda E, b=b, sl=sl: E.tensor_scalar(
                        out=t_W[b][:, sl * 512:(sl + 1) * 512], in0=pg8[:],
                        scalar1=0.5, scalar2=None, op0=ALU.is_lt),
                        reads=["mpg"], writes=[("W", b)])
                    p._record(p.wr[("W", b)], [], ["mpg"])
                p.op("dve", lambda E, b=b: E.tensor_mul(
                    out=t_W[b][:], in0=t_W[b][:], in1=t_fl[b][:]),
                    reads=[("W", b), ("fl", b)], writes=[("W", b)])
                p.op("dve", lambda E, b=b: E.tensor_mul(
                    out=t_wcg[:], in0=t_W[b][:], in1=t_cg[b][:]),
                    reads=[("W", b), ("cg", b)], writes=["wcg"])
                for sl in range(4):
                    p.op("pe", lambda E, sl=sl: E.matmul(
                        out=pgv[0:1, :], lhsT=t_on8[:],
                        rhs=t_wcg[:, sl * 512:(sl + 1) * 512],
                        start=True, stop=True),
                        reads=["mon8", "wcg"], writes=["mpgv"],
                        extra=p.predeps("mpgv"))
                    p.op("act", lambda E, sl=sl: E.copy(
                        out=t_gvr[:, sl * 512:(sl + 1) * 512],
                        in_=pgv[0:1, :]),
                        reads=["mpgv"], writes=["gvr"])
                    p._record(p.wr["gvr"], [], ["mpgv"])
                p.dma("act", GVD, t_gvr[:], reads=["gvr"], writes=["GVD"])
                p.dma("sp", t_gvt[b][:],
                      GVD.rearrange("one (c pp) -> (one pp) c", pp=128),
                      reads=["GVD"], writes=[("gvt", b)],
                      allow_slow_non_contiguous=True)
                p.op("dve", lambda E, b=b: E.tensor_mul(
                    out=t_pF[b][:], in0=t_pf[b][:], in1=t_fl[b][:]),
                    reads=[("pf", b), ("fl", b)], writes=[("pF", b)])
                p.op("dve", lambda E, b=b: E.tensor_scalar(
                    out=t_m[b][:], in0=t_fl[b][:], scalar1=-1.0,
                    scalar2=None, op0=ALU.add),
                    reads=[("fl", b)], writes=[("msk", b)])
                p.op("dve", lambda E, b=b: E.tensor_add(
                    out=t_pF[b][:], in0=t_pF[b][:], in1=t_m[b][:]),
                    reads=[("pF", b), ("msk", b)], writes=[("pF", b)])
                p.op("dve", lambda E, b=b: E.tensor_mul(
                    out=t_pW[:], in0=t_pf[b][:], in1=t_W[b][:]),
                    reads=[("pf", b), ("W", b)], writes=["pW"])
                p.op("dve", lambda E, b=b: E.tensor_scalar(
                    out=t_m[b][:], in0=t_W[b][:], scalar1=-1.0,
                    scalar2=None, op0=ALU.add),
                    reads=[("W", b)], writes=[("msk", b)])
                p.op("dve", lambda E, b=b: E.tensor_add(
                    out=t_pW[:], in0=t_pW[:], in1=t_m[b][:]),
                    reads=["pW", ("msk", b)], writes=["pW"])
                p.dma("act", PWD[b], t_pW[:], reads=["pW"],
                      writes=[("PWD", b)])
                for lc in range(16):
                    p.op("pe", lambda E, b=b, lc=lc: E.transpose(
                        out=ptr[:], in_=t_pF[b][0:8,
                                               lc * 128:(lc + 1) * 128],
                        identity=t_id_f[0:8, 0:8]),
                        reads=[("pF", b), "idf"], writes=["mptr"])
                    p.op("act", lambda E, b=b, lc=lc: E.copy(
                        out=t_pfT[b][:, lc * 8:(lc + 1) * 8], in_=ptr[:]),
                        reads=["mptr"], writes=[("pfT", b)])
            p.barrier()
            stA.close()

            t_xtm = [sb(f"mxt{i}", [128, D0], BF16, st) for i in range(16)]
            t_ct = [sb(f"mct{i}", [128, MTOP], BF16, st) for i in range(2)]
            t_xs = [sb(f"mxs{q}", [128, D0], BF16, st) for q in range(2)]
            t_rx = [sb(f"mrx{d}", [128, 2 * 128], BF16, st)
                    for d in range(NDC0)]
            t_we = sb("mwe", [128, NDC0 * D0], BF16, st)
            t_eo = [sb(f"meo{d}", [128, MTOP], F32, st) for d in range(NDC0)]
            t_vals = [sb(f"mvl{e}", [128, 2 * D0], BF16, st)
                      for e in range(NEXP)]
            t_D = [sb(f"mD{e}", [128, 2 * L], BF16, st) for e in range(NEXP)]
            t_pwr = sb("mpwr", [128, L], F32, st)
            t_mo = [sb(f"mmo{i}", [128, 512], F32, st) for i in range(2)]
            pgx = [ps(f"mpgx{i}", [128, 512], F32, st) for i in range(4)]
            pxT = ps("mpxT", [128, 128], BF16, st)
            pex = [ps(f"mpex{i}", [128, MTOP], F32, st) for i in range(2)]
            pvT = ps("mpvT", [128, 128], F32, st)

            for b in range(B_LOC):
                for lc in range(16):
                    p.dma("sp", t_xtm[lc][:],
                          XNTM[b, lc * 128:(lc + 1) * 128, :],
                          writes=[("xtm", lc)])
                for e in range(NEXP):
                    for kc in range(NDC0):
                        load_bf(t_we[:, kc * D0:(kc + 1) * D0],
                                exps[e, kc * 128:(kc + 1) * 128, :],
                                ("we", kc), st, stg)
                    for lc in range(16):
                        kc2 = lc % 2
                        p.op("dve", lambda E, b=b, e=e, lc=lc, kc2=kc2:
                             E.tensor_scalar(
                                 out=t_ct[kc2][:], in0=t_mio[:],
                                 scalar1=t_pfT[b][:, lc * 8 + e:
                                                  lc * 8 + e + 1],
                                 scalar2=None, op0=ALU.is_equal),
                             reads=[("pfT", b), "mmio"],
                             writes=[("ct", kc2)])
                        for mh in range(2):
                            for dh in range(2):
                                pq = mh * 2 + dh
                                p.op("pe", lambda E, lc=lc, mh=mh, dh=dh,
                                     pq=pq, kc2=kc2: E.matmul(
                                         out=pgx[pq][:],
                                         lhsT=t_ct[kc2][:, mh * 128:
                                                        (mh + 1) * 128],
                                         rhs=t_xtm[lc][:, dh * 512:
                                                       (dh + 1) * 512],
                                         start=(lc == 0), stop=(lc == 15)),
                                     reads=[("ct", kc2), ("xtm", lc)],
                                     writes=([("pgx", pq)]
                                             if lc == 15 else []),
                                     extra=(p.predeps(("pgx", pq))
                                            if lc == 0 else []))
                    for mh in range(2):
                        for dh in range(2):
                            pq = mh * 2 + dh
                            p.op("act", lambda E, mh=mh, dh=dh, pq=pq: E.copy(
                                out=t_xs[mh][:, dh * 512:(dh + 1) * 512],
                                in_=pgx[pq][:]),
                                reads=[("pgx", pq)], writes=[("xs", mh)])
                            p._record(p.wr[("xs", mh)], [], [("pgx", pq)])
                    for dc in range(NDC0):
                        for mh in range(2):
                            p.op("pe", lambda E, dc=dc, mh=mh: E.transpose(
                                out=pxT[:],
                                in_=t_xs[mh][:, dc * 128:(dc + 1) * 128],
                                identity=t_id_bf[:]),
                                reads=[("xs", mh), "idbf"], writes=["pxT"])
                            p.op("act", lambda E, dc=dc, mh=mh: E.copy(
                                out=t_rx[dc][:, mh * 128:(mh + 1) * 128],
                                in_=pxT[:]),
                                reads=["pxT"], writes=[("mrx", dc)])
                            p._record(p.wr[("mrx", dc)], [], ["pxT"])
                    for do in range(NDC0):
                        kq = do % 2
                        for kc in range(NDC0):
                            p.op("pe", lambda E, do=do, kc=kc, kq=kq:
                                 E.matmul(
                                     out=pex[kq][:],
                                     lhsT=t_we[:, kc * D0 + do * 128:
                                               kc * D0 + (do + 1) * 128],
                                     rhs=t_rx[kc][:], start=(kc == 0),
                                     stop=(kc == NDC0 - 1)),
                                 reads=[("we", kc), ("mrx", kc)],
                                 writes=([("pex", kq)]
                                         if kc == NDC0 - 1 else []),
                                 extra=(p.predeps(("pex", kq))
                                        if kc == 0 else []))
                        p.op("act", lambda E, do=do, kq=kq: E.copy(
                            out=t_eo[do][:], in_=pex[kq][:]),
                            reads=[("pex", kq)], writes=[("meo", do)])
                        p._record(p.wr[("meo", do)], [], [("pex", kq)])
                    for do in range(NDC0):
                        for mh in range(2):
                            p.op("pe", lambda E, do=do, mh=mh: E.transpose(
                                out=pvT[:],
                                in_=t_eo[do][:, mh * 128:(mh + 1) * 128],
                                identity=t_id_f[:]),
                                reads=[("meo", do), "idf"], writes=["pvT"])
                            p.op("act", lambda E, do=do, mh=mh, e=e: E.copy(
                                out=t_vals[e][:, mh * D0 + do * 128:
                                              mh * D0 + (do + 1) * 128],
                                in_=pvT[:]),
                                reads=["pvT"], writes=[("vals", e)])
                            p._record(p.wr[("vals", e)], [], ["pvT"])
                    p.dma("sp", t_pwr[:],
                          PWD[b, e:e + 1, :].broadcast_to([128, L]),
                          reads=[("PWD", b)], writes=["pwr"])
                    for mc in range(2):
                        p.op("dve", lambda E, e=e, mc=mc: E.tensor_scalar(
                            out=t_D[e][:, mc * L:(mc + 1) * L],
                            in0=t_pwr[:], scalar1=t_mioc[:, mc:mc + 1],
                            scalar2=None, op0=ALU.is_equal),
                            reads=["pwr", "mmioc"], writes=[("D", e)])
                for lc in range(16):
                    for dh in range(2):
                        pq = dh
                        for e in range(NEXP):
                            for mc in range(2):
                                first = (e == 0 and mc == 0)
                                last = (e == NEXP - 1 and mc == 1)
                                p.op("pe", lambda E, lc=lc, dh=dh, e=e,
                                     mc=mc, pq=pq, first=first, last=last:
                                     E.matmul(
                                         out=pgx[pq][:],
                                         lhsT=t_D[e][:, mc * L + lc * 128:
                                                     mc * L + (lc + 1) * 128],
                                         rhs=t_vals[e][:, mc * D0 + dh * 512:
                                                       mc * D0 + (dh + 1) * 512],
                                         start=first, stop=last),
                                     reads=[("D", e), ("vals", e)],
                                     writes=([("pgx", pq)] if last else []),
                                     extra=(p.predeps(("pgx", pq))
                                            if first else []))
                        p.op("dve", lambda E, b=b, lc=lc, dh=dh, pq=pq:
                             E.tensor_scalar(
                                 out=t_mo[dh][:], in0=pgx[pq][:],
                                 scalar1=t_gvt[b][:, lc:lc + 1],
                                 scalar2=None, op0=ALU.mult),
                             reads=[("pgx", pq), ("gvt", b)],
                             writes=[("mo", dh)])
                        p._record(p.wr[("mo", dh)], [], [("pgx", pq)])
                        p.dma("act", MOTM[b, lc * 128:(lc + 1) * 128,
                                          dh * 512:(dh + 1) * 512],
                              t_mo[dh][:], reads=[("mo", dh)],
                              writes=[("MOTM", b, lc, dh)])
            p.barrier()
    def emit_s1():
        with contextlib.ExitStack() as st:
            t_op = sb("s1op", [128, NDC1 * D0], BF16, st)
            stg = sb("s1stg", [128, 1024], F32, st)
            t_dl = [sb(f"s1dl{d}", [128, PC1], F32, st) for d in range(2)]
            t_u = [sb(f"s1u{d}", [128, PC1], F32, st) for d in range(2)]
            t_uk = [sb(f"s1uk{d}", [128, PC1], F32, st) for d in range(2)]
            t_du = [sb(f"s1du{d}", [128, PC1], F32, st) for d in range(2)]
            t_y2 = [sb(f"s1y2{d}", [128, PC1], F32, st) for d in range(2)]
            t_yb = [sb(f"s1yb{d}", [128, PC1], BF16, st)
                    for d in range(NDC1)]
            t_zs = [sb(f"s1z{i}", [128, PC1], F32, st) for i in range(2)]
            t_dAp1 = [sb(f"s1dAp{i}", [128, NST * PC1], F32, st)
                      for i in range(2)]
            t_Bup1 = [sb("s1Bup0", [128, NST * PC1], F32, st)]
            t_hp1 = [sb(f"s1hp{i}", [128, NST * PC1], F32, st)
                     for i in range(2)]
            t_hbf = sb("s1hbf", [128, NST * PC1], BF16, st)
            t_hT = sb("s1hT", [128, NST * PC1], BF16, st)
            t_dg = sb("s1dg", [128, 2 * NST * 128], BF16, st)
            t_cb = [sb(f"s1cb{i}", [128, 128], F32, st) for i in range(2)]
            t_Bp1 = sb("s1Bp", [128, NST * PC1], F32, st)
            t_sg1 = sb("s1sg", [128, NST], F32, st)
            t_at = [sb(f"s1at{i}", [128, PC1], F32, st)
                    for i in range(NDC0)]
            t_x1 = [sb("s1x10", [128, D0], F32, st)]
            t_o = [sb("s1o0", [128, D0], F32, st)]
            pop = [ps(f"s1pop{i}", [128, PC1], F32, st) for i in range(2)]
            ptm = [ps("s1ptm0", [128, D0], F32, st)]
            phT = [ps(f"s1phT{i}", [128, 1024], BF16, st) for i in range(2)]
            pyf = [ps(f"s1pyf{i}", [128, PC1], F32, st) for i in range(2)]

            for kc in range(NDC1):
                load_bf(t_op[:, kc * D0:(kc + 1) * D0],
                        opT[kc * 128:(kc + 1) * 128, :], ("op", kc), st, stg)
            p.op("dve", lambda E: E.memset(t_st1[:], 0.0), writes=["st1"])
            for b in range(B_LOC):
                if b > 0:
                    p.op("dve", lambda E: E.memset(t_st1[:], 0.0),
                         writes=["st1"])
                for pc in range(L // PC1):
                    c0 = b * L + pc * PC1
                    lc = pc * PC1
                    p.dma("sp", t_Bp1[:],
                          BCR[1, b, 0:16, lc:lc + PC1].rearrange(
                              "(one n) l -> one n l", one=1).broadcast_to(
                              [128, NST, PC1]),
                          writes=["Bp1"])
                    for n in range(NST):
                        for lb in range(PC1 // 128):
                            j = n * 2 + lb
                            kc2 = j % 2
                            p.dma("sp", t_cb[kc2][:],
                                  BCR[1, b, 16 + n:17 + n,
                                      lc + lb * 128:lc + (lb + 1) * 128]
                                  .broadcast_to([128, 128]),
                                  writes=[("cb", kc2)])
                            p.op("dve", lambda E, j=j, kc2=kc2:
                                 E.tensor_mul(
                                     out=t_dg[:, j * 128:(j + 1) * 128],
                                     in0=t_id_f[:], in1=t_cb[kc2][:]),
                                 reads=[("cb", kc2), "idf"],
                                 writes=[("dg", j)])
                    WP1 = NST * PC1

                    def s1_A(dc):
                        kq = dc % 2
                        kh = dc % 2
                        p.dma("sp", t_dl[kq][:],
                              D1[dc * 128:(dc + 1) * 128, c0:c0 + PC1],
                              writes=[("dl", kq)])
                        p.dma("sp", t_u[kq][:],
                              U1[dc * 128:(dc + 1) * 128, c0:c0 + PC1],
                              writes=[("u", kq)])
                        p.op("dve", lambda E, kq=kq: E.tensor_mul(
                            out=t_du[kq][:], in0=t_dl[kq][:],
                            in1=t_u[kq][:]),
                            reads=[("dl", kq), ("u", kq)],
                            writes=[("du", kq)])
                        for n in range(NST):
                            p.op("act", lambda E, dc=dc, n=n, kq=kq:
                                 E.activation(
                                     out=t_dAp1[kq][:, n * PC1:(n + 1) * PC1],
                                     in_=t_dl[kq][:], func=AF.Exp,
                                     scale=t_A1[:, dc * NST + n:
                                                dc * NST + n + 1]),
                                 reads=[("dl", kq), "A1"],
                                 writes=[("dAp1", kq, n)])
                        dA_all = [("dAp1", kq, n) for n in range(NST)]
                        p.op("dve", lambda E, kq=kq: E.tensor_tensor(
                            out=t_Bup1[0][:].rearrange(
                                "p (n l) -> p n l", n=NST),
                            in0=t_du[kq][:].rearrange(
                                "p (one l) -> p one l", one=1).broadcast_to(
                                [128, NST, PC1]),
                            in1=t_Bp1[:].rearrange("p (n l) -> p n l", n=NST),
                            op=ALU.mult),
                            reads=[("du", kq), "Bp1"], writes=[("Bup1", 0)])
                        p.op("dve", lambda E, dc=dc, kq=kq: E.tensor_tensor(
                            out=t_sg1[:], in0=t_dAp1[kq][:, 0:WP1:PC1],
                            in1=t_st1[:, dc * NST:(dc + 1) * NST],
                            op=ALU.mult),
                            reads=dA_all + ["st1"], writes=["sg1"])
                        p.op("dve", lambda E: E.tensor_tensor(
                            out=t_Bup1[0][:, 0:WP1:PC1],
                            in0=t_Bup1[0][:, 0:WP1:PC1], in1=t_sg1[:],
                            op=ALU.add),
                            reads=["sg1", ("Bup1", 0)],
                            writes=[("Bup1", 0)])
                        p.op("dve", lambda E, kq=kq: E.memset(
                            t_dAp1[kq][:, 0:WP1:PC1], 0.0),
                            reads=[], writes=dA_all)
                        p.op("dve", lambda E, kq=kq, kh=kh:
                             E.tensor_tensor_scan(
                                 out=t_hp1[kh][:], data0=t_dAp1[kq][:],
                                 data1=t_Bup1[0][:], initial=0.0,
                                 op0=ALU.mult, op1=ALU.add),
                             reads=dA_all + [("Bup1", 0)],
                             writes=[("hp1", kh)])
                        p.op("dve", lambda E, dc=dc, kh=kh: E.tensor_copy(
                            out=t_st1[:, dc * NST:(dc + 1) * NST],
                            in_=t_hp1[kh][:, PC1 - 1:WP1:PC1]),
                            reads=[("hp1", kh)], writes=["st1"])

                    def s1_B(dc):
                        kh = dc % 2
                        kz = dc % 2
                        kp = dc % 2
                        p.op("dve", lambda E, kh=kh: E.tensor_copy(
                            out=t_hbf[:], in_=t_hp1[kh][:]),
                            reads=[("hp1", kh)], writes=["hbf"])
                        for g in range(4):
                            kt = g % 2
                            for jj in range(8):
                                j = g * 8 + jj
                                n, lb = j // 2, j % 2
                                p.op("pe", lambda E, n=n, lb=lb, jj=jj,
                                     kt=kt: E.transpose(
                                         out=phT[kt][:, jj * 128:
                                                     (jj + 1) * 128],
                                         in_=t_hbf[:, n * PC1 + lb * 128:
                                                   n * PC1 + (lb + 1) * 128],
                                         identity=t_id_bf[:]),
                                     reads=["hbf", "idbf"],
                                     writes=[("phT", kt, jj)])
                            p.op("act", lambda E, g=g, kt=kt: E.copy(
                                out=t_hT[:, g * 1024:(g + 1) * 1024],
                                in_=phT[kt][:]),
                                reads=[("phT", kt, jj) for jj in range(8)],
                                writes=[("hT", g)])
                            p._record(p.wr[("hT", g)], [],
                                      [("phT", kt, jj) for jj in range(8)])
                        for lb in range(PC1 // 128):
                            for n in range(NST):
                                j = n * 2 + lb
                                p.op("pe", lambda E, j=j, lb=lb, kp=kp,
                                     n=n: E.matmul(
                                         out=pyf[kp][:, lb * 128:
                                                     (lb + 1) * 128],
                                         lhsT=t_hT[:, j * 128:(j + 1) * 128],
                                         rhs=t_dg[:, j * 128:(j + 1) * 128],
                                         start=(n == 0), stop=(n == NST - 1)),
                                     reads=[("hT", j // 8), ("dg", j)],
                                     writes=([("pyf", kp, lb)]
                                             if n == NST - 1 else []),
                                     extra=(p.predeps(("pyf", kp, lb))
                                            if n == 0 else []))
                        p.dma("sp", t_uk[kz][:],
                              U1[dc * 128:(dc + 1) * 128, c0:c0 + PC1],
                              writes=[("uk", kz)])
                        p.op("dve", lambda E, dc=dc, kz=kz, kp=kp:
                             E.scalar_tensor_tensor(
                                 out=t_y2[kz][:], in0=t_uk[kz][:],
                                 scalar=t_dd1[:, dc:dc + 1],
                                 in1=pyf[kp][:],
                                 op0=ALU.mult, op1=ALU.add),
                             reads=[("uk", kz), "dd1",
                                    ("pyf", kp, 0), ("pyf", kp, 1)],
                             writes=[("y2", kz)])
                        p._record(p.wr[("y2", kz)], [],
                                  [("pyf", kp, 0), ("pyf", kp, 1)])
                        p.dma("sp", t_zs[kz][:],
                              ZS[dc * 128:(dc + 1) * 128, c0:c0 + PC1],
                              writes=[("zs", kz)])
                        p.op("dve", lambda E, dc=dc, kz=kz: E.tensor_mul(
                            out=t_yb[dc][:], in0=t_y2[kz][:],
                            in1=t_zs[kz][:]),
                            reads=[("y2", kz), ("zs", kz)],
                            writes=[("yb", dc)])

                    for dc in range(NDC1):
                        s1_A(dc)
                        if dc >= 3:
                            s1_B(dc - 3)
                    for dc in range(NDC1 - 3, NDC1):
                        s1_B(dc)

                    for do in range(NDC0):
                        kq = do % 2
                        for kc in range(NDC1):
                            p.op("pe", lambda E, do=do, kc=kc, kq=kq:
                                 E.matmul(
                                     out=pop[kq][:],
                                     lhsT=t_op[:, kc * D0 + do * 128:
                                               kc * D0 + (do + 1) * 128],
                                     rhs=t_yb[kc][:], start=(kc == 0),
                                     stop=(kc == NDC1 - 1)),
                                 reads=[("op", kc), ("yb", kc)],
                                 writes=([("pop", kq)]
                                         if kc == NDC1 - 1 else []),
                                 extra=(p.predeps(("pop", kq))
                                        if kc == 0 else []))
                        p.op("act", lambda E, do=do, kq=kq: E.copy(
                            out=t_at[do][:], in_=pop[kq][:]),
                            reads=[("pop", kq)], writes=[("at", do)])
                        p._record(p.wr[("at", do)], [], [("pop", kq)])
                    for tb in range(PC1 // 128):
                        kq = tb % 2
                        for do in range(NDC0):
                            p.op("pe", lambda E, do=do, tb=tb:
                                 E.transpose(
                                     out=ptm[0][:, do * 128:(do + 1) * 128],
                                     in_=t_at[do][:, tb * 128:(tb + 1) * 128],
                                     identity=t_id_f[:]),
                                 reads=[("at", do), "idf"],
                                 writes=[("ptm", do)])
                        r0 = pc * PC1 + tb * 128
                        p.dma("sp", t_x1[0][:], X1TM[b, r0:r0 + 128, :],
                              writes=[("x1", 0)])
                        p.dma("sp", t_o[0][:], MOTM[b, r0:r0 + 128, :],
                              writes=[("otm", 0)])
                        p.op("dve", lambda E: E.tensor_add(
                            out=t_o[0][:], in0=ptm[0][:], in1=t_o[0][:]),
                            reads=[("otm", 0)] + [("ptm", d)
                                                  for d in range(NDC0)],
                            writes=[("otm", 0)])
                        p._record(p.wr[("otm", 0)], [],
                                  [("ptm", d) for d in range(NDC0)])
                        p.op("dve", lambda E: E.tensor_add(
                            out=t_o[0][:], in0=t_o[0][:], in1=t_x1[0][:]),
                            reads=[("x1", 0), ("otm", 0)],
                            writes=[("otm", 0)])
                        p.dma("sp", out[b, r0:r0 + 128, :], t_o[0][:],
                              reads=[("otm", 0)], writes=[("out", b, r0)])
            p.barrier()

    if stage != "prep":
        emit_ln(0)
    if stage in ("ffn0", "l0", "x1", "i1", "moe", "s1", "full"):
        emit_ffn0()
    if stage in ("l0", "x1", "i1", "moe", "s1", "full"):
        emit_l0()
    if stage in ("i1", "moe", "s1", "full"):
        emit_ln(1)
        emit_i1()
    if stage in ("moe", "s1", "full"):
        emit_moe()
    if stage in ("s1", "full"):
        emit_s1()

    if stage not in ("s1", "full"):
        zo = sb("zout", [128, D0])
        p.op("dve", lambda E: E.memset(zo[:], 0.0), writes=["zout"])
        for b in range(B_LOC):
            for i in range(L // 128):
                p.dma("sp", out[b, i * 128:(i + 1) * 128, :], zo[:],
                      reads=["zout"], writes=[("outz", b, i)])

    if dbg is not None:
        src = dbg_src[debug_out]
        p.barrier()
        flat = int(np.prod(src.shape))
        sf = src.rearrange(
            " ".join(f"a{i}" for i in range(len(src.shape))) + " -> ("
            + " ".join(f"a{i}" for i in range(len(src.shape))) + ")")
        df = dbg.rearrange(
            " ".join(f"a{i}" for i in range(len(src.shape))) + " -> ("
            + " ".join(f"a{i}" for i in range(len(src.shape))) + ")")
        tcp = sb("dbgcp", [128, 4096], src.dtype)
        CH = 128 * 4096
        off = 0
        i = 0
        while off < flat:
            n = min(CH, flat - off)
            rows = n // 4096
            assert rows * 4096 == n, f"debug copy misalign {flat}"
            p.dma("sp", tcp[0:rows, :],
                  sf[off:off + n].rearrange("(p f) -> p f", f=4096),
                  writes=["dbgcp"])
            p.dma("act", df[off:off + n].rearrange("(p f) -> p f", f=4096),
                  tcp[0:rows, :], reads=["dbgcp"], writes=[("dbg", i)])
            off += n
            i += 1

    p.barrier()
    with nc.Block() as block:
        p.emit(block)
    top.close()
    return nc


def _host_inputs(w):
    tr = lambda a: np.ascontiguousarray(np.transpose(np.asarray(a)))
    cols = lambda v, ndc: np.ascontiguousarray(
        np.asarray(v).reshape(ndc, 128).T)
    d = {}
    d["cw0"] = np.ascontiguousarray(
        np.asarray(w["L0_attn_conv_w"])[:, 0, :].reshape(NDC0, 128, 4)
        .transpose(1, 0, 2).reshape(128, NDC0 * 4))
    d["cb0"] = cols(w["L0_attn_conv_b"], NDC0)
    d["xp0T"] = np.ascontiguousarray(tr(w["L0_attn_x_proj_w"]))
    d["dt0T"] = tr(w["L0_attn_dt_proj_w"])
    d["db0"] = cols(w["L0_attn_dt_proj_b"], NDC0)
    d["al0"] = np.asarray(w["L0_attn_A_log"])
    d["dsk0"] = cols(w["L0_attn_D_skip"], NDC0)
    d["wgT"] = tr(w["L0_ffn_w_gate"])
    d["wuT"] = tr(w["L0_ffn_w_up"])
    d["wdT"] = tr(w["L0_ffn_w_down"])
    d["ipT"] = tr(w["L1_attn_in_proj_w"])
    d["cw1"] = np.ascontiguousarray(
        np.asarray(w["L1_attn_conv_w"])[:, 0, :].reshape(NDC1, 128, 4)
        .transpose(1, 0, 2).reshape(128, NDC1 * 4))
    d["cb1"] = cols(w["L1_attn_conv_b"], NDC1)
    d["xp1T"] = np.ascontiguousarray(tr(w["L1_attn_x_proj_w"]))
    d["dt1T"] = tr(w["L1_attn_dt_proj_w"])
    d["db1"] = cols(w["L1_attn_dt_proj_b"], NDC1)
    d["al1"] = np.asarray(w["L1_attn_A_log"])
    d["dd1"] = cols(w["L1_attn_D"], NDC1)
    d["opT"] = tr(w["L1_attn_out_proj_w"])
    d["gw"] = np.asarray(w["L1_ffn_gate_w"])
    d["exps"] = np.asarray(w["L1_ffn_experts"])
    import ml_dtypes
    d["id_bf"] = np.eye(128, dtype=ml_dtypes.bfloat16)
    d["id_f32"] = np.eye(128, dtype=np.float32)
    d["lt8"] = np.tril(np.ones((NEXP, NEXP), np.float32), -1).astype(
        ml_dtypes.bfloat16)
    d["miota"] = np.arange(MTOP, dtype=np.float32).reshape(1, MTOP)
    d["miocol"] = np.arange(MTOP, dtype=np.float32).reshape(
        2, 128).T.copy()
    return d


def kernel(**inputs):
    x = np.asarray(inputs["L0_attn_u"])
    base = _host_inputs(inputs)
    nc = build_program("full")
    in_maps = []
    for c in range(8):
        m = dict(base)
        m["x_in"] = np.ascontiguousarray(x[2 * c:2 * c + 2])
        in_maps.append(m)
    res = run_bass_kernel_spmd(nc, in_maps, core_ids=list(range(8)))
    return np.concatenate([r["out"] for r in res.results], axis=0)


# version 16
# speedup vs baseline: 1.0592x; 1.0135x over previous
import contextlib

import numpy as np

import concourse.bass as bass
import concourse.mybir as mybir
from concourse.bass import IndirectOffsetOnAxis
from concourse.bass_utils import run_bass_kernel_spmd

F32 = mybir.dt.float32
BF16 = mybir.dt.bfloat16
I32 = mybir.dt.int32
AF = mybir.ActivationFunctionType
ALU = mybir.AluOpType

LN_TP = True
L0_DEBUG = False
B_LOC = 2
L = 2048
T = B_LOC * L
D0 = 1024
DI = 2048
NST = 16
RNK = 64
HID = 3072
NEXP = 8
MTOP = 256
PC0 = 256
PCF = 512
PC1 = 256
NDC0 = D0 // 128
NDC1 = DI // 128
INV127 = float(np.float32(1.0) / np.float32(127.0))

ENGS = ("sp", "pe", "act", "dve", "pool")


class Prog:

    DMA_POOL = {"sp": 14, "act": 8, "pool": 8}

    def __init__(self, nc):
        self.nc = nc
        self.q = {e: [] for e in ENGS}
        self.cnt = dict.fromkeys(ENGS, 0)
        self.dcnt = {}
        self.rr = dict.fromkeys(ENGS, 0)
        self.wm = {}
        self.sems = {}
        self.wr = {}
        self.rd = {}
        self.trace_log = []

    def make_sems(self, stack):
        for e in ENGS:
            self.sems["q" + e] = stack.enter_context(self.nc.semaphore("q" + e))
            for i in range(self.DMA_POOL.get(e, 0)):
                self.sems[f"d{e}{i}"] = stack.enter_context(
                    self.nc.semaphore(f"d{e}{i}"))
                self.dcnt[(e, i)] = 0

    def _wait(self, eng, sem_name, val):
        if val <= 0:
            return
        key = (eng, sem_name)
        if self.wm.get(key, 0) >= val:
            return
        self.wm[key] = val
        sem = self.sems[sem_name]
        self.q[eng].append(lambda E, s=sem, v=val: E.wait_ge(s, v))

    def _deps(self, eng, toks):
        best = {}
        for t in toks:
            if t is None:
                continue
            s, v = t
            if best.get(s, 0) >= v:
                continue
            best[s] = v
        for s, v in best.items():
            self._wait(eng, s, v)

    def _collect(self, reads, writes):
        toks = []
        for k in reads:
            t = self.wr.get(k)
            if t is not None:
                toks.append(t)
        for k in writes:
            t = self.wr.get(k)
            if t is not None:
                toks.append(t)
            toks.extend(self.rd.get(k, ()))
        return toks

    def _record(self, tok, reads, writes):
        for k in reads:
            self.rd.setdefault(k, []).append(tok)
        for k in writes:
            self.wr[k] = tok
            self.rd[k] = []

    def predeps(self, key):
        t = [self.wr[key]] if key in self.wr else []
        return t + list(self.rd.get(key, ()))

    def op(self, eng, fn, reads=(), writes=(), extra=(), tag=None):
        if tag is None:
            tag = getattr(fn, "__qualname__", "?")
        self.trace_log.append((eng, tag))
        self._deps(eng, self._collect(reads, writes) + list(extra))
        self.cnt[eng] += 1
        sem = self.sems["q" + eng]

        def run(E, fn=fn, sem=sem):
            fn(E).then_inc(sem, 1)
        self.q[eng].append(run)
        tok = ("q" + eng, self.cnt[eng])
        self._record(tok, reads, writes)
        return tok

    def _dma_slot(self, eng):
        i = self.rr[eng]
        self.rr[eng] = (i + 1) % self.DMA_POOL[eng]
        sem_name = f"d{eng}{i}"
        prev = self.dcnt[(eng, i)]
        self._wait(eng, sem_name, prev)
        self.dcnt[(eng, i)] = prev + 16
        return sem_name, prev + 16

    def dma(self, eng, out, in_, reads=(), writes=(), extra=(), **kw):
        self._deps(eng, self._collect(reads, writes) + list(extra))
        sem_name, val = self._dma_slot(eng)
        sem = self.sems[sem_name]

        def run(E, out=out, in_=in_, sem=sem, kw=kw):
            E.dma_start(out=out, in_=in_, **kw).then_inc(sem, 16)
        self.q[eng].append(run)
        tok = (sem_name, val)
        self._record(tok, reads, writes)
        return tok

    def idma(self, out, out_off, in_, in_off, reads=(), writes=(), extra=(),
             bounds_check=None):
        eng = "pool"
        self._deps(eng, self._collect(reads, writes) + list(extra))
        sem_name, val = self._dma_slot(eng)
        sem = self.sems[sem_name]

        def run(E, sem=sem):
            E.indirect_dma_start(out=out, out_offset=out_off, in_=in_,
                                 in_offset=in_off,
                                 bounds_check=bounds_check).then_inc(sem, 16)
        self.q[eng].append(run)
        tok = (sem_name, val)
        self._record(tok, reads, writes)
        return tok

    def barrier(self):
        toks = []
        for e in ENGS:
            toks.append(("q" + e, self.cnt[e]))
            for i in range(self.DMA_POOL.get(e, 0)):
                toks.append((f"d{e}{i}", self.dcnt[(e, i)]))
        for e in ENGS:
            self._deps(e, toks)

    def emit(self, block):
        secs = {"sp": block.sync, "pe": block.tensor, "act": block.scalar,
                "dve": block.vector, "pool": block.gpsimd}
        for e in ENGS:
            ops = self.q[e]

            def body(E, ops=ops):
                for f in ops:
                    f(E)
            secs[e](body)


def build_program(stage="full", debug_out=None):
    nc = bass.Bass("TRN2", target_bir_lowering=False, debug=False,
                   detect_race_conditions=False)
    top = contextlib.ExitStack()

    def din(name, shape, dtype=F32):
        return nc.dram_tensor(name, shape, dtype, kind="ExternalInput").ap()

    x_in = din("x_in", [B_LOC, L, D0])
    cw0 = din("cw0", [128, 4 * NDC0]); cb0 = din("cb0", [128, NDC0])
    xp0T = din("xp0T", [D0, 96])
    dt0T = din("dt0T", [RNK, D0]); db0 = din("db0", [128, NDC0])
    al0 = din("al0", [D0, NST]); dsk0 = din("dsk0", [128, NDC0])
    wgT = din("wgT", [D0, HID], I32)
    wuT = din("wuT", [D0, HID], I32)
    wdT = din("wdT", [HID, D0], I32)
    ipT = din("ipT", [D0, 2 * DI])
    cw1 = din("cw1", [128, 4 * NDC1]); cb1 = din("cb1", [128, NDC1])
    xp1T = din("xp1T", [DI, 96])
    dt1T = din("dt1T", [RNK, DI]); db1 = din("db1", [128, NDC1])
    al1 = din("al1", [DI, NST]); dd1 = din("dd1", [128, NDC1])
    opT = din("opT", [DI, D0])
    gw = din("gw", [D0, NEXP])
    exps = din("exps", [NEXP, D0, D0])
    id_bf = din("id_bf", [128, 128], BF16)
    id_f32 = din("id_f32", [128, 128])
    lt8 = din("lt8", [NEXP, NEXP], BF16)
    miota = din("miota", [1, MTOP])
    miocol = din("miocol", [128, 2])

    out = nc.dram_tensor("out", [B_LOC, L, D0], F32, kind="ExternalOutput").ap()

    def scr(name, shape, dtype=F32):
        return nc.dram_tensor(name, shape, dtype, kind="Internal").ap()

    XN32 = scr("XN32", [D0, T])
    XNBF = scr("XNBF", [D0, T], BF16)
    XNTM = scr("XNTM", [B_LOC, L, D0], BF16)
    X1TM = scr("X1TM", [B_LOC, L, D0])
    FFN0 = scr("FFN0", [D0, T])
    BCR = scr("BCR", [2, B_LOC, 32, L])
    WGBF = scr("WGBF", [D0, HID], BF16)
    WUBF = scr("WUBF", [D0, HID], BF16)
    U1 = scr("U1", [DI, T])
    D1 = scr("D1", [DI, T])
    ZS = scr("ZS", [DI, T])
    MOTM = scr("MOTM", [B_LOC, L, D0])
    GVD = scr("GVD", [1, L])
    OPBF = scr("OPBF", [8 * 128, D0], BF16)
    PWD = scr("PWD", [B_LOC, NEXP, L])

    dbg_src = {"XN32": XN32, "XNBF": XNBF, "X1TM": X1TM, "FFN0": FFN0,
               "U1": U1, "D1": D1, "ZS": ZS, "MOTM": MOTM, "XNTM": XNTM,
               "BCR": BCR, "PWD": PWD}
    dbg = None
    if debug_out is not None:
        src = dbg_src[debug_out]
        dbg = nc.dram_tensor("dbg", list(src.shape), src.dtype,
                             kind="ExternalOutput").ap()

    p = Prog(nc)
    globals()["LAST_PROG"] = p
    p.make_sems(top)

    def sb(name, shape, dtype=F32, stack=top):
        return stack.enter_context(nc.sbuf_tensor(name, shape, dtype))

    def ps(name, shape, dtype=F32, stack=top):
        return stack.enter_context(nc.psum_tensor(name, shape, dtype))

    t_id_bf = sb("t_id_bf", [128, 128], BF16)
    t_id_f = sb("t_id_f", [128, 128])
    t_A0 = sb("t_A0", [128, NST * NDC0])
    t_iA0 = sb("t_iA0", [128, NST * NDC0])
    t_niA0 = sb("t_niA0", [128, NST * NDC0])
    t_A1 = sb("t_A1", [128, NST * NDC1])
    t_cw0 = sb("t_cw0", [128, 4 * NDC0])
    t_cw1 = sb("t_cw1", [128, 4 * NDC1])
    t_cb0 = sb("t_cb0", [128, NDC0])
    t_cb1 = sb("t_cb1", [128, NDC1])
    t_db0 = sb("t_db0", [128, NDC0])
    t_db1 = sb("t_db1", [128, NDC1])
    t_dsk = sb("t_dsk", [128, NDC0])
    t_dd1 = sb("t_dd1", [128, NDC1])
    t_st0 = sb("t_st0", [128, NST * NDC0])
    t_st1 = sb("t_st1", [128, NST * NDC1])

    for i, v in enumerate([1e-5, -0.5, -1.0, 1.0, INV127]):
        ct = sb(f"cst{i}", [128, 1])
        p.op("pool", lambda E, ct=ct, v=v: E.memset(ct[:], v),
             writes=[("cst", v)])
        nc.const_aps.register(F32, v, ct[:, 0:1])

    p.dma("sp", t_id_bf[:], id_bf, writes=["idbf"])
    p.dma("sp", t_id_f[:], id_f32, writes=["idf"])
    for tile, src, key in ((t_cw0, cw0, "cw0"), (t_cw1, cw1, "cw1"),
                           (t_cb0, cb0, "cb0"), (t_cb1, cb1, "cb1"),
                           (t_db0, db0, "db0"), (t_db1, db1, "db1"),
                           (t_dsk, dsk0, "dsk"), (t_dd1, dd1, "dd1")):
        p.dma("sp", tile[:, :], src, writes=[key])

    def prep_A(tA, alog, ndc, key, neg_exp_of_neg=False):
        for dc in range(ndc):
            sl = tA[:, dc * NST:(dc + 1) * NST]
            p.dma("sp", sl, alog[dc * 128:(dc + 1) * 128, :],
                  writes=[(key, dc)])
            p.op("act", lambda E, sl=sl: E.activation(
                out=sl, in_=sl, func=AF.Exp,
                scale=(-1.0 if neg_exp_of_neg else 1.0)),
                reads=[(key, dc)], writes=[(key, dc)])
            p.op("dve", lambda E, sl=sl: E.tensor_scalar_mul(
                out=sl, in0=sl, scalar1=-1.0),
                reads=[(key, dc)], writes=[(key, dc)])
    prep_A(t_A0, al0, NDC0, "A0")
    prep_A(t_iA0, al0, NDC0, "iA0", neg_exp_of_neg=True)
    for dc in range(NDC0):
        sl = t_niA0[:, dc * NST:(dc + 1) * NST]
        p.dma("sp", sl, al0[dc * 128:(dc + 1) * 128, :],
              writes=[("niA0", dc)])
        p.op("act", lambda E, sl=sl: E.activation(
            out=sl, in_=sl, func=AF.Exp, scale=-1.0),
            reads=[("niA0", dc)], writes=[("niA0", dc), "niA0"])
    prep_A(t_A1, al1, NDC1, "A1")

    with contextlib.ExitStack() as st:
        c_i = sb("cvt_i", [128, HID], I32, st)
        c_o = sb("cvt_o", [128, HID], BF16, st)
        for srcT, dstT, key in ((wgT, WGBF, "wg"), (wuT, WUBF, "wu")):
            for dc in range(NDC0):
                p.dma("sp", c_i[:], srcT[dc * 128:(dc + 1) * 128, :],
                      writes=["cvt_i"])
                p.op("dve", lambda E: E.tensor_copy(out=c_o[:], in_=c_i[:]),
                     reads=["cvt_i"], writes=["cvt_o"])
                p.dma("sp", dstT[dc * 128:(dc + 1) * 128, :], c_o[:],
                      reads=["cvt_o"], writes=[(key, dc)])
        p.barrier()

    def load_bf(dst_sl, src_ap, key, st, stage_tile, rows=128):
        skey = ("stg", id(stage_tile))
        p.dma("sp", stage_tile[0:rows, :src_ap.shape[-1]], src_ap,
              writes=[skey])
        p.op("dve", lambda E: E.tensor_copy(
            out=dst_sl, in_=stage_tile[0:rows, :src_ap.shape[-1]]),
            reads=[skey], writes=[key])

    def emit_ln(layer):
        src_tm = x_in if layer == 0 else X1TM
        with contextlib.ExitStack() as st:
            t_x = [sb(f"ln{layer}x{i}", [128, D0], F32, st) for i in range(3)]
            t_stt = [sb(f"ln{layer}st{i}", [128, 12], F32, st) for i in range(3)]
            t_mv = [sb(f"ln{layer}mv{i}", [128, 2], F32, st) for i in range(3)]
            t_rs = [sb(f"ln{layer}rs{i}", [128, 1], F32, st) for i in range(3)]
            t_nm = [sb(f"ln{layer}nm{i}", [128, 1], F32, st) for i in range(3)]
            t_xn = [sb(f"ln{layer}xn{i}", [128, D0], F32, st) for i in range(3)]
            t_xnb = [sb(f"ln{layer}xb{i}", [128, D0], BF16, st)
                     for i in range(3)]
            pp = [ps(f"ln{layer}ps{q}", [128, 128], F32, st)
                  for q in range(3)] if layer == 0 else []
            ppb = [ps(f"ln{layer}pb{q}", [128, 128], BF16, st)
                   for q in range(3)]
            t_evb = [sb(f"ln{layer}eb{q}", [128, 128], BF16, st)
                     for q in range(3)]
            t_evf = [sb(f"ln{layer}ef{q}", [128, 128], F32, st)
                     for q in range(3)]
            for tt in range(T // 128):
                k = tt % 3
                b, l0 = divmod(tt * 128, L)
                p.dma("sp", t_x[k][:], src_tm[b, l0:l0 + 128, :],
                      writes=[("lx", k)])
                p.op("dve", lambda E, k=k: E.bn_stats(
                    t_stt[k][:, 0:6], t_x[k][:, 0:512]),
                    reads=[("lx", k)], writes=[("lst", k)])
                p.op("dve", lambda E, k=k: E.bn_stats(
                    t_stt[k][:, 6:12], t_x[k][:, 512:1024]),
                    reads=[("lx", k)], writes=[("lst2", k)])
                p.op("dve", lambda E, k=k: E.bn_aggr(
                    t_mv[k][:], t_stt[k][:]),
                    reads=[("lst", k), ("lst2", k)], writes=[("lmv", k)])
                p.op("act", lambda E, k=k: E.activation(
                    out=t_rs[k][:], in_=t_mv[k][:, 1:2], func=AF.Ln,
                    bias=1e-5), reads=[("lmv", k)], writes=[("lrs", k)])
                p.op("act", lambda E, k=k: E.activation(
                    out=t_rs[k][:], in_=t_rs[k][:], func=AF.Exp, scale=-0.5),
                    reads=[("lrs", k)], writes=[("lrs", k)])
                p.op("dve", lambda E, k=k: E.tensor_scalar_mul(
                    out=t_nm[k][:], in0=t_mv[k][:, 0:1], scalar1=-1.0),
                    reads=[("lmv", k)], writes=[("lnm", k)])
                p.op("dve", lambda E, k=k: E.tensor_scalar(
                    out=t_xn[k][:], in0=t_x[k][:], scalar1=t_nm[k][:, 0:1],
                    scalar2=t_rs[k][:, 0:1], op0=ALU.add, op1=ALU.mult),
                    reads=[("lx", k), ("lnm", k), ("lrs", k)],
                    writes=[("lxn", k)])
                p.op("act", lambda E, k=k: E.copy(
                    out=t_xnb[k][:], in_=t_xn[k][:]),
                    reads=[("lxn", k)], writes=[("lxb", k)])
                if layer == 1:
                    p.dma("sp", XNTM[b, l0:l0 + 128, :], t_xnb[k][:],
                          reads=[("lxb", k)], writes=[("XNTM", tt)])
                for dc in range(NDC0 if LN_TP else 0):
                    q = dc % 3
                    p.op("pe", lambda E, k=k, dc=dc, q=q: E.transpose(
                        out=ppb[q][:],
                        in_=t_xnb[k][:, dc * 128:(dc + 1) * 128],
                        identity=t_id_bf[:]),
                        reads=[("lxb", k), "idbf"], writes=[("tpb", q)])
                    p.op("act", lambda E, q=q: E.copy(
                        out=t_evb[q][:], in_=ppb[q][:]),
                        reads=[("tpb", q)], writes=[("evb", q)])
                    p.dma("act", XNBF[dc * 128:(dc + 1) * 128,
                                      tt * 128:(tt + 1) * 128],
                          t_evb[q][:],
                          reads=[("evb", q)], writes=[("XNBF", tt, dc)])
                    if layer == 0:
                        p.op("pe", lambda E, k=k, dc=dc, q=q: E.transpose(
                            out=pp[q][:],
                            in_=t_xn[k][:, dc * 128:(dc + 1) * 128],
                            identity=t_id_f[:]),
                            reads=[("lxn", k), "idf"], writes=[("tpf", q)])
                        p.op("dve", lambda E, q=q: E.tensor_copy(
                            out=t_evf[q][:],
                            in_=pp[q][:]),
                            reads=[("tpf", q)], writes=[("evf", q)])
                        p.dma("pool", XN32[dc * 128:(dc + 1) * 128,
                                           tt * 128:(tt + 1) * 128],
                              t_evf[q][:],
                              reads=[("evf", q)], writes=[("XN32", tt, dc)])
            p.barrier()

    def emit_ffn0():
        with contextlib.ExitStack() as st:
            t_wd = sb("f0wd", [128, 24 * 1024], BF16, st)
            stg_i = sb("f0stgi", [128, 1024], I32, st)
            t_xnb = [sb(f"f0xn{d}", [128, PCF], BF16, st) for d in range(NDC0)]
            t_wg = [sb(f"f0wg{i}", [128, 1024], BF16, st) for i in range(2)]
            t_wu = [sb(f"f0wu{i}", [128, 1024], BF16, st) for i in range(2)]
            t_sg = [sb(f"f0sg{i}", [128, PCF], F32, st) for i in range(2)]
            t_h = sb("f0h", [128, 24 * PCF], BF16, st)
            t_fo = [sb(f"f0o{i}", [128, PCF], F32, st) for i in range(2)]
            pg = [ps(f"f0pg{i}", [128, PCF], F32, st) for i in range(2)]
            pu = [ps(f"f0pu{i}", [128, PCF], F32, st) for i in range(2)]
            pd = [ps(f"f0pd{i}", [128, PCF], F32, st) for i in range(4)]
            for hc in range(24):
                p.dma("sp", stg_i[:], wdT[hc * 128:(hc + 1) * 128, :],
                      writes=["f0stg"])
                p.op("dve", lambda E, hc=hc: E.tensor_copy(
                    out=t_wd[:, hc * 1024:(hc + 1) * 1024], in_=stg_i[:]),
                    reads=["f0stg"], writes=[("wd", hc)])
            for pc in range(T // PCF):
                c0 = pc * PCF
                for dc in range(NDC0):
                    p.dma("sp", t_xnb[dc][:],
                          XNBF[dc * 128:(dc + 1) * 128, c0:c0 + PCF],
                          writes=[("f0x", dc)])
                for hc in range(24):
                    kb = hc % 2
                    p.dma("sp", t_wg[kb][:],
                          WGBF[:, hc * 128:(hc + 1) * 128].rearrange(
                              "(c p) h -> p c h", p=128),
                          writes=[("f0wg", kb)])
                    p.dma("sp", t_wu[kb][:],
                          WUBF[:, hc * 128:(hc + 1) * 128].rearrange(
                              "(c p) h -> p c h", p=128),
                          writes=[("f0wu", kb)])
                    for dc in range(NDC0):
                        p.op("pe", lambda E, kb=kb, dc=dc: E.matmul(
                            out=pg[kb][:], lhsT=t_wg[kb][:, dc * 128:(dc + 1) * 128],
                            rhs=t_xnb[dc][:], start=(dc == 0), stop=(dc == 7)),
                            reads=[("f0wg", kb), ("f0x", dc)],
                            writes=([("psg", kb)] if dc == 7 else []),
                            extra=p.predeps(("psg", kb)) if dc == 0 else [])
                        p.op("pe", lambda E, kb=kb, dc=dc: E.matmul(
                            out=pu[kb][:], lhsT=t_wu[kb][:, dc * 128:(dc + 1) * 128],
                            rhs=t_xnb[dc][:], start=(dc == 0), stop=(dc == 7)),
                            reads=[("f0wu", kb), ("f0x", dc)],
                            writes=([("psu", kb)] if dc == 7 else []),
                            extra=p.predeps(("psu", kb)) if dc == 0 else [])
                    p.op("act", lambda E, kb=kb: E.activation(
                        out=t_sg[kb][:], in_=pg[kb][:], func=AF.Silu,
                        scale=INV127),
                        reads=[("psg", kb)], writes=[("fsg", kb)])
                    p.op("dve", lambda E, kb=kb, hc=hc: E.scalar_tensor_tensor(
                        out=t_h[:, hc * PCF:(hc + 1) * PCF], in0=pu[kb][:],
                        scalar=INV127, in1=t_sg[kb][:], op0=ALU.mult,
                        op1=ALU.mult),
                        reads=[("psu", kb), ("fsg", kb)], writes=[("fh", hc)])
                for half in range(2):
                    for hc in range(24):
                        for do in range(4):
                            dd = half * 4 + do
                            p.op("pe", lambda E, hc=hc, do=do, dd=dd: E.matmul(
                                out=pd[do][:],
                                lhsT=t_wd[:, hc * 1024 + dd * 128:
                                          hc * 1024 + (dd + 1) * 128],
                                rhs=t_h[:, hc * PCF:(hc + 1) * PCF],
                                start=(hc == 0), stop=(hc == 23)),
                                reads=[("wd", hc), ("fh", hc)],
                                writes=([("psd", do)] if hc == 23 else []),
                                extra=p.predeps(("psd", do)) if hc == 0 else [])
                    for do in range(4):
                        dd = half * 4 + do
                        ko = dd % 2
                        p.op("dve", lambda E, do=do, ko=ko: E.tensor_scalar_mul(
                            out=t_fo[ko][:], in0=pd[do][:], scalar1=INV127),
                            reads=[("psd", do)], writes=[("f0o", ko)])
                        p.dma("sp", FFN0[dd * 128:(dd + 1) * 128, c0:c0 + PCF],
                              t_fo[ko][:], reads=[("f0o", ko)],
                              writes=[("FFN0", pc, dd)])
            p.barrier()

    def emit_l0():
        with contextlib.ExitStack() as st:
            t_xp = sb("l0xp", [128, NDC0 * 96], BF16, st)
            t_dt = sb("l0dt", [RNK, D0], BF16, st)
            stg = sb("l0stg", [128, 1024], F32, st)
            t_xh = [sb(f"l0xh{d}", [128, PC0 + 3], F32, st)
                    for d in range(NDC0)]
            t_co = [sb(f"l0co{d}", [128, PC0], BF16, st) for d in range(NDC0)]
            t_dl = [sb(f"l0dl{d}", [128, PC0], F32, st) for d in range(NDC0)]
            t_y = [sb(f"l0y{d}", [128, PC0], F32, st) for d in range(NDC0)]
            t_bc = sb("l0bc", [32, PC0], F32, st)
            t_dtl = sb("l0dtl", [RNK, PC0], BF16, st)
            t_t = sb("l0t", [128, PC0], F32, st)
            t_a = sb("l0a", [128, PC0], F32, st)
            t_r = sb("l0r", [128, PC0], F32, st)
            t_dAp = [sb(f"l0dAp{i}", [128, NST * PC0], F32, st)
                     for i in range(2)]
            t_T1p = [sb("l0T1p0", [128, NST * PC0], F32, st)]
            t_Bup = [sb("l0Bup0", [128, NST * PC0], F32, st)]
            t_hp = [sb(f"l0hp{i}", [128, NST * PC0], F32, st)
                     for i in range(3)]
            t_Bp = sb("l0Bp", [128, NST * PC0], F32, st)
            t_Cp = sb("l0Cp", [128, NST * PC0], F32, st)
            t_sg = sb("l0sg", [128, NST], F32, st)
            t_f = [sb(f"l0f{i}", [128, PC0], F32, st) for i in range(2)]
            t_xtm = [sb(f"l0xt{i}", [128, D0], F32, st) for i in range(2)]
            t_o = [sb("l0o0", [128, D0], F32, st)]
            p96 = ps("l0p96", [96, PC0], F32, st)
            pdt = [ps(f"l0pdt{i}", [128, PC0], F32, st) for i in range(2)]
            ptm = [ps(f"l0ptm{i}", [128, D0], F32, st) for i in range(2)]

            for dc in range(NDC0):
                load_bf(t_xp[:, dc * 96:(dc + 1) * 96],
                        xp0T[dc * 128:(dc + 1) * 128, :], ("xp0", dc), st, stg)
            load_bf(t_dt[0:RNK, :], dt0T, "dt0", st, stg, rows=RNK)
            p.op("dve", lambda E: E.memset(t_st0[:], 0.0), writes=["st0"])

            for b in range(B_LOC):
                if b > 0:
                    p.op("dve", lambda E: E.memset(t_st0[:], 0.0),
                         writes=["st0"])
                for pc in range(L // PC0):
                    c0 = b * L + pc * PC0
                    for dc in range(NDC0):
                        if pc == 0:
                            p.op("dve", lambda E, dc=dc: E.memset(
                                t_xh[dc][:, 0:3], 0.0), writes=[("xh", dc)])
                            p.dma("sp", t_xh[dc][:, 3:3 + PC0],
                                  XN32[dc * 128:(dc + 1) * 128, c0:c0 + PC0],
                                  writes=[("xh", dc)])
                        else:
                            p.dma("sp", t_xh[dc][:, 0:PC0 + 3],
                                  XN32[dc * 128:(dc + 1) * 128,
                                       c0 - 3:c0 + PC0],
                                  writes=[("xh", dc)])
                    for dc in range(NDC0):
                        p.op("dve", lambda E, dc=dc: E.tensor_scalar(
                            out=t_co[dc][:], in0=t_xh[dc][:, 3:3 + PC0],
                            scalar1=t_cw0[:, dc * 4 + 3:dc * 4 + 4],
                            scalar2=t_cb0[:, dc:dc + 1],
                            op0=ALU.mult, op1=ALU.add),
                            reads=[("xh", dc), "cw0", "cb0"],
                            writes=[("co", dc)])
                        for j in range(3):
                            p.op("dve", lambda E, dc=dc, j=j:
                                 E.scalar_tensor_tensor(
                                     out=t_co[dc][:],
                                     in0=t_xh[dc][:, j:j + PC0],
                                     scalar=t_cw0[:, dc * 4 + j:dc * 4 + j + 1],
                                     in1=t_co[dc][:],
                                     op0=ALU.mult, op1=ALU.add),
                                 reads=[("xh", dc), "cw0", ("co", dc)],
                                 writes=[("co", dc)])
                    if L0_DEBUG:
                        for dc in range(NDC0):
                            p.dma("act", XNBF[dc * 128:(dc + 1) * 128,
                                              c0:c0 + PC0], t_co[dc][:],
                                  reads=[("co", dc)], writes=[("dbgC", dc)])
                    for dc in range(NDC0):
                        p.op("pe", lambda E, dc=dc: E.matmul(
                            out=p96[:], lhsT=t_xp[:, dc * 96:(dc + 1) * 96],
                            rhs=t_co[dc][:], start=(dc == 0),
                            stop=(dc == NDC0 - 1)),
                            reads=[("xp0", dc), ("co", dc)],
                            writes=(["p96"] if dc == NDC0 - 1 else []),
                            extra=p.predeps("p96") if dc == 0 else [])
                    p.op("act", lambda E: E.copy(out=t_bc[:], in_=p96[64:96, :]),
                         reads=["p96"], writes=["bc"])
                    p.op("act", lambda E: E.copy(out=t_dtl[:],
                                                 in_=p96[0:64, :]),
                         reads=["p96"], writes=["dtl"])
                    p.dma("sp", BCR[0, b, :, pc * PC0:pc * PC0 + PC0],
                          t_bc[:], reads=["bc"], writes=[("BCR0", b, pc)])
                    for dc in range(NDC0):
                        kq = dc % 2
                        p.op("pe", lambda E, dc=dc, kq=kq: E.matmul(
                            out=pdt[kq][:], lhsT=t_dt[0:RNK,
                                                      dc * 128:(dc + 1) * 128],
                            rhs=t_dtl[:], start=True, stop=True),
                            reads=["dt0", "dtl"], writes=[("pdt", kq)],
                            extra=p.predeps(("pdt", kq)))
                        p.op("dve", lambda E, dc=dc, kq=kq: E.tensor_scalar(
                            out=t_t[:], in0=pdt[kq][:],
                            scalar1=t_db0[:, dc:dc + 1], scalar2=None,
                            op0=ALU.add),
                            reads=[("pdt", kq), "db0"], writes=["spt"])
                        p._record(p.wr["spt"], [], [("pdtq", kq)])
                        p.op("dve", lambda E: E.scalar_tensor_tensor(
                            out=t_a[:], in0=t_t[:], scalar=-1.0, in1=t_t[:],
                            op0=ALU.mult, op1=ALU.max),
                            reads=["spt"], writes=["spa"])
                        p.op("act", lambda E: E.activation(
                            out=t_a[:], in_=t_a[:], func=AF.Exp, scale=-1.0),
                            reads=["spa"], writes=["spa"])
                        p.op("act", lambda E: E.activation(
                            out=t_a[:], in_=t_a[:], func=AF.Ln, bias=1.0),
                            reads=["spa"], writes=["spa"])
                        p.op("dve", lambda E: E.tensor_scalar_max(
                            out=t_r[:], in0=t_t[:], scalar1=0.0),
                            reads=["spt"], writes=["spr"])
                        p.op("dve", lambda E, dc=dc: E.tensor_add(
                            out=t_dl[dc][:], in0=t_r[:], in1=t_a[:]),
                            reads=["spr", "spa"], writes=[("dl", dc)])
                        if L0_DEBUG:
                            p.dma("act", D1[dc * 128:(dc + 1) * 128,
                                            c0:c0 + PC0], t_dl[dc][:],
                                  reads=[("dl", dc)], writes=[("dbgD", dc)])
                    p.dma("sp", t_Bp[:],
                          BCR[0, b, 0:16,
                              pc * PC0:pc * PC0 + PC0].rearrange(
                              "(one n) l -> one n l", one=1).broadcast_to(
                              [128, NST, PC0]),
                          reads=[("BCR0", b, pc)], writes=["Bp"])
                    p.dma("sp", t_Cp[:],
                          BCR[0, b, 16:32,
                              pc * PC0:pc * PC0 + PC0].rearrange(
                              "(one n) l -> one n l", one=1).broadcast_to(
                              [128, NST, PC0]),
                          reads=[("BCR0", b, pc)], writes=["Cp"])
                    WP = NST * PC0

                    def l0_A(dc):
                        kq = dc % 2
                        kh = dc % 3
                        for n in range(NST):
                            p.op("act", lambda E, dc=dc, n=n, kq=kq:
                                 E.activation(
                                     out=t_dAp[kq][:, n * PC0:(n + 1) * PC0],
                                     in_=t_dl[dc][:], func=AF.Exp,
                                     scale=t_A0[:, dc * NST + n:
                                                dc * NST + n + 1]),
                                 reads=[("dl", dc), "A0"],
                                 writes=[("dAp", kq, n)])
                            p.op("act", lambda E, dc=dc, n=n, kq=kq:
                                 E.activation(
                                     out=t_T1p[0][:, n * PC0:(n + 1) * PC0],
                                     in_=t_dAp[kq][:, n * PC0:(n + 1) * PC0],
                                     func=AF.Abs,
                                     scale=t_iA0[:, dc * NST + n:
                                                 dc * NST + n + 1],
                                     bias=t_niA0[:, dc * NST + n:
                                                 dc * NST + n + 1]),
                                 reads=[("dAp", kq, n), "iA0", "niA0"],
                                 writes=[("T1p", 0, n)])
                        dA_all = [("dAp", kq, n) for n in range(NST)]
                        T1_all = [("T1p", 0, n) for n in range(NST)]
                        p.op("dve", lambda E, dc=dc: E.tensor_tensor(
                            out=t_Bup[0][:].rearrange(
                                "p (n l) -> p n l", n=NST),
                            in0=t_xh[dc][:, 3:3 + PC0].rearrange(
                                "p (one l) -> p one l", one=1).broadcast_to(
                                [128, NST, PC0]),
                            in1=t_Bp[:].rearrange("p (n l) -> p n l", n=NST),
                            op=ALU.mult),
                            reads=[("xh", dc), "Bp"], writes=[("Bup", 0)])
                        p.op("dve", lambda E: E.tensor_mul(
                            out=t_Bup[0][:], in0=t_T1p[0][:],
                            in1=t_Bup[0][:]),
                            reads=T1_all + [("Bup", 0)], writes=[("Bup", 0)])
                        p.op("dve", lambda E, dc=dc, kq=kq: E.tensor_tensor(
                            out=t_sg[:], in0=t_dAp[kq][:, 0:WP:PC0],
                            in1=t_st0[:, dc * NST:(dc + 1) * NST],
                            op=ALU.mult),
                            reads=dA_all + ["st0"], writes=["sg0"])
                        p.op("dve", lambda E: E.tensor_tensor(
                            out=t_Bup[0][:, 0:WP:PC0],
                            in0=t_Bup[0][:, 0:WP:PC0], in1=t_sg[:],
                            op=ALU.add),
                            reads=["sg0", ("Bup", 0)], writes=[("Bup", 0)])
                        p.op("dve", lambda E, kq=kq: E.memset(
                            t_dAp[kq][:, 0:WP:PC0], 0.0),
                            reads=[], writes=dA_all)
                        p.op("dve", lambda E, kq=kq, kh=kh:
                             E.tensor_tensor_scan(
                                 out=t_hp[kh][:], data0=t_dAp[kq][:],
                                 data1=t_Bup[0][:], initial=0.0,
                                 op0=ALU.mult, op1=ALU.add),
                             reads=dA_all + [("Bup", 0)],
                             writes=[("hp", kh)])
                        p.op("dve", lambda E, dc=dc, kh=kh: E.tensor_copy(
                            out=t_st0[:, dc * NST:(dc + 1) * NST],
                            in_=t_hp[kh][:, PC0 - 1:WP:PC0]),
                            reads=[("hp", kh)], writes=["st0"])

                    def l0_B(dc):
                        kh = dc % 3
                        p.op("dve", lambda E, kh=kh: E.tensor_mul(
                            out=t_hp[kh][:], in0=t_hp[kh][:], in1=t_Cp[:]),
                            reads=[("hp", kh), "Cp"],
                            writes=[("hp", kh)])
                        p.op("pool", lambda E, kh=kh: E.tensor_add(
                            out=t_hp[kh][:, 0:8 * PC0],
                            in0=t_hp[kh][:, 0:8 * PC0],
                            in1=t_hp[kh][:, 8 * PC0:16 * PC0]),
                            reads=[("hp", kh)], writes=[("hp", kh)])
                        for w in (4, 2):
                            p.op("dve", lambda E, kh=kh, w=w: E.tensor_add(
                                out=t_hp[kh][:, 0:w * PC0],
                                in0=t_hp[kh][:, 0:w * PC0],
                                in1=t_hp[kh][:, w * PC0:2 * w * PC0]),
                                reads=[("hp", kh)], writes=[("hp", kh)])
                        p.op("dve", lambda E, dc=dc, kh=kh: E.tensor_add(
                            out=t_y[dc][:], in0=t_hp[kh][:, 0:PC0],
                            in1=t_hp[kh][:, PC0:2 * PC0]),
                            reads=[("hp", kh)],
                            writes=[("y", dc), ("hp", kh)])

                    for dc in range(NDC0):
                        l0_A(dc)
                        if dc >= 2:
                            l0_B(dc - 2)
                    l0_B(NDC0 - 2)
                    l0_B(NDC0 - 1)
                    for dc in range(NDC0):
                        kq = dc % 2
                        if L0_DEBUG:
                            p.dma("act", U1[dc * 128:(dc + 1) * 128,
                                            c0:c0 + PC0], t_y[dc][:],
                                  reads=[("y", dc)], writes=[("dbgY", dc)])
                        p.op("dve", lambda E, dc=dc: E.scalar_tensor_tensor(
                            out=t_y[dc][:], in0=t_xh[dc][:, 3:3 + PC0],
                            scalar=t_dsk[:, dc:dc + 1], in1=t_y[dc][:],
                            op0=ALU.mult, op1=ALU.add),
                            reads=[("xh", dc), "dsk", ("y", dc)],
                            writes=[("y", dc)])
                        p.dma("sp", t_f[kq][:],
                              FFN0[dc * 128:(dc + 1) * 128, c0:c0 + PC0],
                              writes=[("ff", kq)])
                        p.op("dve", lambda E, dc=dc, kq=kq: E.tensor_add(
                            out=t_y[dc][:], in0=t_y[dc][:], in1=t_f[kq][:]),
                            reads=[("ff", kq), ("y", dc)], writes=[("y", dc)])
                    for tb in range(PC0 // 128):
                        kq = tb % 2
                        for dc in range(NDC0):
                            p.op("pe", lambda E, dc=dc, tb=tb, kq=kq:
                                 E.transpose(
                                     out=ptm[kq][:, dc * 128:(dc + 1) * 128],
                                     in_=t_y[dc][:, tb * 128:(tb + 1) * 128],
                                     identity=t_id_f[:]),
                                 reads=[("y", dc), "idf"],
                                 writes=[("ptm", kq, dc)])
                        r0 = pc * PC0 + tb * 128
                        p.dma("sp", t_xtm[kq][:], x_in[b, r0:r0 + 128, :],
                              writes=[("xtm", kq)])
                        p.op("dve", lambda E, kq=kq: E.tensor_add(
                            out=t_o[0][:], in0=ptm[kq][:], in1=t_xtm[kq][:]),
                            reads=[("xtm", kq)] + [("ptm", kq, d)
                                                   for d in range(NDC0)],
                            writes=[("otm", 0)])
                        p.dma("sp", X1TM[b, r0:r0 + 128, :], t_o[0][:],
                              reads=[("otm", 0)], writes=[("X1TM", b, r0)])
            p.barrier()

    def emit_i1():
        with contextlib.ExitStack() as st:
            t_ip = sb("i1ip", [128, NDC0 * 4096], BF16, st)
            stg = sb("i1stg", [128, 4096], F32, st)
            t_xnb = [sb(f"i1x{d}", [128, PCF], BF16, st) for d in range(NDC0)]
            t_xh = [sb(f"i1xh{d}", [128, PCF + 3], F32, st)
                    for d in range(NDC1)]
            t_tail = sb("i1tail", [128, 3 * NDC1], F32, st)
            t_zs = [sb(f"i1z{i}", [128, PCF], F32, st) for i in range(2)]
            t_u = [sb(f"i1u{d}", [128, PCF], F32, st) for d in range(2)]
            t_ub = [sb(f"i1ub{d}", [128, PCF], BF16, st) for d in range(2)]
            t_xp = sb("i1xp", [128, NDC1 * 96], BF16, st)
            t_dt = sb("i1dt", [RNK, DI], BF16, st)
            t_bc = sb("i1bc", [32, PCF], F32, st)
            t_dtl = sb("i1dtl", [RNK, PCF], BF16, st)
            t_t = sb("i1t", [128, PCF], F32, st)
            t_a = sb("i1a", [128, PCF], F32, st)
            t_r = sb("i1r", [128, PCF], F32, st)
            t_dl = [sb(f"i1dl{i}", [128, PCF], F32, st) for i in range(2)]
            pxz = [ps(f"i1px{i}", [128, PCF], F32, st) for i in range(4)]
            p96 = ps("i1p96", [96, PCF], F32, st)
            pdt = [ps(f"i1pdt{i}", [128, PCF], F32, st) for i in range(2)]

            for kc in range(NDC0):
                load_bf(t_ip[:, kc * 4096:(kc + 1) * 4096],
                        ipT[kc * 128:(kc + 1) * 128, :], ("ip", kc), st, stg)
            for dc in range(NDC1):
                load_bf(t_xp[:, dc * 96:(dc + 1) * 96],
                        xp1T[dc * 128:(dc + 1) * 128, :], ("xp1", dc), st, stg)
            load_bf(t_dt[0:RNK, :], dt1T, "dt1", st, stg, rows=RNK)

            for pc in range(T // PCF):
                c0 = pc * PCF
                new_seq = (c0 % L) == 0
                for dc in range(NDC0):
                    p.dma("sp", t_xnb[dc][:],
                          XNBF[dc * 128:(dc + 1) * 128, c0:c0 + PCF],
                          writes=[("i1x", dc)])
                for do in range(NDC1):
                    kq = do % 4
                    for kc in range(NDC0):
                        p.op("pe", lambda E, do=do, kc=kc, kq=kq: E.matmul(
                            out=pxz[kq][:],
                            lhsT=t_ip[:, kc * 4096 + do * 128:
                                      kc * 4096 + (do + 1) * 128],
                            rhs=t_xnb[kc][:], start=(kc == 0),
                            stop=(kc == NDC0 - 1)),
                            reads=[("ip", kc), ("i1x", kc)],
                            writes=([("pxz", kq)] if kc == NDC0 - 1 else []),
                            extra=p.predeps(("pxz", kq)) if kc == 0 else [])
                    if new_seq:
                        p.op("dve", lambda E, do=do: E.memset(
                            t_xh[do][:, 0:3], 0.0), writes=[("i1xh", do)])
                    else:
                        p.op("dve", lambda E, do=do: E.tensor_copy(
                            out=t_xh[do][:, 0:3],
                            in_=t_tail[:, do * 3:(do + 1) * 3]),
                            reads=[("tail", do)], writes=[("i1xh", do)])
                    p.op("act", lambda E, do=do, kq=kq: E.copy(
                        out=t_xh[do][:, 3:3 + PCF], in_=pxz[kq][:]),
                        reads=[("pxz", kq)], writes=[("i1xh", do)])
                    p.op("dve", lambda E, do=do: E.tensor_copy(
                        out=t_tail[:, do * 3:(do + 1) * 3],
                        in_=t_xh[do][:, PCF:PCF + 3]),
                        reads=[("i1xh", do)], writes=[("tail", do)])
                for dz in range(NDC1):
                    do = NDC1 + dz
                    kq = do % 4
                    kz = dz % 2
                    for kc in range(NDC0):
                        p.op("pe", lambda E, do=do, kc=kc, kq=kq: E.matmul(
                            out=pxz[kq][:],
                            lhsT=t_ip[:, kc * 4096 + do * 128:
                                      kc * 4096 + (do + 1) * 128],
                            rhs=t_xnb[kc][:], start=(kc == 0),
                            stop=(kc == NDC0 - 1)),
                            reads=[("ip", kc), ("i1x", kc)],
                            writes=([("pxz", kq)] if kc == NDC0 - 1 else []),
                            extra=p.predeps(("pxz", kq)) if kc == 0 else [])
                    p.op("act", lambda E, dz=dz, kq=kq, kz=kz: E.activation(
                        out=t_zs[kz][:], in_=pxz[kq][:], func=AF.Silu),
                        reads=[("pxz", kq)], writes=[("zs", kz)])
                    p.dma("sp", ZS[dz * 128:(dz + 1) * 128, c0:c0 + PCF],
                          t_zs[kz][:], reads=[("zs", kz)],
                          writes=[("ZS", pc, dz)])
                for dc in range(NDC1):
                    ku = dc % 2
                    p.op("dve", lambda E, dc=dc, ku=ku: E.tensor_scalar(
                        out=t_u[ku][:], in0=t_xh[dc][:, 3:3 + PCF],
                        scalar1=t_cw1[:, dc * 4 + 3:dc * 4 + 4],
                        scalar2=t_cb1[:, dc:dc + 1],
                        op0=ALU.mult, op1=ALU.add),
                        reads=[("i1xh", dc), "cw1", "cb1"],
                        writes=[("u", ku)])
                    for j in range(3):
                        p.op("dve", lambda E, dc=dc, j=j, ku=ku:
                             E.scalar_tensor_tensor(
                                 out=t_u[ku][:], in0=t_xh[dc][:, j:j + PCF],
                                 scalar=t_cw1[:, dc * 4 + j:dc * 4 + j + 1],
                                 in1=t_u[ku][:], op0=ALU.mult, op1=ALU.add),
                             reads=[("i1xh", dc), "cw1", ("u", ku)],
                             writes=[("u", ku)])
                    p.op("act", lambda E, ku=ku: E.activation(
                        out=t_u[ku][:], in_=t_u[ku][:], func=AF.Silu),
                        reads=[("u", ku)], writes=[("u", ku)])
                    p.dma("sp", U1[dc * 128:(dc + 1) * 128, c0:c0 + PCF],
                          t_u[ku][:], reads=[("u", ku)],
                          writes=[("U1", pc, dc)])
                    p.op("dve", lambda E, ku=ku: E.tensor_copy(
                        out=t_ub[ku][:], in_=t_u[ku][:]),
                        reads=[("u", ku)], writes=[("ub", ku)])
                    p.op("pe", lambda E, dc=dc, ku=ku: E.matmul(
                        out=p96[:], lhsT=t_xp[:, dc * 96:(dc + 1) * 96],
                        rhs=t_ub[ku][:], start=(dc == 0),
                        stop=(dc == NDC1 - 1)),
                        reads=[("xp1", dc), ("ub", ku)],
                        writes=(["i1p96"] if dc == NDC1 - 1 else []),
                        extra=p.predeps("i1p96") if dc == 0 else [])
                p.op("act", lambda E: E.copy(out=t_bc[:], in_=p96[64:96, :]),
                     reads=["i1p96"], writes=["i1bc"])
                p.op("act", lambda E: E.copy(out=t_dtl[:], in_=p96[0:64, :]),
                     reads=["i1p96"], writes=["i1dtl"])
                b, lc = divmod(c0, L)
                p.dma("sp", BCR[1, b, :, lc:lc + PCF], t_bc[:],
                      reads=["i1bc"], writes=[("BCR1", pc)])
                for dc in range(NDC1):
                    kq = dc % 2
                    p.op("pe", lambda E, dc=dc, kq=kq: E.matmul(
                        out=pdt[kq][:],
                        lhsT=t_dt[0:RNK, dc * 128:(dc + 1) * 128],
                        rhs=t_dtl[:], start=True, stop=True),
                        reads=["dt1", "i1dtl"], writes=[("i1pdt", kq)],
                        extra=p.predeps(("i1pdt", kq)))
                    p.op("dve", lambda E, dc=dc, kq=kq: E.tensor_scalar(
                        out=t_t[:], in0=pdt[kq][:],
                        scalar1=t_db1[:, dc:dc + 1], scalar2=None,
                        op0=ALU.add),
                        reads=[("i1pdt", kq), "db1"], writes=["i1spt"])
                    p._record(p.wr["i1spt"], [], [("i1pdtq", kq)])
                    p.op("dve", lambda E: E.scalar_tensor_tensor(
                        out=t_a[:], in0=t_t[:], scalar=-1.0, in1=t_t[:],
                        op0=ALU.mult, op1=ALU.max),
                        reads=["i1spt"], writes=["i1spa"])
                    p.op("act", lambda E: E.activation(
                        out=t_a[:], in_=t_a[:], func=AF.Exp, scale=-1.0),
                        reads=["i1spa"], writes=["i1spa"])
                    p.op("act", lambda E: E.activation(
                        out=t_a[:], in_=t_a[:], func=AF.Ln, bias=1.0),
                        reads=["i1spa"], writes=["i1spa"])
                    p.op("dve", lambda E: E.tensor_scalar_max(
                        out=t_r[:], in0=t_t[:], scalar1=0.0),
                        reads=["i1spt"], writes=["i1spr"])
                    kd = dc % 2
                    p.op("dve", lambda E, kd=kd: E.tensor_add(
                        out=t_dl[kd][:], in0=t_r[:], in1=t_a[:]),
                        reads=["i1spr", "i1spa"], writes=[("i1dl", kd)])
                    p.dma("sp", D1[dc * 128:(dc + 1) * 128, c0:c0 + PCF],
                          t_dl[kd][:], reads=[("i1dl", kd)],
                          writes=[("D1", pc, dc)])
            p.barrier()

    def emit_moe():
        with contextlib.ExitStack() as st:
            stg = sb("mstg", [128, 1024], F32, st)
            t_gvt = [sb(f"mgvt{b}", [128, 16], F32, st) for b in range(B_LOC)]
            t_pfT = [sb(f"mpfT{b}", [128, 128], F32, st)
                     for b in range(B_LOC)]
            t_mio = sb("mmio", [128, MTOP], F32, st)
            t_mioc = sb("mmioc", [128, 2], F32, st)
            stA = contextlib.ExitStack()
            t_gwb = sb("mgw", [128, NDC0 * 8], BF16, stA)
            t_xnb = [sb(f"mx{i}", [128, 512], BF16, stA) for i in range(2)]
            t_gl = [sb(f"mgl{b}", [8, L], F32, stA) for b in range(B_LOC)]
            t_ex = [sb(f"mex{b}", [8, L], F32, stA) for b in range(B_LOC)]
            t_tt = [sb(f"mtt{b}", [128, 128], F32, stA) for b in range(B_LOC)]
            t_sc1 = [sb(f"msc1{b}", [8, 1], F32, stA) for b in range(B_LOC)]
            t_sc2 = [sb(f"msc2{b}", [8, 1], F32, stA) for b in range(B_LOC)]
            t_sc3 = [sb(f"msc3{b}", [8, 1], F32, stA) for b in range(B_LOC)]
            t_nmx = [sb(f"mnm{b}", [128, 16], F32, stA) for b in range(B_LOC)]
            t_exs = sb("mexs", [128, 8], F32, stA)
            t_sm = [sb(f"msm{b}", [128, 16], F32, stA) for b in range(B_LOC)]
            t_cg = [sb(f"mcg{b}", [8, L], F32, stA) for b in range(B_LOC)]
            t_lo = [sb(f"mlo{b}", [8, 1], F32, stA) for b in range(B_LOC)]
            t_hi = [sb(f"mhi{b}", [8, 1], F32, stA) for b in range(B_LOC)]
            t_ta = [sb(f"mta{b}", [8, 1], F32, stA) for b in range(B_LOC)]
            t_g = [sb(f"mg{b}", [8, 1], F32, stA) for b in range(B_LOC)]
            t_m = [sb(f"mm{b}", [8, L], F32, stA) for b in range(B_LOC)]
            t_c = [sb(f"mc{b}", [8, 1], F32, stA) for b in range(B_LOC)]
            t_ones = sb("mones", [8, L], F32, stA)
            t_fl = [sb(f"mfl{b}", [8, L], F32, stA) for b in range(B_LOC)]
            t_pf = [sb(f"mpf{b}", [8, L], F32, stA) for b in range(B_LOC)]
            t_W = [sb(f"mW{b}", [8, L], F32, stA) for b in range(B_LOC)]
            t_pF = [sb(f"mpF{b}", [8, L], F32, stA) for b in range(B_LOC)]
            t_pW = sb("mpW", [8, L], F32, stA)
            t_wcg = sb("mwcg", [8, L], F32, stA)
            t_gvr = sb("mgvr", [1, L], F32, stA)
            t_lt8 = sb("mlt8", [8, 8], BF16, stA)
            t_on8 = sb("mon8", [8, 1], F32, stA)

            for kc in range(NDC0):
                load_bf(t_gwb[:, kc * 8:(kc + 1) * 8],
                        gw[kc * 128:(kc + 1) * 128, :], ("gw", kc), st, stg)
            p.op("dve", lambda E: E.memset(t_ones[:], 1.0), writes=["mones"])
            p.op("dve", lambda E: E.memset(t_on8[:], 1.0), writes=["mon8"])
            p.dma("sp", t_lt8[:], lt8, writes=["mlt8"])
            p.dma("sp", t_mio[:], miota.broadcast_to([128, MTOP]),
                  writes=["mmio"])
            p.dma("sp", t_mioc[:], miocol, writes=["mmioc"])

            pg8 = ps("mpg", [8, 512], F32, stA)
            ptr = ps("mptr", [128, 8], F32, stA)
            ptr2 = ps("mptr2", [8, 128], F32, stA)
            pgv = ps("mpgv", [8, 512], F32, stA)

            for b in range(B_LOC):
                for half in range(L // 512):
                    c0 = b * L + half * 512
                    for kc in range(NDC0):
                        kx = kc % 2
                        p.dma("sp", t_xnb[kx][:],
                              XNBF[kc * 128:(kc + 1) * 128, c0:c0 + 512],
                              writes=[("mxn", kx)])
                        p.op("pe", lambda E, kc=kc, kx=kx: E.matmul(
                            out=pg8[:], lhsT=t_gwb[:, kc * 8:(kc + 1) * 8],
                            rhs=t_xnb[kx][:], start=(kc == 0),
                            stop=(kc == NDC0 - 1)),
                            reads=[("gw", kc), ("mxn", kx)],
                            writes=(["mpg"] if kc == NDC0 - 1 else []),
                            extra=p.predeps("mpg") if kc == 0 else [])
                    p.op("act", lambda E, b=b, half=half: E.copy(
                        out=t_gl[b][:, half * 512:(half + 1) * 512],
                        in_=pg8[:]), reads=["mpg"], writes=[("gl", b)])
            for b in range(B_LOC):
                p.op("dve", lambda E, b=b: E.tensor_scalar_max(
                    out=t_gl[b][:], in0=t_gl[b][:], scalar1=1e-6),
                    reads=[("gl", b)], writes=[("gl", b)])
                p.op("act", lambda E, b=b: E.activation(
                    out=t_gl[b][:], in_=t_gl[b][:], func=AF.Ln),
                    reads=[("gl", b)], writes=[("gl", b)])
            for it in range(8):
                for b in range(B_LOC):
                    p.op("dve", lambda E, b=b: E.reduce_max(
                        out=t_sc1[b][:], in_=t_gl[b][:],
                        axis=mybir.AxisListType.X, negate=True),
                        reads=[("gl", b)], writes=[("sc1", b)])
                    p.op("act", lambda E, b=b: E.activation(
                        out=t_ex[b][:], in_=t_gl[b][:], func=AF.Exp,
                        bias=t_sc1[b][:, 0:1]),
                        reads=[("gl", b), ("sc1", b)], writes=[("ex", b)])
                    p.op("dve", lambda E, b=b: E.reduce_sum(
                        out=t_sc2[b][:], in_=t_ex[b][:],
                        axis=mybir.AxisListType.X),
                        reads=[("ex", b)], writes=[("sc2", b)])
                    p.op("act", lambda E, b=b: E.activation(
                        out=t_sc3[b][:], in_=t_sc2[b][:], func=AF.Ln),
                        reads=[("sc2", b)], writes=[("sc3", b)])
                    p.op("dve", lambda E, b=b: E.tensor_sub(
                        out=t_sc3[b][:], in0=t_sc3[b][:], in1=t_sc1[b][:]),
                        reads=[("sc3", b), ("sc1", b)], writes=[("sc3", b)])
                    p.op("dve", lambda E, b=b: E.tensor_scalar(
                        out=t_gl[b][:], in0=t_gl[b][:],
                        scalar1=t_sc3[b][:, 0:1], scalar2=None,
                        op0=ALU.subtract),
                        reads=[("gl", b), ("sc3", b)], writes=[("gl", b)])
                    for lc in range(L // 128):
                        p.op("pe", lambda E, b=b, lc=lc: E.transpose(
                            out=ptr[:], in_=t_gl[b][0:8,
                                                    lc * 128:(lc + 1) * 128],
                            identity=t_id_f[0:8, 0:8]),
                            reads=[("gl", b), "idf"], writes=["mptr"])
                        p.op("act", lambda E, b=b, lc=lc: E.copy(
                            out=t_tt[b][:, lc * 8:(lc + 1) * 8], in_=ptr[:]),
                            reads=["mptr"], writes=[("mtt", b, lc)])
                    p.op("dve", lambda E, b=b: E.tensor_reduce(
                        out=t_nmx[b].rearrange("p (a one) -> p a one", one=1),
                        in_=t_tt[b][:].rearrange("p (a x) -> p a x", x=8),
                        axis=mybir.AxisListType.X, op=ALU.max, negate=True),
                        reads=[("mtt", b, lc) for lc in range(16)],
                        writes=[("nmx", b)])
                    for lc in range(16):
                        p.op("act", lambda E, b=b, lc=lc: E.activation(
                            out=t_exs[:],
                            in_=t_tt[b][:, lc * 8:(lc + 1) * 8],
                            func=AF.Exp, bias=t_nmx[b][:, lc:lc + 1],
                            accum_out=t_sm[b][:, lc:lc + 1]),
                            reads=[("mtt", b, lc), ("nmx", b)],
                            writes=[("msm", b, lc), "mexs"])
                    p.op("act", lambda E, b=b: E.activation(
                        out=t_sm[b][:], in_=t_sm[b][:], func=AF.Ln),
                        reads=[("msm", b, lc) for lc in range(16)],
                        writes=[("msml", b)])
                    p.op("dve", lambda E, b=b: E.tensor_sub(
                        out=t_sm[b][:], in0=t_sm[b][:], in1=t_nmx[b][:]),
                        reads=[("msml", b), ("nmx", b)], writes=[("msml", b)])
                    for lc in range(16):
                        p.op("dve", lambda E, b=b, lc=lc: E.tensor_scalar(
                            out=t_tt[b][:, lc * 8:(lc + 1) * 8],
                            in0=t_tt[b][:, lc * 8:(lc + 1) * 8],
                            scalar1=t_sm[b][:, lc:lc + 1], scalar2=None,
                            op0=ALU.subtract),
                            reads=[("msml", b), ("mtt", b, lc)],
                            writes=[("mtt", b, lc)])
                        p.op("pe", lambda E, b=b, lc=lc: E.transpose(
                            out=ptr2[:], in_=t_tt[b][:, lc * 8:(lc + 1) * 8],
                            identity=t_id_f[:]),
                            reads=[("mtt", b, lc), "idf"], writes=["mptr2"])
                        p.op("act", lambda E, b=b, lc=lc: E.copy(
                            out=t_gl[b][0:8, lc * 128:(lc + 1) * 128],
                            in_=ptr2[0:8, :]),
                            reads=["mptr2"], writes=[("gl", b)])
            for b in range(B_LOC):
                p.op("act", lambda E, b=b: E.activation(
                    out=t_cg[b][:], in_=t_gl[b][:], func=AF.Exp),
                    reads=[("gl", b)], writes=[("cg", b)])
                p.op("dve", lambda E, b=b: E.memset(t_lo[b][:], 0.0),
                     writes=[("lo", b)])
                p.op("dve", lambda E, b=b: E.memset(t_hi[b][:], 1.01),
                     writes=[("hi", b)])
            for it in range(30):
                for b in range(B_LOC):
                    p.op("dve", lambda E, b=b: E.tensor_add(
                        out=t_ta[b][:], in0=t_lo[b][:], in1=t_hi[b][:]),
                        reads=[("lo", b), ("hi", b)], writes=[("ta", b)])
                    p.op("dve", lambda E, b=b: E.tensor_scalar_mul(
                        out=t_ta[b][:], in0=t_ta[b][:], scalar1=0.5),
                        reads=[("ta", b)], writes=[("ta", b)])
                    p.op("dve", lambda E, b=b: E.tensor_scalar(
                        out=t_m[b][:], in0=t_cg[b][:],
                        scalar1=t_ta[b][:, 0:1], scalar2=None, op0=ALU.is_gt),
                        reads=[("cg", b), ("ta", b)], writes=[("msk", b)])
                    p.op("dve", lambda E, b=b: E.reduce_sum(
                        out=t_c[b][:], in_=t_m[b][:],
                        axis=mybir.AxisListType.X),
                        reads=[("msk", b)], writes=[("cnt", b)])
                    p.op("dve", lambda E, b=b: E.tensor_scalar(
                        out=t_g[b][:], in0=t_c[b][:], scalar1=float(MTOP),
                        scalar2=None, op0=ALU.is_gt),
                        reads=[("cnt", b)], writes=[("g", b)])
                    p.op("dve", lambda E, b=b: E.tensor_sub(
                        out=t_c[b][:], in0=t_ta[b][:], in1=t_lo[b][:]),
                        reads=[("ta", b), ("lo", b)], writes=[("cnt", b)])
                    p.op("dve", lambda E, b=b: E.tensor_mul(
                        out=t_c[b][:], in0=t_c[b][:], in1=t_g[b][:]),
                        reads=[("cnt", b), ("g", b)], writes=[("cnt", b)])
                    p.op("dve", lambda E, b=b: E.tensor_add(
                        out=t_lo[b][:], in0=t_lo[b][:], in1=t_c[b][:]),
                        reads=[("cnt", b), ("lo", b)], writes=[("lo", b)])
                    p.op("dve", lambda E, b=b: E.tensor_sub(
                        out=t_c[b][:], in0=t_hi[b][:], in1=t_ta[b][:]),
                        reads=[("hi", b), ("ta", b)], writes=[("cnt", b)])
                    p.op("dve", lambda E, b=b: E.tensor_mul(
                        out=t_c[b][:], in0=t_c[b][:], in1=t_g[b][:]),
                        reads=[("cnt", b), ("g", b)], writes=[("cnt", b)])
                    p.op("dve", lambda E, b=b: E.tensor_add(
                        out=t_hi[b][:], in0=t_ta[b][:], in1=t_c[b][:]),
                        reads=[("cnt", b), ("ta", b)], writes=[("hi", b)])
            for b in range(B_LOC):
                p.op("dve", lambda E, b=b: E.tensor_scalar(
                    out=t_fl[b][:], in0=t_cg[b][:], scalar1=t_hi[b][:, 0:1],
                    scalar2=None, op0=ALU.is_gt),
                    reads=[("cg", b), ("hi", b)], writes=[("str", b)])
                p.op("dve", lambda E, b=b: E.reduce_sum(
                    out=t_c[b][:], in_=t_fl[b][:], axis=mybir.AxisListType.X),
                    reads=[("str", b)], writes=[("c1", b)])
                p.op("dve", lambda E, b=b: E.tensor_scalar(
                    out=t_c[b][:], in0=t_c[b][:], scalar1=-1.0,
                    scalar2=float(MTOP), op0=ALU.mult, op1=ALU.add),
                    reads=[("c1", b)], writes=[("c1", b)])
                p.op("dve", lambda E, b=b: E.tensor_scalar(
                    out=t_m[b][:], in0=t_cg[b][:], scalar1=t_lo[b][:, 0:1],
                    scalar2=None, op0=ALU.is_gt),
                    reads=[("cg", b), ("lo", b)], writes=[("msk", b)])
                p.op("dve", lambda E, b=b: E.tensor_sub(
                    out=t_m[b][:], in0=t_m[b][:], in1=t_fl[b][:]),
                    reads=[("msk", b), ("str", b)], writes=[("msk", b)])
                p.op("dve", lambda E, b=b: E.tensor_tensor_scan(
                    out=t_pf[b][:], data0=t_ones[0:8, :], data1=t_m[b][:],
                    initial=0.0, op0=ALU.mult, op1=ALU.add),
                    reads=[("msk", b), "mones"], writes=[("pf", b)])
                p.op("dve", lambda E, b=b: E.tensor_scalar(
                    out=t_pf[b][:], in0=t_pf[b][:], scalar1=t_c[b][:, 0:1],
                    scalar2=None, op0=ALU.is_le),
                    reads=[("pf", b), ("c1", b)], writes=[("pf", b)])
                p.op("dve", lambda E, b=b: E.tensor_mul(
                    out=t_m[b][:], in0=t_m[b][:], in1=t_pf[b][:]),
                    reads=[("msk", b), ("pf", b)], writes=[("msk", b)])
                p.op("dve", lambda E, b=b: E.tensor_add(
                    out=t_fl[b][:], in0=t_fl[b][:], in1=t_m[b][:]),
                    reads=[("str", b), ("msk", b)], writes=[("fl", b)])
                p.op("dve", lambda E, b=b: E.tensor_tensor_scan(
                    out=t_pf[b][:], data0=t_ones[0:8, :], data1=t_fl[b][:],
                    initial=0.0, op0=ALU.mult, op1=ALU.add),
                    reads=[("fl", b), "mones"], writes=[("pf", b)])
                p.op("dve", lambda E, b=b: E.tensor_sub(
                    out=t_pf[b][:], in0=t_pf[b][:], in1=t_fl[b][:]),
                    reads=[("pf", b), ("fl", b)], writes=[("pf", b)])
                t_flb = sb(f"mflb{b}", [8, L], BF16, stA)
                p.op("dve", lambda E, b=b, t_flb=t_flb: E.tensor_copy(
                    out=t_flb[:], in_=t_fl[b][:]),
                    reads=[("fl", b)], writes=[("flb", b)])
                for sl in range(4):
                    p.op("pe", lambda E, b=b, sl=sl, t_flb=t_flb: E.matmul(
                        out=pg8[:], lhsT=t_lt8[:],
                        rhs=t_flb[:, sl * 512:(sl + 1) * 512],
                        start=True, stop=True),
                        reads=["mlt8", ("flb", b)], writes=["mpg"],
                        extra=p.predeps("mpg"))
                    p.op("dve", lambda E, b=b, sl=sl: E.tensor_scalar(
                        out=t_W[b][:, sl * 512:(sl + 1) * 512], in0=pg8[:],
                        scalar1=0.5, scalar2=None, op0=ALU.is_lt),
                        reads=["mpg"], writes=[("W", b)])
                    p._record(p.wr[("W", b)], [], ["mpg"])
                p.op("dve", lambda E, b=b: E.tensor_mul(
                    out=t_W[b][:], in0=t_W[b][:], in1=t_fl[b][:]),
                    reads=[("W", b), ("fl", b)], writes=[("W", b)])
                p.op("dve", lambda E, b=b: E.tensor_mul(
                    out=t_wcg[:], in0=t_W[b][:], in1=t_cg[b][:]),
                    reads=[("W", b), ("cg", b)], writes=["wcg"])
                for sl in range(4):
                    p.op("pe", lambda E, sl=sl: E.matmul(
                        out=pgv[0:1, :], lhsT=t_on8[:],
                        rhs=t_wcg[:, sl * 512:(sl + 1) * 512],
                        start=True, stop=True),
                        reads=["mon8", "wcg"], writes=["mpgv"],
                        extra=p.predeps("mpgv"))
                    p.op("act", lambda E, sl=sl: E.copy(
                        out=t_gvr[:, sl * 512:(sl + 1) * 512],
                        in_=pgv[0:1, :]),
                        reads=["mpgv"], writes=["gvr"])
                    p._record(p.wr["gvr"], [], ["mpgv"])
                p.dma("act", GVD, t_gvr[:], reads=["gvr"], writes=["GVD"])
                p.dma("sp", t_gvt[b][:],
                      GVD.rearrange("one (c pp) -> (one pp) c", pp=128),
                      reads=["GVD"], writes=[("gvt", b)],
                      allow_slow_non_contiguous=True)
                p.op("dve", lambda E, b=b: E.tensor_mul(
                    out=t_pF[b][:], in0=t_pf[b][:], in1=t_fl[b][:]),
                    reads=[("pf", b), ("fl", b)], writes=[("pF", b)])
                p.op("dve", lambda E, b=b: E.tensor_scalar(
                    out=t_m[b][:], in0=t_fl[b][:], scalar1=-1.0,
                    scalar2=None, op0=ALU.add),
                    reads=[("fl", b)], writes=[("msk", b)])
                p.op("dve", lambda E, b=b: E.tensor_add(
                    out=t_pF[b][:], in0=t_pF[b][:], in1=t_m[b][:]),
                    reads=[("pF", b), ("msk", b)], writes=[("pF", b)])
                p.op("dve", lambda E, b=b: E.tensor_mul(
                    out=t_pW[:], in0=t_pf[b][:], in1=t_W[b][:]),
                    reads=[("pf", b), ("W", b)], writes=["pW"])
                p.op("dve", lambda E, b=b: E.tensor_scalar(
                    out=t_m[b][:], in0=t_W[b][:], scalar1=-1.0,
                    scalar2=None, op0=ALU.add),
                    reads=[("W", b)], writes=[("msk", b)])
                p.op("dve", lambda E, b=b: E.tensor_add(
                    out=t_pW[:], in0=t_pW[:], in1=t_m[b][:]),
                    reads=["pW", ("msk", b)], writes=["pW"])
                p.dma("act", PWD[b], t_pW[:], reads=["pW"],
                      writes=[("PWD", b)])
                for lc in range(16):
                    p.op("pe", lambda E, b=b, lc=lc: E.transpose(
                        out=ptr[:], in_=t_pF[b][0:8,
                                               lc * 128:(lc + 1) * 128],
                        identity=t_id_f[0:8, 0:8]),
                        reads=[("pF", b), "idf"], writes=["mptr"])
                    p.op("act", lambda E, b=b, lc=lc: E.copy(
                        out=t_pfT[b][:, lc * 8:(lc + 1) * 8], in_=ptr[:]),
                        reads=["mptr"], writes=[("pfT", b)])
            p.barrier()
            stA.close()

            t_xtm = [sb(f"mxt{i}", [128, D0], BF16, st) for i in range(16)]
            t_ct = [sb(f"mct{i}", [128, MTOP], BF16, st) for i in range(2)]
            t_xs = [sb(f"mxs{q}", [128, D0], BF16, st) for q in range(2)]
            t_rx = [sb(f"mrx{d}", [128, 2 * 128], BF16, st)
                    for d in range(NDC0)]
            t_we = sb("mwe", [128, NDC0 * D0], BF16, st)
            t_eo = [sb(f"meo{d}", [128, MTOP], F32, st) for d in range(NDC0)]
            t_vals = [sb(f"mvl{e}", [128, 2 * D0], BF16, st)
                      for e in range(NEXP)]
            t_D = [sb(f"mD{e}", [128, 2 * L], BF16, st) for e in range(NEXP)]
            t_pwr = sb("mpwr", [128, L], F32, st)
            t_mo = [sb(f"mmo{i}", [128, 512], F32, st) for i in range(2)]
            pgx = [ps(f"mpgx{i}", [128, 512], F32, st) for i in range(4)]
            pxT = ps("mpxT", [128, 128], BF16, st)
            pex = [ps(f"mpex{i}", [128, MTOP], F32, st) for i in range(2)]
            pvT = ps("mpvT", [128, 128], F32, st)

            for b in range(B_LOC):
                for lc in range(16):
                    p.dma("sp", t_xtm[lc][:],
                          XNTM[b, lc * 128:(lc + 1) * 128, :],
                          writes=[("xtm", lc)])
                for e in range(NEXP):
                    for kc in range(NDC0):
                        load_bf(t_we[:, kc * D0:(kc + 1) * D0],
                                exps[e, kc * 128:(kc + 1) * 128, :],
                                ("we", kc), st, stg)
                    for lc in range(16):
                        kc2 = lc % 2
                        p.op("dve", lambda E, b=b, e=e, lc=lc, kc2=kc2:
                             E.tensor_scalar(
                                 out=t_ct[kc2][:], in0=t_mio[:],
                                 scalar1=t_pfT[b][:, lc * 8 + e:
                                                  lc * 8 + e + 1],
                                 scalar2=None, op0=ALU.is_equal),
                             reads=[("pfT", b), "mmio"],
                             writes=[("ct", kc2)])
                        for mh in range(2):
                            for dh in range(2):
                                pq = mh * 2 + dh
                                p.op("pe", lambda E, lc=lc, mh=mh, dh=dh,
                                     pq=pq, kc2=kc2: E.matmul(
                                         out=pgx[pq][:],
                                         lhsT=t_ct[kc2][:, mh * 128:
                                                        (mh + 1) * 128],
                                         rhs=t_xtm[lc][:, dh * 512:
                                                       (dh + 1) * 512],
                                         start=(lc == 0), stop=(lc == 15)),
                                     reads=[("ct", kc2), ("xtm", lc)],
                                     writes=([("pgx", pq)]
                                             if lc == 15 else []),
                                     extra=(p.predeps(("pgx", pq))
                                            if lc == 0 else []))
                    for mh in range(2):
                        for dh in range(2):
                            pq = mh * 2 + dh
                            p.op("act", lambda E, mh=mh, dh=dh, pq=pq: E.copy(
                                out=t_xs[mh][:, dh * 512:(dh + 1) * 512],
                                in_=pgx[pq][:]),
                                reads=[("pgx", pq)], writes=[("xs", mh)])
                            p._record(p.wr[("xs", mh)], [], [("pgx", pq)])
                    for dc in range(NDC0):
                        for mh in range(2):
                            p.op("pe", lambda E, dc=dc, mh=mh: E.transpose(
                                out=pxT[:],
                                in_=t_xs[mh][:, dc * 128:(dc + 1) * 128],
                                identity=t_id_bf[:]),
                                reads=[("xs", mh), "idbf"], writes=["pxT"])
                            p.op("act", lambda E, dc=dc, mh=mh: E.copy(
                                out=t_rx[dc][:, mh * 128:(mh + 1) * 128],
                                in_=pxT[:]),
                                reads=["pxT"], writes=[("mrx", dc)])
                            p._record(p.wr[("mrx", dc)], [], ["pxT"])
                    for do in range(NDC0):
                        kq = do % 2
                        for kc in range(NDC0):
                            p.op("pe", lambda E, do=do, kc=kc, kq=kq:
                                 E.matmul(
                                     out=pex[kq][:],
                                     lhsT=t_we[:, kc * D0 + do * 128:
                                               kc * D0 + (do + 1) * 128],
                                     rhs=t_rx[kc][:], start=(kc == 0),
                                     stop=(kc == NDC0 - 1)),
                                 reads=[("we", kc), ("mrx", kc)],
                                 writes=([("pex", kq)]
                                         if kc == NDC0 - 1 else []),
                                 extra=(p.predeps(("pex", kq))
                                        if kc == 0 else []))
                        p.op("act", lambda E, do=do, kq=kq: E.copy(
                            out=t_eo[do][:], in_=pex[kq][:]),
                            reads=[("pex", kq)], writes=[("meo", do)])
                        p._record(p.wr[("meo", do)], [], [("pex", kq)])
                    for do in range(NDC0):
                        for mh in range(2):
                            p.op("pe", lambda E, do=do, mh=mh: E.transpose(
                                out=pvT[:],
                                in_=t_eo[do][:, mh * 128:(mh + 1) * 128],
                                identity=t_id_f[:]),
                                reads=[("meo", do), "idf"], writes=["pvT"])
                            p.op("act", lambda E, do=do, mh=mh, e=e: E.copy(
                                out=t_vals[e][:, mh * D0 + do * 128:
                                              mh * D0 + (do + 1) * 128],
                                in_=pvT[:]),
                                reads=["pvT"], writes=[("vals", e)])
                            p._record(p.wr[("vals", e)], [], ["pvT"])
                    p.dma("sp", t_pwr[:],
                          PWD[b, e:e + 1, :].broadcast_to([128, L]),
                          reads=[("PWD", b)], writes=["pwr"])
                    for mc in range(2):
                        p.op("dve", lambda E, e=e, mc=mc: E.tensor_scalar(
                            out=t_D[e][:, mc * L:(mc + 1) * L],
                            in0=t_pwr[:], scalar1=t_mioc[:, mc:mc + 1],
                            scalar2=None, op0=ALU.is_equal),
                            reads=["pwr", "mmioc"], writes=[("D", e)])
                for lc in range(16):
                    for dh in range(2):
                        pq = dh
                        for e in range(NEXP):
                            for mc in range(2):
                                first = (e == 0 and mc == 0)
                                last = (e == NEXP - 1 and mc == 1)
                                p.op("pe", lambda E, lc=lc, dh=dh, e=e,
                                     mc=mc, pq=pq, first=first, last=last:
                                     E.matmul(
                                         out=pgx[pq][:],
                                         lhsT=t_D[e][:, mc * L + lc * 128:
                                                     mc * L + (lc + 1) * 128],
                                         rhs=t_vals[e][:, mc * D0 + dh * 512:
                                                       mc * D0 + (dh + 1) * 512],
                                         start=first, stop=last),
                                     reads=[("D", e), ("vals", e)],
                                     writes=([("pgx", pq)] if last else []),
                                     extra=(p.predeps(("pgx", pq))
                                            if first else []))
                        p.op("dve", lambda E, b=b, lc=lc, dh=dh, pq=pq:
                             E.tensor_scalar(
                                 out=t_mo[dh][:], in0=pgx[pq][:],
                                 scalar1=t_gvt[b][:, lc:lc + 1],
                                 scalar2=None, op0=ALU.mult),
                             reads=[("pgx", pq), ("gvt", b)],
                             writes=[("mo", dh)])
                        p._record(p.wr[("mo", dh)], [], [("pgx", pq)])
                        p.dma("act", MOTM[b, lc * 128:(lc + 1) * 128,
                                          dh * 512:(dh + 1) * 512],
                              t_mo[dh][:], reads=[("mo", dh)],
                              writes=[("MOTM", b, lc, dh)])
            p.barrier()
    def emit_s1():
        with contextlib.ExitStack() as st:
            t_op = sb("s1op", [128, NDC1 * D0], BF16, st)
            stg = sb("s1stg", [128, 1024], F32, st)
            t_dl = [sb(f"s1dl{d}", [128, PC1], F32, st) for d in range(2)]
            t_u = [sb(f"s1u{d}", [128, PC1], F32, st) for d in range(2)]
            t_uk = [sb(f"s1uk{d}", [128, PC1], F32, st) for d in range(2)]
            t_du = [sb(f"s1du{d}", [128, PC1], F32, st) for d in range(2)]
            t_y2 = [sb(f"s1y2{d}", [128, PC1], F32, st) for d in range(2)]
            t_yb = [sb(f"s1yb{d}", [128, PC1], BF16, st)
                    for d in range(NDC1)]
            t_zs = [sb(f"s1z{i}", [128, PC1], F32, st) for i in range(2)]
            t_dAp1 = [sb(f"s1dAp{i}", [128, NST * PC1], F32, st)
                      for i in range(2)]
            t_Bup1 = [sb("s1Bup0", [128, NST * PC1], F32, st)]
            t_hp1 = [sb(f"s1hp{i}", [128, NST * PC1], F32, st)
                     for i in range(2)]
            t_hbf = sb("s1hbf", [128, NST * PC1], BF16, st)
            t_hT = sb("s1hT", [128, NST * PC1], BF16, st)
            t_dg = sb("s1dg", [128, 2 * NST * 128], BF16, st)
            t_cb = [sb(f"s1cb{i}", [128, 128], F32, st) for i in range(2)]
            t_Bp1 = sb("s1Bp", [128, NST * PC1], F32, st)
            t_sg1 = sb("s1sg", [128, NST], F32, st)
            t_at = [sb(f"s1at{i}", [128, PC1], F32, st)
                    for i in range(NDC0)]
            t_x1 = [sb("s1x10", [128, D0], F32, st)]
            t_o = [sb("s1o0", [128, D0], F32, st)]
            pop = [ps(f"s1pop{i}", [128, PC1], F32, st) for i in range(2)]
            ptm = [ps("s1ptm0", [128, D0], F32, st)]
            phT = [ps(f"s1phT{i}", [128, 1024], BF16, st) for i in range(2)]
            pyf = [ps(f"s1pyf{i}", [128, PC1], F32, st) for i in range(2)]

            for kc in range(NDC1):
                load_bf(t_op[:, kc * D0:(kc + 1) * D0],
                        opT[kc * 128:(kc + 1) * 128, :], ("op", kc), st, stg)
            p.op("dve", lambda E: E.memset(t_st1[:], 0.0), writes=["st1"])
            for b in range(B_LOC):
                if b > 0:
                    p.op("dve", lambda E: E.memset(t_st1[:], 0.0),
                         writes=["st1"])
                for pc in range(L // PC1):
                    c0 = b * L + pc * PC1
                    lc = pc * PC1
                    p.dma("sp", t_Bp1[:],
                          BCR[1, b, 0:16, lc:lc + PC1].rearrange(
                              "(one n) l -> one n l", one=1).broadcast_to(
                              [128, NST, PC1]),
                          writes=["Bp1"])
                    for n in range(NST):
                        for lb in range(PC1 // 128):
                            j = n * 2 + lb
                            kc2 = j % 2
                            p.dma("sp", t_cb[kc2][:],
                                  BCR[1, b, 16 + n:17 + n,
                                      lc + lb * 128:lc + (lb + 1) * 128]
                                  .broadcast_to([128, 128]),
                                  writes=[("cb", kc2)])
                            p.op("dve", lambda E, j=j, kc2=kc2:
                                 E.tensor_mul(
                                     out=t_dg[:, j * 128:(j + 1) * 128],
                                     in0=t_id_f[:], in1=t_cb[kc2][:]),
                                 reads=[("cb", kc2), "idf"],
                                 writes=[("dg", j)])
                    WP1 = NST * PC1

                    def s1_A(dc):
                        kq = dc % 2
                        kh = dc % 2
                        p.dma("sp", t_dl[kq][:],
                              D1[dc * 128:(dc + 1) * 128, c0:c0 + PC1],
                              writes=[("dl", kq)])
                        p.dma("sp", t_u[kq][:],
                              U1[dc * 128:(dc + 1) * 128, c0:c0 + PC1],
                              writes=[("u", kq)])
                        p.op("dve", lambda E, kq=kq: E.tensor_mul(
                            out=t_du[kq][:], in0=t_dl[kq][:],
                            in1=t_u[kq][:]),
                            reads=[("dl", kq), ("u", kq)],
                            writes=[("du", kq)])
                        for n in range(NST):
                            p.op("act", lambda E, dc=dc, n=n, kq=kq:
                                 E.activation(
                                     out=t_dAp1[kq][:, n * PC1:(n + 1) * PC1],
                                     in_=t_dl[kq][:], func=AF.Exp,
                                     scale=t_A1[:, dc * NST + n:
                                                dc * NST + n + 1]),
                                 reads=[("dl", kq), "A1"],
                                 writes=[("dAp1", kq, n)])
                        dA_all = [("dAp1", kq, n) for n in range(NST)]
                        p.op("dve", lambda E, kq=kq: E.tensor_tensor(
                            out=t_Bup1[0][:].rearrange(
                                "p (n l) -> p n l", n=NST),
                            in0=t_du[kq][:].rearrange(
                                "p (one l) -> p one l", one=1).broadcast_to(
                                [128, NST, PC1]),
                            in1=t_Bp1[:].rearrange("p (n l) -> p n l", n=NST),
                            op=ALU.mult),
                            reads=[("du", kq), "Bp1"], writes=[("Bup1", 0)])
                        p.op("dve", lambda E, dc=dc, kq=kq: E.tensor_tensor(
                            out=t_sg1[:], in0=t_dAp1[kq][:, 0:WP1:PC1],
                            in1=t_st1[:, dc * NST:(dc + 1) * NST],
                            op=ALU.mult),
                            reads=dA_all + ["st1"], writes=["sg1"])
                        p.op("dve", lambda E: E.tensor_tensor(
                            out=t_Bup1[0][:, 0:WP1:PC1],
                            in0=t_Bup1[0][:, 0:WP1:PC1], in1=t_sg1[:],
                            op=ALU.add),
                            reads=["sg1", ("Bup1", 0)],
                            writes=[("Bup1", 0)])
                        p.op("dve", lambda E, kq=kq: E.memset(
                            t_dAp1[kq][:, 0:WP1:PC1], 0.0),
                            reads=[], writes=dA_all)
                        p.op("dve", lambda E, kq=kq, kh=kh:
                             E.tensor_tensor_scan(
                                 out=t_hp1[kh][:], data0=t_dAp1[kq][:],
                                 data1=t_Bup1[0][:], initial=0.0,
                                 op0=ALU.mult, op1=ALU.add),
                             reads=dA_all + [("Bup1", 0)],
                             writes=[("hp1", kh)])
                        p.op("dve", lambda E, dc=dc, kh=kh: E.tensor_copy(
                            out=t_st1[:, dc * NST:(dc + 1) * NST],
                            in_=t_hp1[kh][:, PC1 - 1:WP1:PC1]),
                            reads=[("hp1", kh)], writes=["st1"])

                    def s1_B(dc):
                        kh = dc % 2
                        kz = dc % 2
                        kp = dc % 2
                        p.op("dve", lambda E, kh=kh: E.tensor_copy(
                            out=t_hbf[:], in_=t_hp1[kh][:]),
                            reads=[("hp1", kh)], writes=["hbf"])
                        for g in range(4):
                            kt = g % 2
                            for jj in range(8):
                                j = g * 8 + jj
                                n, lb = j // 2, j % 2
                                p.op("pe", lambda E, n=n, lb=lb, jj=jj,
                                     kt=kt: E.transpose(
                                         out=phT[kt][:, jj * 128:
                                                     (jj + 1) * 128],
                                         in_=t_hbf[:, n * PC1 + lb * 128:
                                                   n * PC1 + (lb + 1) * 128],
                                         identity=t_id_bf[:]),
                                     reads=["hbf", "idbf"],
                                     writes=[("phT", kt, jj)])
                            p.op("act", lambda E, g=g, kt=kt: E.copy(
                                out=t_hT[:, g * 1024:(g + 1) * 1024],
                                in_=phT[kt][:]),
                                reads=[("phT", kt, jj) for jj in range(8)],
                                writes=[("hT", g)])
                            p._record(p.wr[("hT", g)], [],
                                      [("phT", kt, jj) for jj in range(8)])
                        for lb in range(PC1 // 128):
                            for n in range(NST):
                                j = n * 2 + lb
                                p.op("pe", lambda E, j=j, lb=lb, kp=kp,
                                     n=n: E.matmul(
                                         out=pyf[kp][:, lb * 128:
                                                     (lb + 1) * 128],
                                         lhsT=t_hT[:, j * 128:(j + 1) * 128],
                                         rhs=t_dg[:, j * 128:(j + 1) * 128],
                                         start=(n == 0), stop=(n == NST - 1)),
                                     reads=[("hT", j // 8), ("dg", j)],
                                     writes=([("pyf", kp, lb)]
                                             if n == NST - 1 else []),
                                     extra=(p.predeps(("pyf", kp, lb))
                                            if n == 0 else []))
                        p.dma("sp", t_uk[kz][:],
                              U1[dc * 128:(dc + 1) * 128, c0:c0 + PC1],
                              writes=[("uk", kz)])
                        p.op("dve", lambda E, dc=dc, kz=kz, kp=kp:
                             E.scalar_tensor_tensor(
                                 out=t_y2[kz][:], in0=t_uk[kz][:],
                                 scalar=t_dd1[:, dc:dc + 1],
                                 in1=pyf[kp][:],
                                 op0=ALU.mult, op1=ALU.add),
                             reads=[("uk", kz), "dd1",
                                    ("pyf", kp, 0), ("pyf", kp, 1)],
                             writes=[("y2", kz)])
                        p._record(p.wr[("y2", kz)], [],
                                  [("pyf", kp, 0), ("pyf", kp, 1)])
                        p.dma("sp", t_zs[kz][:],
                              ZS[dc * 128:(dc + 1) * 128, c0:c0 + PC1],
                              writes=[("zs", kz)])
                        p.op("dve", lambda E, dc=dc, kz=kz: E.tensor_mul(
                            out=t_yb[dc][:], in0=t_y2[kz][:],
                            in1=t_zs[kz][:]),
                            reads=[("y2", kz), ("zs", kz)],
                            writes=[("yb", dc)])

                    for dc in range(NDC1):
                        s1_A(dc)
                        if dc >= 3:
                            s1_B(dc - 3)
                    for dc in range(NDC1 - 3, NDC1):
                        s1_B(dc)

                    for do in range(NDC0):
                        kq = do % 2
                        for kc in range(NDC1):
                            p.op("pe", lambda E, do=do, kc=kc, kq=kq:
                                 E.matmul(
                                     out=pop[kq][:],
                                     lhsT=t_op[:, kc * D0 + do * 128:
                                               kc * D0 + (do + 1) * 128],
                                     rhs=t_yb[kc][:], start=(kc == 0),
                                     stop=(kc == NDC1 - 1)),
                                 reads=[("op", kc), ("yb", kc)],
                                 writes=([("pop", kq)]
                                         if kc == NDC1 - 1 else []),
                                 extra=(p.predeps(("pop", kq))
                                        if kc == 0 else []))
                        p.op("act", lambda E, do=do, kq=kq: E.copy(
                            out=t_at[do][:], in_=pop[kq][:]),
                            reads=[("pop", kq)], writes=[("at", do)])
                        p._record(p.wr[("at", do)], [], [("pop", kq)])
                    for tb in range(PC1 // 128):
                        kq = tb % 2
                        for do in range(NDC0):
                            p.op("pe", lambda E, do=do, tb=tb:
                                 E.transpose(
                                     out=ptm[0][:, do * 128:(do + 1) * 128],
                                     in_=t_at[do][:, tb * 128:(tb + 1) * 128],
                                     identity=t_id_f[:]),
                                 reads=[("at", do), "idf"],
                                 writes=[("ptm", do)])
                        r0 = pc * PC1 + tb * 128
                        p.dma("sp", t_x1[0][:], X1TM[b, r0:r0 + 128, :],
                              writes=[("x1", 0)])
                        p.dma("sp", t_o[0][:], MOTM[b, r0:r0 + 128, :],
                              writes=[("otm", 0)])
                        p.op("dve", lambda E: E.tensor_add(
                            out=t_o[0][:], in0=ptm[0][:], in1=t_o[0][:]),
                            reads=[("otm", 0)] + [("ptm", d)
                                                  for d in range(NDC0)],
                            writes=[("otm", 0)])
                        p._record(p.wr[("otm", 0)], [],
                                  [("ptm", d) for d in range(NDC0)])
                        p.op("dve", lambda E: E.tensor_add(
                            out=t_o[0][:], in0=t_o[0][:], in1=t_x1[0][:]),
                            reads=[("x1", 0), ("otm", 0)],
                            writes=[("otm", 0)])
                        p.dma("sp", out[b, r0:r0 + 128, :], t_o[0][:],
                              reads=[("otm", 0)], writes=[("out", b, r0)])
            p.barrier()

    if stage != "prep":
        emit_ln(0)
    if stage in ("ffn0", "l0", "x1", "i1", "moe", "s1", "full"):
        emit_ffn0()
    if stage in ("l0", "x1", "i1", "moe", "s1", "full"):
        emit_l0()
    if stage in ("i1", "moe", "s1", "full"):
        emit_ln(1)
        emit_i1()
    if stage in ("moe", "s1", "full"):
        emit_moe()
    if stage in ("s1", "full"):
        emit_s1()

    if stage not in ("s1", "full"):
        zo = sb("zout", [128, D0])
        p.op("dve", lambda E: E.memset(zo[:], 0.0), writes=["zout"])
        for b in range(B_LOC):
            for i in range(L // 128):
                p.dma("sp", out[b, i * 128:(i + 1) * 128, :], zo[:],
                      reads=["zout"], writes=[("outz", b, i)])

    if dbg is not None:
        src = dbg_src[debug_out]
        p.barrier()
        flat = int(np.prod(src.shape))
        sf = src.rearrange(
            " ".join(f"a{i}" for i in range(len(src.shape))) + " -> ("
            + " ".join(f"a{i}" for i in range(len(src.shape))) + ")")
        df = dbg.rearrange(
            " ".join(f"a{i}" for i in range(len(src.shape))) + " -> ("
            + " ".join(f"a{i}" for i in range(len(src.shape))) + ")")
        tcp = sb("dbgcp", [128, 4096], src.dtype)
        CH = 128 * 4096
        off = 0
        i = 0
        while off < flat:
            n = min(CH, flat - off)
            rows = n // 4096
            assert rows * 4096 == n, f"debug copy misalign {flat}"
            p.dma("sp", tcp[0:rows, :],
                  sf[off:off + n].rearrange("(p f) -> p f", f=4096),
                  writes=["dbgcp"])
            p.dma("act", df[off:off + n].rearrange("(p f) -> p f", f=4096),
                  tcp[0:rows, :], reads=["dbgcp"], writes=[("dbg", i)])
            off += n
            i += 1

    p.barrier()
    with nc.Block() as block:
        p.emit(block)
    top.close()
    return nc


def _host_inputs(w):
    tr = lambda a: np.ascontiguousarray(np.transpose(np.asarray(a)))
    cols = lambda v, ndc: np.ascontiguousarray(
        np.asarray(v).reshape(ndc, 128).T)
    d = {}
    d["cw0"] = np.ascontiguousarray(
        np.asarray(w["L0_attn_conv_w"])[:, 0, :].reshape(NDC0, 128, 4)
        .transpose(1, 0, 2).reshape(128, NDC0 * 4))
    d["cb0"] = cols(w["L0_attn_conv_b"], NDC0)
    d["xp0T"] = np.ascontiguousarray(tr(w["L0_attn_x_proj_w"]))
    d["dt0T"] = tr(w["L0_attn_dt_proj_w"])
    d["db0"] = cols(w["L0_attn_dt_proj_b"], NDC0)
    d["al0"] = np.asarray(w["L0_attn_A_log"])
    d["dsk0"] = cols(w["L0_attn_D_skip"], NDC0)
    d["wgT"] = tr(w["L0_ffn_w_gate"])
    d["wuT"] = tr(w["L0_ffn_w_up"])
    d["wdT"] = tr(w["L0_ffn_w_down"])
    d["ipT"] = tr(w["L1_attn_in_proj_w"])
    d["cw1"] = np.ascontiguousarray(
        np.asarray(w["L1_attn_conv_w"])[:, 0, :].reshape(NDC1, 128, 4)
        .transpose(1, 0, 2).reshape(128, NDC1 * 4))
    d["cb1"] = cols(w["L1_attn_conv_b"], NDC1)
    d["xp1T"] = np.ascontiguousarray(tr(w["L1_attn_x_proj_w"]))
    d["dt1T"] = tr(w["L1_attn_dt_proj_w"])
    d["db1"] = cols(w["L1_attn_dt_proj_b"], NDC1)
    d["al1"] = np.asarray(w["L1_attn_A_log"])
    d["dd1"] = cols(w["L1_attn_D"], NDC1)
    d["opT"] = tr(w["L1_attn_out_proj_w"])
    d["gw"] = np.asarray(w["L1_ffn_gate_w"])
    d["exps"] = np.asarray(w["L1_ffn_experts"])
    import ml_dtypes
    d["id_bf"] = np.eye(128, dtype=ml_dtypes.bfloat16)
    d["id_f32"] = np.eye(128, dtype=np.float32)
    d["lt8"] = np.tril(np.ones((NEXP, NEXP), np.float32), -1).astype(
        ml_dtypes.bfloat16)
    d["miota"] = np.arange(MTOP, dtype=np.float32).reshape(1, MTOP)
    d["miocol"] = np.arange(MTOP, dtype=np.float32).reshape(
        2, 128).T.copy()
    return d


def kernel(**inputs):
    x = np.asarray(inputs["L0_attn_u"])
    base = _host_inputs(inputs)
    nc = build_program("full")
    in_maps = []
    for c in range(8):
        m = dict(base)
        m["x_in"] = np.ascontiguousarray(x[2 * c:2 * c + 2])
        in_maps.append(m)
    res = run_bass_kernel_spmd(nc, in_maps, core_ids=list(range(8)))
    return np.concatenate([r["out"] for r in res.results], axis=0)
